# version 30
# baseline (speedup 1.0000x reference)
import bisect
import contextlib
import numpy as np

import concourse.bass as bass
from concourse.bass import broadcast_tensor_aps
import concourse.mybir as mybir
from concourse.bass_utils import run_bass_kernel_spmd

F32 = mybir.dt.float32
BF16 = mybir.dt.bfloat16
AF = mybir.ActivationFunctionType
ALU = mybir.AluOpType
AX = mybir.AxisListType

T = 4096
C = 1024
H = 16
NT = T // 128
TAU = 512
NTAU = T // TAU
NQB = 8
FTAU = 1024
NFT = T // FTAU
NE = 9
INV_SQ = float(1.0 / np.sqrt(np.float32(96.0)))
SQ96 = float(np.sqrt(np.float32(96.0)))
EPS = 1e-5
NEG = -30000.0

QUEUES = ("pe", "act", "dve", "pool", "sp")


class Buf:
    __slots__ = ("writers", "readers", "chan")

    def __init__(self):
        self.writers = {}
        self.readers = {}
        self.chan = {}


class Prog:

    def __init__(self):
        self.q = {e: [] for e in QUEUES}
        self.cnt = {p: 0 for p in QUEUES}
        self.marks = {p: set() for p in QUEUES}
        self.know = {e: {} for e in QUEUES}
        self.clock = {p: {} for p in QUEUES}
        self.nchan = 0

    def _chan(self, buf, queue):
        ch = buf.chan.get(queue)
        if ch is None:
            ch = buf.chan[queue] = f"D{self.nchan}"
            self.nchan += 1
            self.cnt[ch] = 0
            self.marks[ch] = set()
            self.clock[ch] = {}
        return ch

    def _emit(self, eng, fn, reads, writes, producer):
        need = {}
        for b in reads:
            for p, v in b.writers.items():
                if need.get(p, 0) < v:
                    need[p] = v
        for b in writes:
            for d in (b.writers, b.readers):
                for p, v in d.items():
                    if need.get(p, 0) < v:
                        need[p] = v
        k = self.know[eng]
        for p in sorted(need):
            v = need[p]
            if p == "pe" and producer == "pe":
                continue
            if k.get(p, 0) >= v:
                continue
            self.q[eng].append(("w", p, v))
            self.marks[p].add(v)
            c = self.clock[p].get(v)
            if c:
                for pp, vv in c.items():
                    if k.get(pp, 0) < vv:
                        k[pp] = vv
            if k.get(p, 0) < v:
                k[p] = v
        vidx = self.cnt[producer] + 1
        self.cnt[producer] = vidx
        if producer.startswith("D"):
            self.marks[producer].add(vidx)
        c = dict(k)
        c[producer] = vidx
        self.clock[producer][vidx] = c
        self.q[eng].append(("i", fn, producer, vidx))
        for b in reads:
            if b.readers.get(producer, 0) < vidx:
                b.readers[producer] = vidx
        for b in writes:
            if b.writers.get(producer, 0) < vidx:
                b.writers[producer] = vidx

    def ins(self, eng, fn, reads=(), writes=()):
        self._emit(eng, fn, reads, writes, eng)

    def dma(self, queue, fn, reads=(), writes=()):
        assert writes, "dma needs a target buf (its channel)"
        self._emit(queue, fn, reads, writes, self._chan(writes[0], queue))

    def barrier(self):
        snap = {p: v for p, v in self.cnt.items() if v > 0}
        for eng in QUEUES:
            k = self.know[eng]
            for p, v in snap.items():
                if k.get(p, 0) < v:
                    self.q[eng].append(("w", p, v))
                    self.marks[p].add(v)
                    k[p] = v

    def retire(self):
        k = self.know["sp"]
        for p, v in self.cnt.items():
            if p.startswith("D") and v > 0 and k.get(p, 0) < v:
                self.q["sp"].append(("w", p, v))
                self.marks[p].add(v)
                k[p] = v

    def replay(self, nc):
        smarks = {p: sorted(m) for p, m in self.marks.items()}
        step = {p: (16 if p.startswith("D") else 1) for p in self.marks}
        with contextlib.ExitStack() as es:
            sems = {p: es.enter_context(nc.semaphore("sem_" + p))
                    for p, m in self.marks.items() if m}

            def runner(entries):
                def run(handle):
                    for e in entries:
                        if e[0] == "w":
                            _, p, v = e
                            handle.wait_ge(
                                sems[p],
                                bisect.bisect_right(smarks[p], v) * step[p])
                        else:
                            _, fn, p, vidx = e
                            inst = fn(handle)
                            if vidx in self.marks[p]:
                                inst.then_inc(sems[p], step[p])
                return run

            with nc.Block() as block:
                block.tensor(runner(self.q["pe"]))
                block.scalar(runner(self.q["act"]))
                block.vector(runner(self.q["dve"]))
                block.gpsimd(runner(self.q["pool"]))
                block.sync(runner(self.q["sp"]))


def rope_scale_table():
    t = np.arange(T, dtype=np.float64)
    inv = 1.0 / 10000.0 ** (np.arange(0, 32, 2, dtype=np.float64) / 32)
    freqs = t[:, None] * inv[None, :]
    emb = np.concatenate([freqs, freqs], -1)
    return (np.cos(emb) + np.sin(emb)).astype(np.float32)


def make_consts():
    scT = np.ascontiguousarray(rope_scale_table().T)
    sc4 = np.tile(scT, (4, 1)) * np.float32(INV_SQ)
    masks = np.zeros((4, 128, 512), np.float32)
    p = np.arange(128)[:, None]
    cc = np.arange(512)[None, :]
    for m in range(4):
        masks[m] = np.where(cc >= 128 * m + p, 0.0, NEG)
    ident = np.eye(128, dtype=np.float32)
    emat = np.zeros((13, 128, 16), np.float32)
    for m in range(8):
        emat[m, 0:64, 2 * m] = 1.0
        emat[m, 64:128, 2 * m + 1] = 1.0
    for m in range(4):
        for g in range(4):
            emat[8 + m, 32 * g:32 * g + 32, 4 * m + g] = 1.0
    emat[12, 0:32, :] = 1.0
    return sc4, masks, ident, emat


def build_program(stop_after=None):
    nc = bass.Bass("TRN2", target_bir_lowering=False, debug=False)
    P = Prog()

    def din(name, shape):
        return nc.dram_tensor(name, shape, F32, kind="ExternalInput").ap()

    x_h = din("x", [T, C])
    wdqt_h = din("wdqt", [4096, C])
    wuq_h = din("wuq", [4096, C])
    wqr_h = din("wqr", [4096, 512])
    wdkv_h = din("wdkv", [C, 256])
    wuk_h = din("wuk", [256, C])
    wuv_h = din("wuv", [256, C])
    wkr_h = din("wkr", [C, 32])
    wo_h = din("wo", [C, C])
    gwt_h = din("gwt", [C, 8])
    w1_h = din("w1", [NE, C, C])
    w2_h = din("w2", [NE, C, C])
    w3_h = din("w3", [NE, C, C])
    sc4_h = din("sc4", [128, T])
    mask_h = din("maskc", [4, 128, 512])
    ident_h = din("identc", [128, 128])
    emat_h = din("ematc", [13, 128, 16])

    out_h = nc.dram_tensor("out", [T, C], F32, kind="ExternalOutput").ap()

    q_scr = nc.dram_tensor("q_scr", [H, 96, T], BF16, kind="Internal").ap()
    k_scr = nc.dram_tensor("k_scr", [H, 64, T], BF16, kind="Internal").ap()
    v_scr = nc.dram_tensor("v_scr", [H, 64, T], BF16, kind="Internal").ap()
    x1_scr = nc.dram_tensor("x1_scr", [T, C], F32, kind="Internal").ap()
    w1b_scr = nc.dram_tensor("w1b_scr", [NE, C, C], BF16, kind="Internal").ap()
    w2b_scr = nc.dram_tensor("w2b_scr", [NE, C, C], BF16, kind="Internal").ap()
    w3b_scr = nc.dram_tensor("w3b_scr", [NE, C, C], BF16, kind="Internal").ap()
    ln1_scr = nc.dram_tensor("ln1_scr", [8, 128, T], BF16, kind="Internal").ap()
    g_scr = nc.dram_tensor("g_scr", [8, T], F32, kind="Internal").ap()

    DB = {}
    for n in ("x", "wdqt", "wuq", "wqr", "wdkv", "wuk", "wuv", "wkr", "wo",
              "gwt", "w1", "w2", "w3", "sc4", "maskc", "identc", "ematc",
              "out", "q_scr", "k_scr", "v_scr", "x1_scr", "ln1_scr", "g_scr",
              "w1b", "w2b", "w3b"):
        DB[n] = Buf()

    def sb(es, name, shape, dtype):
        t = es.enter_context(nc.sbuf_tensor(name, shape, dtype))
        return t, Buf()

    es_top = contextlib.ExitStack()
    with es_top:
        identf, identf_b = sb(es_top, "identf", [128, 128], F32)
        identb, identb_b = sb(es_top, "identb", [128, 128], BF16)
        onesf, onesf_b = sb(es_top, "onesf", [1, 128], F32)

        P.dma("sp", lambda e: e.dma_start(out=identf[:, :], in_=ident_h[:, :]),
              reads=[DB["identc"]], writes=[identf_b])
        P.dma("pool", lambda e: e.dma_start(out=identb[:, :], in_=ident_h[:, :]),
              reads=[DB["identc"]], writes=[identb_b])
        epsb, epsb_b = sb(es_top, "epsb", [128, 1], F32)
        P.ins("dve", lambda e: e.memset(onesf[:, :], 1.0), writes=[onesf_b])
        P.ins("dve", lambda e: e.memset(epsb[:, :], EPS), writes=[epsb_b])

        STATS = [sb(es_top, f"st{i}", [128, 8], F32) for i in range(16)]

        def layer_norm_group(items):
            views = []
            for (x_ap, x_buf, out_ap, out_buf, sq_t, sq_tb, slot) in items:
                st_t, st_b = slot
                views.append((x_ap, x_buf, out_ap, out_buf, sq_t, sq_tb,
                              st_t[:, 0:1], st_t[:, 1:2], st_t[:, 2:3],
                              st_t[:, 3:4], st_t[:, 4:5], st_t[:, 5:6], st_b))
            for (x_ap, x_buf, _, _, _, _, s1, *_r, st_b) in views:
                P.ins("dve", lambda e, o=s1, i=x_ap: e.reduce_sum(o, i, axis=AX.X),
                      reads=[x_buf], writes=[st_b])
            for (x_ap, x_buf, _, _, sq_t, sq_tb, *_r) in views:
                P.ins("act", lambda e, o=sq_t[:, :], i=x_ap:
                      e.activation(o, i, AF.Square),
                      reads=[x_buf], writes=[sq_tb])
            for (_, _, _, _, sq_t, sq_tb, s1, s2, *_r, st_b) in views:
                P.ins("dve", lambda e, o=s2, i=sq_t[:, :]:
                      e.reduce_sum(o, i, axis=AX.X),
                      reads=[sq_tb], writes=[st_b])
            for (_, _, _, _, _, _, s1, s2, mm_, m2, vv, rr, st_b) in views:
                P.ins("dve", lambda e, o=mm_, i=s1:
                      e.tensor_scalar_mul(o, i, 1.0 / C),
                      reads=[st_b], writes=[st_b])
            for (_, _, _, _, _, _, s1, s2, mm_, m2, vv, rr, st_b) in views:
                P.ins("dve", lambda e, o=m2, i=s1: e.scalar_tensor_tensor(
                    o, i, 1.0 / (C * C), i, ALU.mult, ALU.mult),
                    reads=[st_b], writes=[st_b])
            for (_, _, _, _, _, _, s1, s2, mm_, m2, vv, rr, st_b) in views:
                P.ins("dve", lambda e, o=vv, i=s2, m=m2: e.scalar_tensor_tensor(
                    o, i, 1.0 / C, m, ALU.mult, ALU.subtract),
                    reads=[st_b], writes=[st_b])
            for (_, _, _, _, _, _, s1, s2, mm_, m2, vv, rr, st_b) in views:
                P.ins("act", lambda e, o=vv: e.activation(o, o, AF.Sqrt,
                                                          bias=epsb[:, 0:1]),
                      reads=[st_b, epsb_b], writes=[st_b])
            for (_, _, _, _, _, _, s1, s2, mm_, m2, vv, rr, st_b) in views:
                P.ins("dve", lambda e, o=rr, i=vv: e.reciprocal(o, i),
                      reads=[st_b], writes=[st_b])
            for (x_ap, x_buf, out_ap, out_buf, _, _, s1, s2, mm_, m2, vv, rr,
                 st_b) in views:
                P.ins("dve", lambda e, o=out_ap, i=x_ap, m=mm_, r=rr:
                      e.tensor_scalar(o, i, m, r, ALU.subtract, ALU.mult),
                      reads=[x_buf, st_b], writes=[out_buf])

        def layer_norm(x_ap, x_buf, out_ap, out_buf, sq_t, sq_tb, slot):
            layer_norm_group([(x_ap, x_buf, out_ap, out_buf, sq_t, sq_tb, slot)])

        def make_gstat(es, name):
            t, _ = sb(es, name, [128, 32], F32)
            return (t, [Buf() for _ in range(4)], [Buf() for _ in range(4)],
                    Buf(), Buf(), Buf(), Buf())

        def layer_norm_group_v2(items, gs):
            gst, s1b, s2b, mqb, m2qb, vvqb, rrqb = gs
            n = len(items)
            assert n <= 4
            s1q, s2q = gst[:, 0:4], gst[:, 4:8]
            mq, m2q = gst[:, 8:12], gst[:, 12:16]
            vvq, rrq = gst[:, 16:20], gst[:, 20:24]
            for i, (x_ap, x_buf, _, _, _, _) in enumerate(items):
                P.ins("dve", lambda e, o=gst[:, i:i + 1], x=x_ap:
                      e.reduce_sum(o, x, axis=AX.X),
                      reads=[x_buf], writes=[s1b[i]])
            for i, (x_ap, x_buf, _, _, sq_t, sq_tb) in enumerate(items):
                P.ins("act", lambda e, o=sq_t[:, :], x=x_ap,
                      a=gst[:, 4 + i:5 + i]:
                      e.activation(o, x, AF.Square, accum_out=a),
                      reads=[x_buf], writes=[sq_tb, s2b[i]])
            P.ins("dve", lambda e: e.tensor_scalar_mul(mq, s1q, 1.0 / C),
                  reads=s1b, writes=[mqb])
            P.ins("dve", lambda e: e.scalar_tensor_tensor(
                m2q, s1q, 1.0 / (C * C), s1q, ALU.mult, ALU.mult),
                reads=s1b, writes=[m2qb])
            P.ins("dve", lambda e: e.scalar_tensor_tensor(
                vvq, s2q, 1.0 / C, m2q, ALU.mult, ALU.subtract),
                reads=s2b + [m2qb], writes=[vvqb])
            P.ins("act", lambda e: e.activation(vvq, vvq, AF.Sqrt,
                                                bias=epsb[:, 0:1]),
                  reads=[vvqb, epsb_b], writes=[vvqb])
            P.ins("dve", lambda e: e.reciprocal(rrq, vvq),
                  reads=[vvqb], writes=[rrqb])
            for i, (x_ap, x_buf, out_ap, out_buf, _, _) in enumerate(items):
                P.ins("dve", lambda e, o=out_ap, x=x_ap, m=gst[:, 8 + i:9 + i],
                      r=gst[:, 20 + i:21 + i]:
                      e.tensor_scalar(o, x, m, r, ALU.subtract, ALU.mult),
                      reads=[x_buf, mqb, rrqb], writes=[out_buf])

        es_bk1 = contextlib.ExitStack()
        bound, bound_b = sb(es_bk1, "bound", [16, T], F32)
        kr_s, kr_b = sb(es_bk1, "kr_s", [32, T], BF16)
        neg1b, neg1b_b = sb(es_bk1, "neg1b", [1, T], BF16)
        P.ins("dve", lambda e: e.memset(neg1b[:, :], -1.0), writes=[neg1b_b])

        es_weff = contextlib.ExitStack()
        weff, weff_b = sb(es_weff, "weff", [128, 8 * C], BF16)
        wqre, wqre_b = sb(es_weff, "wqre", [128, 8 * 512], BF16)
        es_a = contextlib.ExitStack()
        kn2, kn2_b = sb(es_a, "kn2", [16, T], BF16)
        qn2, qn2_b = sb(es_a, "qn2", [16, T], BF16)
        sc4s, sc4s_b = sb(es_a, "sc4s", [128, T], BF16)
        ematb, ematb_b = sb(es_a, "ematb", [128, 13 * 16], BF16)
        wdkv_s, wdkv_sb = sb(es_a, "wdkv_s", [128, 8 * 256], BF16)
        wuk_s, wuk_sb = sb(es_a, "wuk_s", [128, 2 * C], BF16)
        wuv_s, wuv_sb = sb(es_a, "wuv_s", [128, 2 * C], BF16)
        wkr_s, wkr_sb = sb(es_a, "wkr_s", [128, 8 * 32], BF16)

        STG = [sb(es_a, f"stg{i}", [128, 512], BF16) for i in range(4)]
        ckv, ckv_b = sb(es_a, "ckv", [128, 2 * 512], BF16)

        es_w = contextlib.ExitStack()
        wdqt_s, wdqt_sb = sb(es_w, "wdqt_s", [128, 32 * C], BF16)
        WRH = [sb(es_w, f"wrhs{i}", [128, 512], BF16) for i in range(4)]
        ps, psb = [], []
        for i in range(8):
            ps.append(es_w.enter_context(
                nc.psum_tensor(f"pw{i}", [128, 512], F32)))
            psb.append(Buf())

        for kb in range(32):
            P.dma("pool", lambda e, d=wdqt_s[:, kb * C:(kb + 1) * C],
                  s=wdqt_h[kb * 128:(kb + 1) * 128, :]:
                  e.dma_start(out=d, in_=s),
                  reads=[DB["wdqt"]], writes=[wdqt_sb])

        passes = [(wuq_h, DB["wuq"], 0, "weff"),
                  (wuq_h, DB["wuq"], 512, "weff"),
                  (wqr_h, DB["wqr"], 0, "wqre")]
        for rhs_h, rhs_db, off, dname in passes:
            for kb in range(32):
                wr, wr_b = WRH[kb % 4]
                P.dma("pool", lambda e, d=wr[:, :],
                      s=rhs_h[kb * 128:(kb + 1) * 128, off:off + 512]:
                      e.dma_start(out=d, in_=s),
                      reads=[rhs_db], writes=[wr_b])
                for mi in range(8):
                    P.ins("pe", lambda e, o=ps[mi][:, 0:512],
                          a=wdqt_s[:, kb * C + mi * 128:kb * C + (mi + 1) * 128],
                          b=wr[:, :],
                          st=(kb == 0), sp=(kb == 31):
                          e.matmul(o, a, b, start=st, stop=sp),
                          reads=[wdqt_sb, wr_b], writes=[psb[mi]])
            for mi in range(8):
                if dname == "weff":
                    d = weff[:, mi * C + off:mi * C + off + 512]
                    dbf = weff_b
                else:
                    d = wqre[:, mi * 512:(mi + 1) * 512]
                    dbf = wqre_b
                P.ins("act", lambda e, dd=d, s=ps[mi][:, 0:512]:
                      e.activation(dd, s, AF.Identity),
                      reads=[psb[mi]], writes=[dbf])
        es_w.close()
        P.barrier()

        es_a2 = contextlib.ExitStack()
        SQB = [sb(es_a2, f"sqb{i}", [128, 512], BF16) for i in range(12)]
        SQK = [sb(es_a2, f"sqk{i}", [128, 512], BF16) for i in range(9)]
        LNX = [sb(es_a2, f"lnx{i}", [128, 8 * TAU], BF16) for i in range(2)]
        XT = [sb(es_a2, f"xta{i}", [128, C], F32) for i in range(4)]
        SQA = [sb(es_a2, f"sqa{i}", [128, C], F32) for i in range(4)]
        LNT = [sb(es_a2, f"lnt{i}", [128, C], BF16) for i in range(4)]

        ps, psb = [], []
        for i in range(6):
            ps.append(es_top.enter_context(
                nc.psum_tensor(f"p{i}", [128, 512], F32)))
            psb.append(Buf())
        for i in (6, 7):
            ps.append(es_top.enter_context(
                nc.psum_tensor(f"p{i}", [128, 1024], BF16)))
            psb.append(Buf())

        P.dma("pool", lambda e: e.dma_start(out=sc4s[:, :], in_=sc4_h[:, :]),
              reads=[DB["sc4"]], writes=[sc4s_b])
        P.dma("pool", lambda e: e.dma_start(
            out=ematb[:, :].rearrange("p (k j) -> p k j", j=16),
            in_=emat_h.rearrange("k p j -> p k j")),
            reads=[DB["ematc"]], writes=[ematb_b])
        P.dma("pool", lambda e: e.dma_start(
            out=wdkv_s[:, :].rearrange("p (c j) -> p c j", j=256),
            in_=wdkv_h.rearrange("(c p) j -> p c j", p=128)),
            reads=[DB["wdkv"]], writes=[wdkv_sb])
        P.dma("pool", lambda e: e.dma_start(
            out=wuk_s[:, :].rearrange("p (c j) -> p c j", j=C),
            in_=wuk_h.rearrange("(c p) j -> p c j", p=128)),
            reads=[DB["wuk"]], writes=[wuk_sb])
        P.dma("pool", lambda e: e.dma_start(
            out=wuv_s[:, :].rearrange("p (c j) -> p c j", j=C),
            in_=wuv_h.rearrange("(c p) j -> p c j", p=128)),
            reads=[DB["wuv"]], writes=[wuv_sb])
        P.dma("pool", lambda e: e.dma_start(
            out=wkr_s[:, :].rearrange("p (c j) -> p c j", j=32),
            in_=wkr_h.rearrange("(c p) j -> p c j", p=128)),
            reads=[DB["wkr"]], writes=[wkr_sb])

        def emit_ln1_chain(tau):
            items = []
            for tt in range(4):
                tg = tau * 4 + tt
                xv, xv_b = XT[tt]
                sqv_t, sqv_tb = SQA[tt]
                lnt, lnt_b = LNT[tt]
                P.dma("sp", lambda e, d=xv[:, :],
                      s=x_h[tg * 128:(tg + 1) * 128, :]:
                      e.dma_start(out=d, in_=s),
                      reads=[DB["x"]], writes=[xv_b])
                items.append((xv[:, :], xv_b, lnt[:, :], lnt_b, sqv_t, sqv_tb,
                              STATS[tt]))
            layer_norm_group(items)

        def emit_ln1_transposes(tau):
            lnxv, lnx_b = LNX[tau % 2]
            for tt in range(4):
                lnt, lnt_b = LNT[tt]
                for g in range(2):
                    pt, pt_b = ps[6 + g], psb[6 + g]
                    for c4 in range(4):
                        c = g * 4 + c4
                        P.ins("pe", lambda e, o=pt[:, c4 * 128:(c4 + 1) * 128],
                              i=lnt[:, c * 128:(c + 1) * 128]:
                              e.transpose(o, i, identb[:, :]),
                              reads=[lnt_b, identb_b], writes=[pt_b])
                    src = pt[:, 0:512].rearrange("p (c j) -> p c j", j=128)
                    dst = lnxv[:, :].rearrange("p (c j) -> p c j", j=TAU)[
                        :, g * 4:(g + 1) * 4, tt * 128:(tt + 1) * 128]
                    P.ins("dve", lambda e, d=dst, s=src: e.tensor_copy(d, s),
                          reads=[pt_b], writes=[lnx_b])

        emit_ln1_chain(0)
        emit_ln1_transposes(0)
        for tau in range(NTAU):
            lnxv, lnx_b = LNX[tau % 2]
            t0 = tau * TAU
            if tau + 1 < NTAU:
                emit_ln1_chain(tau + 1)

            qnorm = []
            knorm = []
            for m in range(8):
                pm, pm_b = ps[m % 2], psb[m % 2]
                for c_ in range(8):
                    P.ins("pe", lambda e, o=pm[:, 0:512],
                          a=weff[:, c_ * C + m * 128:c_ * C + (m + 1) * 128],
                          b=lnxv[:, c_ * TAU:(c_ + 1) * TAU],
                          st=(c_ == 0), sp=(c_ == 7):
                          e.matmul(o, a, b, start=st, stop=sp),
                          reads=[weff_b, lnx_b], writes=[pm_b])
                stv, stv_b = STG[m % 4]
                P.ins("act", lambda e, d=stv[:, :], s=pm[:, 0:512]:
                      e.activation(d, s, AF.Identity, scale=INV_SQ),
                      reads=[pm_b], writes=[stv_b])
                dst = q_scr[2 * m:2 * m + 2, 0:64, t0:t0 + TAU]
                P.dma("sp", lambda e, d=dst, s=stv[:, :]: e.dma_start(out=d, in_=s),
                      reads=[stv_b], writes=[DB["q_scr"]])
                sqv, sqv_b = SQB[m]
                P.ins("act", lambda e, d=sqv[:, :], s=stv[:, :]:
                      e.activation(d, s, AF.Square),
                      reads=[stv_b], writes=[sqv_b])
                qnorm.append((ematb[:, m * 16:(m + 1) * 16], sqv[:, :], sqv_b,
                              128))

            for m in range(4):
                pm, pm_b = ps[m % 2], psb[m % 2]
                for c_ in range(8):
                    P.ins("pe", lambda e, o=pm[:, 0:512],
                          a=wqre[:, c_ * 512 + m * 128:c_ * 512 + (m + 1) * 128],
                          b=lnxv[:, c_ * TAU:(c_ + 1) * TAU],
                          st=(c_ == 0), sp=(c_ == 7):
                          e.matmul(o, a, b, start=st, stop=sp),
                          reads=[wqre_b, lnx_b], writes=[pm_b])
                stv, stv_b = STG[m % 4]
                P.ins("dve", lambda e, d=stv[:, :], s=pm[:, 0:512],
                      sc=sc4s[:, t0:t0 + TAU]: e.tensor_mul(d, s, sc),
                      reads=[pm_b, sc4s_b], writes=[stv_b])
                dst = q_scr[4 * m:4 * m + 4, 64:96, t0:t0 + TAU]
                P.dma("sp", lambda e, d=dst, s=stv[:, :]: e.dma_start(out=d, in_=s),
                      reads=[stv_b], writes=[DB["q_scr"]])
                sqv, sqv_b = SQB[8 + m]
                P.ins("act", lambda e, d=sqv[:, :], s=stv[:, :]:
                      e.activation(d, s, AF.Square),
                      reads=[stv_b], writes=[sqv_b])
                qnorm.append((ematb[:, (8 + m) * 16:(9 + m) * 16], sqv[:, :],
                              sqv_b, 128))

            for mc in range(2):
                for c_ in range(8):
                    P.ins("pe", lambda e, o=ps[4][:, 0:512],
                          a=wdkv_s[:, c_ * 256 + mc * 128:c_ * 256 + (mc + 1) * 128],
                          b=lnxv[:, c_ * TAU:(c_ + 1) * TAU],
                          st=(c_ == 0), sp=(c_ == 7):
                          e.matmul(o, a, b, start=st, stop=sp),
                          reads=[wdkv_sb, lnx_b], writes=[psb[4]])
                P.ins("act", lambda e, d=ckv[:, mc * 512:(mc + 1) * 512],
                      s=ps[4][:, 0:512]: e.activation(d, s, AF.Identity),
                      reads=[psb[4]], writes=[ckv_b])

            for m in range(8):
                pm, pm_b = ps[m % 2], psb[m % 2]
                for c2 in range(2):
                    P.ins("pe", lambda e, o=pm[:, 0:512],
                          a=wuk_s[:, c2 * C + m * 128:c2 * C + (m + 1) * 128],
                          b=ckv[:, c2 * 512:(c2 + 1) * 512],
                          st=(c2 == 0), sp=(c2 == 1):
                          e.matmul(o, a, b, start=st, stop=sp),
                          reads=[wuk_sb, ckv_b], writes=[pm_b])
                stv, stv_b = STG[m % 4]
                P.ins("act", lambda e, d=stv[:, :], s=pm[:, 0:512]:
                      e.activation(d, s, AF.Identity),
                      reads=[pm_b], writes=[stv_b])
                dst = k_scr[2 * m:2 * m + 2, :, t0:t0 + TAU]
                P.dma("sp", lambda e, d=dst, s=stv[:, :]: e.dma_start(out=d, in_=s),
                      reads=[stv_b], writes=[DB["k_scr"]])
                sqv, sqv_b = SQK[m]
                P.ins("act", lambda e, d=sqv[:, :], s=stv[:, :]:
                      e.activation(d, s, AF.Square),
                      reads=[stv_b], writes=[sqv_b])
                knorm.append((ematb[:, m * 16:(m + 1) * 16], sqv[:, :],
                              sqv_b, 128))

            for c_ in range(8):
                P.ins("pe", lambda e, o=ps[4][0:32, 0:512],
                      a=wkr_s[:, c_ * 32:(c_ + 1) * 32],
                      b=lnxv[:, c_ * TAU:(c_ + 1) * TAU],
                      st=(c_ == 0), sp=(c_ == 7):
                      e.matmul(o, a, b, start=st, stop=sp),
                      reads=[wkr_sb, lnx_b], writes=[psb[4]])
            P.ins("dve", lambda e, d=kr_s[:, t0:t0 + TAU],
                  s=ps[4][0:32, 0:512], sc=sc4s[0:32, t0:t0 + TAU]:
                  e.scalar_tensor_tensor(d, s, SQ96, sc, ALU.mult, ALU.mult),
                  reads=[psb[4], sc4s_b], writes=[kr_b])
            sqv, sqv_b = SQK[8]
            P.ins("act", lambda e, d=sqv[0:32, :],
                  s=kr_s[0:32, t0:t0 + TAU]: e.activation(d, s, AF.Square),
                  reads=[kr_b], writes=[sqv_b])
            knorm.append((ematb[0:32, 12 * 16:13 * 16], sqv[0:32, :],
                          sqv_b, 32))

            for m in range(8):
                for c2 in range(2):
                    P.ins("pe", lambda e, o=ps[5][:, 0:512],
                          a=wuv_s[:, c2 * C + m * 128:c2 * C + (m + 1) * 128],
                          b=ckv[:, c2 * 512:(c2 + 1) * 512],
                          st=(c2 == 0), sp=(c2 == 1):
                          e.matmul(o, a, b, start=st, stop=sp),
                          reads=[wuv_sb, ckv_b], writes=[psb[5]])
                stv, stv_b = STG[m % 4]
                P.ins("act", lambda e, d=stv[:, :], s=ps[5][:, 0:512]:
                      e.activation(d, s, AF.Identity),
                      reads=[psb[5]], writes=[stv_b])
                dst = v_scr[2 * m:2 * m + 2, :, t0:t0 + TAU]
                P.dma("sp", lambda e, d=dst, s=stv[:, :]: e.dma_start(out=d, in_=s),
                      reads=[stv_b], writes=[DB["v_scr"]])

            for i, (ea, sqa_, sqb_, kdim) in enumerate(qnorm):
                P.ins("pe", lambda e, a=ea, bq=sqa_,
                      st=(i == 0), sp=(i == len(qnorm) - 1):
                      e.matmul(ps[2][0:16, 0:512], a, bq, start=st, stop=sp),
                      reads=[ematb_b, sqb_], writes=[psb[2]])
            P.ins("dve", lambda e, d=qn2[:, t0:t0 + TAU],
                  s=ps[2][0:16, 0:512]: e.tensor_copy(d, s),
                  reads=[psb[2]], writes=[qn2_b])
            for i, (ea, sqa_, sqb_, kdim) in enumerate(knorm):
                P.ins("pe", lambda e, a=ea, bq=sqa_,
                      st=(i == 0), sp=(i == len(knorm) - 1):
                      e.matmul(ps[3][0:16, 0:512], a, bq, start=st, stop=sp),
                      reads=[ematb_b, sqb_], writes=[psb[3]])
            P.ins("dve", lambda e, d=kn2[:, t0:t0 + TAU],
                  s=ps[3][0:16, 0:512]: e.tensor_copy(d, s),
                  reads=[psb[3]], writes=[kn2_b])

            if tau + 1 < NTAU:
                emit_ln1_transposes(tau + 1)

        kst, kst_b = STATS[2]
        kmax = kst[0:16, 6:7]
        P.ins("dve", lambda e: e.reduce_max(kmax, kn2[:, :], axis=AX.X),
              reads=[kn2_b], writes=[kst_b])
        P.ins("act", lambda e: e.activation(kmax, kmax, AF.Sqrt),
              reads=[kst_b], writes=[kst_b])
        P.ins("act", lambda e: e.activation(bound[:, :], qn2[:, :], AF.Sqrt),
              reads=[qn2_b], writes=[bound_b])
        P.ins("dve", lambda e: e.tensor_scalar_mul(bound[:, :], bound[:, :], kmax),
              reads=[bound_b, kst_b], writes=[bound_b])

        es_a2.close()
        es_a.close()
        es_weff.close()
        P.barrier()
        if stop_after == "A":
            es_bk1.close()
            P.retire()
            P.replay(nc)
            return nc

        es_bk2 = contextlib.ExitStack()
        o_all, o_all_b = sb(es_bk2, "o_all", [128, 8 * T], BF16)

        es_bs = contextlib.ExitStack()
        QBUF = [sb(es_bs, f"qbuf{i}", [128, T], BF16) for i in range(2)]
        KBUF = [sb(es_bs, f"kbuf{i}", [128, T], BF16) for i in range(2)]
        VFM = [sb(es_bs, f"vfm{i}", [64, T], BF16) for i in range(2)]
        VA = [sb(es_bs, f"va{i}", [128, 32 * 65], BF16) for i in range(2)]
        PTB = [sb(es_bs, f"ptb{i}", [128, 512], BF16) for i in range(4)]
        maskb, maskb_b = sb(es_bs, "maskb", [128, 4 * 512], BF16)
        tsc, tsc_b = sb(es_bs, "tsc", [128, 8 * C], BF16)
        RB64 = [sb(es_bs, f"rb64_{i}", [64, 512], F32) for i in range(2)]
        RROW = [sb(es_bs, f"rrow{i}", [1, 512], F32) for i in range(2)]
        RH = [sb(es_bs, f"rh{i}", [1, 512], BF16) for i in range(2)]
        RL = [sb(es_bs, f"rl{i}", [1, 512], BF16) for i in range(2)]
        RLF = [sb(es_bs, f"rlf{i}", [1, 512], F32) for i in range(2)]
        onesbf, onesbf_b = sb(es_bs, "onesbf", [1, 128], BF16)
        P.ins("dve", lambda e: e.memset(onesbf[:, :], 1.0),
              writes=[onesbf_b])

        P.dma("pool", lambda e: e.dma_start(
            out=maskb[:, :].rearrange("p (m j) -> p m j", j=512),
            in_=mask_h.rearrange("m p j -> p m j")),
            reads=[DB["maskc"]], writes=[maskb_b])

        for sl in range(2):
            qb_, qb_b = QBUF[sl]
            kb_, kb_b = KBUF[sl]
            va_, va_b = VA[sl]
            P.ins("dve", lambda e, d=qb_[64:128, :]: e.memset(d, 0.0),
                  writes=[qb_b])
            P.ins("dve", lambda e, d=kb_[64:128, :]: e.memset(d, 0.0),
                  writes=[kb_b])
            P.ins("dve", lambda e, d=va_[:, :]: e.memset(d, 1.0), writes=[va_b])
            P.dma("sp", lambda e, d=kb_[64:96, :]:
                  e.dma_start(out=d, in_=kr_s[:, :]),
                  reads=[kr_b], writes=[kb_b])
            P.dma("sp", lambda e, d=kb_[96:97, :]:
                  e.dma_start(out=d, in_=neg1b[:, :]),
                  reads=[neg1b_b], writes=[kb_b])

        _tlist = [(wh, db, cache, cdb, e)
                  for (wh, db, cache, cdb) in
                  ((w1_h, DB["w1"], w1b_scr, DB["w1b"]),
                   (w3_h, DB["w3"], w3b_scr, DB["w3b"]),
                   (w2_h, DB["w2"], w2b_scr, DB["w2b"]))
                  for e in range(NE)]

        def emit_transcode(idx):
            if idx >= len(_tlist):
                return
            wh, db, cache, cdb, e_idx = _tlist[idx]
            dst, dst_b = tsc, tsc_b
            for c_ in range(8):
                P.dma("pool", lambda e, d=dst[:, c_ * C:(c_ + 1) * C],
                      s=wh[e_idx, c_ * 128:(c_ + 1) * 128, :]:
                      e.dma_start(out=d, in_=s),
                      reads=[db], writes=[dst_b])
            for c_ in range(8):
                P.dma("sp", lambda e,
                      d=cache[e_idx, c_ * 128:(c_ + 1) * 128, :],
                      s=dst[:, c_ * C:(c_ + 1) * C]:
                      e.dma_start(out=d, in_=s),
                      reads=[dst_b], writes=[cdb])

        def emit_head_loads(h):
            sl = h % 2
            qb_, qb_b = QBUF[sl]
            kb_, kb_b = KBUF[sl]
            vf_, vf_b = VFM[sl]
            P.dma("sp", lambda e, d=qb_[0:96, :], s=q_scr[h, :, :]:
                  e.dma_start(out=d, in_=s),
                  reads=[DB["q_scr"]], writes=[qb_b])
            P.dma("pool", lambda e, d=qb_[96:97, :], s=bound[h:h + 1, :]:
                  e.dma_start(out=d, in_=s),
                  reads=[bound_b], writes=[qb_b])
            P.dma("sp", lambda e, d=kb_[0:64, :], s=k_scr[h, :, :]:
                  e.dma_start(out=d, in_=s),
                  reads=[DB["k_scr"]], writes=[kb_b])
            P.dma("sp", lambda e, d=vf_[:, :], s=v_scr[h, :, :]:
                  e.dma_start(out=d, in_=s),
                  reads=[DB["v_scr"]], writes=[vf_b])

        def emit_vtrans(h):
            sl = h % 2
            vf_, vf_b = VFM[sl]
            va_, va_b = VA[sl]
            for g in range(4):
                pt, pt_b = ps[6 + (g % 2)], psb[6 + (g % 2)]
                for j in range(8):
                    kt = g * 8 + j
                    P.ins("pe", lambda e, o=pt[:, j * 64:(j + 1) * 64],
                          i=vf_[:, kt * 128:(kt + 1) * 128]:
                          e.transpose(o, i, identb[0:64, 0:64]),
                          reads=[vf_b, identb_b], writes=[pt_b])
                src = pt[:, 0:512].rearrange("p (t c) -> p t c", c=64)
                dst = va_[:, :].rearrange("p (t c) -> p t c", c=65)[
                    :, g * 8:(g + 1) * 8, 0:64]
                P.ins("dve", lambda e, d=dst, s=src: e.tensor_copy(d, s),
                      reads=[pt_b], writes=[va_b])

        pend_norm = [None]
        emit_head_loads(0)
        emit_vtrans(0)
        for h in range(H):
            sl = h % 2
            qb_, qb_b = QBUF[sl]
            kb_, kb_b = KBUF[sl]
            va_, va_b = VA[sl]
            emit_transcode(2 * h)

            for qi in range(NQB):
                if qi == 4:
                    emit_transcode(2 * h + 1)
                    if h + 1 < H:
                        emit_head_loads(h + 1)
                if qi == 6 and h + 1 < H:
                    emit_vtrans(h + 1)
                po_i = 2 + (qi % 2)
                nkt = 4 * qi + 4

                def emit_ST(kt, qi2=None, h2=None, qi=qi, h=h):
                    if qi2 is None:
                        qi2 = qi
                    if h2 is None:
                        h2 = h
                    qb2, qb2_b = QBUF[h2 % 2]
                    kb2, kb2_b = KBUF[h2 % 2]
                    si = (0, 1, 4, 5)[kt % 4]
                    pS, pS_b = ps[si], psb[si]
                    diag = kt >= 4 * qi2
                    P.ins("pe", lambda e, o=pS[:, 0:512],
                          a=kb2[:, kt * 128:(kt + 1) * 128],
                          b=qb2[:, qi2 * 512:(qi2 + 1) * 512], sp=not diag:
                          e.matmul(o, a, b, start=True, stop=sp),
                          reads=[kb2_b, qb2_b], writes=[pS_b])
                    if diag:
                        mvar = kt - 4 * qi2
                        P.ins("pe", lambda e, o=pS[:, 0:512],
                              mk=maskb[:, mvar * 512:(mvar + 1) * 512]:
                              e.matmul(o, identb[:, :], mk,
                                       start=False, stop=True),
                              reads=[identb_b, maskb_b], writes=[pS_b])

                if h == 0 and qi == 0:
                    for kt in range(4):
                        emit_ST(kt)
                for kt in range(nkt):
                    if pend_norm[0] is not None:
                        if kt == 1:
                            pend_norm[0][0]()
                        if kt == min(4, nkt - 1):
                            pend_norm[0][1]()
                            pend_norm[0] = None
                    si = (0, 1, 4, 5)[kt % 4]
                    pS, pS_b = ps[si], psb[si]
                    ptv, ptv_b = PTB[kt % 4]
                    P.ins("act", lambda e, d=ptv[:, :], s=pS[:, 0:512]:
                          e.activation(d, s, AF.Exp),
                          reads=[pS_b], writes=[ptv_b])
                    P.ins("pe", lambda e, o=ps[po_i][0:65, 0:512],
                          a=va_[:, kt * 65:(kt + 1) * 65], b=ptv[:, :],
                          st=(kt == 0), sp=(kt == nkt - 1):
                          e.matmul(o, a, b, start=st, stop=sp),
                          reads=[va_b, ptv_b], writes=[psb[po_i]])
                    nx4 = kt + 4
                    if nx4 < nkt:
                        emit_ST(nx4)
                    elif qi + 1 < NQB:
                        emit_ST(nx4 - nkt, qi2=qi + 1)
                    elif h + 1 < H:
                        emit_ST(nx4 - nkt, qi2=0, h2=h + 1)

                def mk_norm(qi=qi, po_i=po_i, h=h):
                    rrow, rrow_b = RROW[qi % 2]
                    rh, rh_b = RH[qi % 2]
                    rl, rl_b = RL[qi % 2]
                    rlf, rlf_b = RLF[qi % 2]
                    rb64, rb64_b = RB64[qi % 2]

                    def norm_a():
                        P.ins("dve", lambda e, d=rrow[:, :], o=ps[po_i]:
                              e.reciprocal(d, o[64:65, 0:512]),
                              reads=[psb[po_i]], writes=[rrow_b])
                        P.ins("dve", lambda e: e.tensor_copy(rh[:, :],
                                                             rrow[:, :]),
                              reads=[rrow_b], writes=[rh_b])
                        P.ins("dve", lambda e: e.scalar_tensor_tensor(
                            rlf[:, :], rh[:, :], -1.0, rrow[:, :],
                            ALU.mult, ALU.add),
                            reads=[rh_b, rrow_b], writes=[rlf_b])
                        P.ins("dve", lambda e: e.tensor_copy(rl[:, :],
                                                             rlf[:, :]),
                              reads=[rlf_b], writes=[rl_b])

                    def norm_b():
                        P.ins("pe", lambda e, o=ps[po_i][64:128, 0:512],
                              r=rh[0:1, :]:
                              e.matmul(o, onesbf[0:1, 0:64], r,
                                       start=True, stop=False),
                              reads=[onesbf_b, rh_b], writes=[psb[po_i]])
                        P.ins("pe", lambda e, o=ps[po_i][64:128, 0:512],
                              r=rl[0:1, :]:
                              e.matmul(o, onesbf[0:1, 0:64], r,
                                       start=False, stop=True),
                              reads=[onesbf_b, rl_b], writes=[psb[po_i]])
                        P.ins("dve", lambda e, s=ps[po_i][64:128, 0:512],
                              d=rb64[:, :]: e.tensor_copy(d, s),
                              reads=[psb[po_i]], writes=[rb64_b])
                        dsto = o_all[64 * (h % 2):64 * (h % 2) + 64,
                                     (h // 2) * T + qi * 512:
                                     (h // 2) * T + (qi + 1) * 512]
                        P.ins("dve", lambda e, d=dsto, o=ps[po_i],
                              r=rb64[:, :]:
                              e.tensor_mul(d, o[0:64, 0:512], r),
                              reads=[psb[po_i], rb64_b], writes=[o_all_b])
                    return norm_a, norm_b

                pend_norm[0] = mk_norm()

        if pend_norm[0] is not None:
            pend_norm[0][0]()
            pend_norm[0][1]()
            pend_norm[0] = None
        es_bs.close()
        P.barrier()
        if stop_after == "B":
            es_bk2.close()
            es_bk1.close()
            P.retire()
            P.replay(nc)
            return nc

        es_c = contextlib.ExitStack()
        wo_s, wo_sb = sb(es_c, "wo_s", [128, 8 * C], BF16)
        gwt_s, gwt_sb = sb(es_c, "gwt_s", [128, 8 * 8], BF16)
        AO = [sb(es_c, f"ao{i}", [128, C], F32) for i in range(4)]
        SQC = [sb(es_c, f"sqc{i}", [128, C], F32) for i in range(4)]
        XTC = [sb(es_c, f"xtc{i}", [128, C], F32) for i in range(4)]
        LN1T = [sb(es_c, f"ln1t{i}", [128, C], BF16) for i in range(4)]
        ln1s, ln1s_b = sb(es_c, "ln1s", [128, 8 * 1024], BF16)
        gs_ln2 = make_gstat(es_c, "gsln2")
        gs_ln3 = make_gstat(es_c, "gsln3")
        gtsE, gtsE_b = sb(es_c, "gtsE", [128, 32], F32)
        gwb, gwb_b = sb(es_c, "gwb", [128, 32], F32)
        g4, _g4unused = sb(es_c, "g4", [128, 16], F32)
        g4s_b, g4r_b, g4m_b, g4m2_b = Buf(), Buf(), Buf(), Buf()
        gstg, gstg_b = sb(es_c, "gstg", [32, 128], F32)

        P.dma("pool", lambda e: e.dma_start(
            out=wo_s[:, :].rearrange("p (c j) -> p c j", j=C),
            in_=wo_h.rearrange("(c p) j -> p c j", p=128)),
            reads=[DB["wo"]], writes=[wo_sb])
        P.dma("pool", lambda e: e.dma_start(
            out=gwt_s[:, :].rearrange("p (c j) -> p c j", j=8),
            in_=gwt_h.rearrange("(c p) j -> p c j", p=128)),
            reads=[DB["gwt"]], writes=[gwt_sb])

        def emit_wo(t):
            ao, ao_b = AO[t % 4]
            for half in range(2):
                ph, ph_b = ps[half], psb[half]
                for c_ in range(8):
                    P.ins("pe", lambda e, o=ph[:, 0:512],
                          a=o_all[:, c_ * T + t * 128:c_ * T + (t + 1) * 128],
                          b=wo_s[:, c_ * C + half * 512:c_ * C + half * 512 + 512],
                          st=(c_ == 0), sp=(c_ == 7):
                          e.matmul(o, a, b, start=st, stop=sp),
                          reads=[o_all_b, wo_sb], writes=[ph_b])
                P.ins("act", lambda e, d=ao[:, half * 512:half * 512 + 512],
                      s=ph[:, 0:512]: e.activation(d, s, AF.Identity),
                      reads=[ph_b], writes=[ao_b])

        def emit_xload(t):
            xv, xv_b = XTC[t % 4]
            P.dma("sp", lambda e, d=xv[:, :], s=x_h[t * 128:(t + 1) * 128, :]:
                  e.dma_start(out=d, in_=s),
                  reads=[DB["x"]], writes=[xv_b])

        for t in range(4):
            emit_wo(t)
            emit_xload(t)
        for grp in range(NT // 4):
            ts4 = list(range(grp * 4, grp * 4 + 4))
            layer_norm_group_v2([
                (AO[t % 4][0][:, :], AO[t % 4][1], AO[t % 4][0][:, :],
                 AO[t % 4][1], SQC[t % 4][0], SQC[t % 4][1])
                for t in ts4], gs_ln2)
            for t in ts4:
                xv, xv_b = XTC[t % 4]
                P.ins("dve", lambda e, d=xv[:, :],
                      l=AO[t % 4][0][:, :]: e.tensor_add(d, d, l),
                      reads=[xv_b, AO[t % 4][1]], writes=[xv_b])
            if grp + 1 < NT // 4:
                for t in range(grp * 4 + 4, grp * 4 + 8):
                    emit_wo(t)
            for t in ts4:
                xv, xv_b = XTC[t % 4]
                P.dma("sp", lambda e, s=xv[:, :],
                      d=x1_scr[t * 128:(t + 1) * 128, :]:
                      e.dma_start(out=d, in_=s),
                      reads=[xv_b], writes=[DB["x1_scr"]])
            layer_norm_group_v2([
                (XTC[t % 4][0][:, :], XTC[t % 4][1], LN1T[t % 4][0][:, :],
                 LN1T[t % 4][1], SQC[t % 4][0], SQC[t % 4][1])
                for t in ts4], gs_ln3)
            if grp + 1 < NT // 4:
                for t in range(grp * 4 + 4, grp * 4 + 8):
                    emit_xload(t)
            for t in ts4:
                ln1t, ln1t_b = LN1T[t % 4]
                for g in range(2):
                    pt, pt_b = ps[6 + g], psb[6 + g]
                    for c4 in range(4):
                        c = g * 4 + c4
                        P.ins("pe", lambda e, o=pt[:, c4 * 128:(c4 + 1) * 128],
                              i=ln1t[:, c * 128:(c + 1) * 128]:
                              e.transpose(o, i, identb[:, :]),
                              reads=[ln1t_b, identb_b], writes=[pt_b])
                    src = pt[:, 0:512].rearrange("p (c j) -> p c j", j=128)
                    dst = ln1s[:, :].rearrange("p (c j) -> p c j", j=1024)[
                        :, g * 4:(g + 1) * 4, (t % 8) * 128:(t % 8) * 128 + 128]
                    P.ins("dve", lambda e, d=dst, s=src: e.tensor_copy(d, s),
                          reads=[pt_b], writes=[ln1s_b])
            for t in ts4:
                for c_ in range(8):
                    P.ins("pe", lambda e, o=ps[2][:, (t % 4) * 8:
                                                  (t % 4) * 8 + 8],
                          a=ln1s[:, c_ * 1024 + (t % 8) * 128:
                                 c_ * 1024 + (t % 8) * 128 + 128],
                          b=gwt_s[:, c_ * 8:(c_ + 1) * 8],
                          st=(c_ == 0), sp=(c_ == 7):
                          e.matmul(o, a, b, start=st, stop=sp),
                          reads=[ln1s_b, gwt_sb], writes=[psb[2]])
            g3 = gtsE[:, :].rearrange("p (g j) -> p g j", j=8)
            w3_ = gwb[:, :].rearrange("p (g j) -> p g j", j=8)
            g4s = g4[:, 0:4].rearrange("p (g j) -> p g j", j=1)
            g4r = g4[:, 4:8].rearrange("p (g j) -> p g j", j=1)
            g4m = g4[:, 8:12].rearrange("p (g j) -> p g j", j=1)
            g4m2 = g4[:, 12:16].rearrange("p (g j) -> p g j", j=1)
            P.ins("act", lambda e: e.activation(gtsE[:, :], ps[2][:, 0:32],
                                                AF.Exp),
                  reads=[psb[2]], writes=[gtsE_b])
            P.ins("dve", lambda e: e.tensor_reduce(g4s, g3, axis=AX.X,
                                                   op=ALU.add),
                  reads=[gtsE_b], writes=[g4s_b])
            P.ins("dve", lambda e: e.reciprocal(g4[:, 4:8], g4[:, 0:4]),
                  reads=[g4s_b], writes=[g4r_b])
            P.ins("dve", lambda e: e.tensor_reduce(g4m, g3, axis=AX.X,
                                                   op=ALU.max),
                  reads=[gtsE_b], writes=[g4m_b])
            P.ins("dve", lambda e, ab=broadcast_tensor_aps(g3, g4m):
                  e.tensor_tensor(w3_, ab[0], ab[1], ALU.is_ge),
                  reads=[gtsE_b, g4m_b], writes=[gwb_b])
            P.ins("dve", lambda e: e.scalar_tensor_tensor(
                gwb[:, :], gwb[:, :], -1e30, gtsE[:, :], ALU.mult, ALU.add),
                reads=[gwb_b, gtsE_b], writes=[gwb_b])
            P.ins("dve", lambda e: e.tensor_reduce(g4m2, w3_, axis=AX.X,
                                                   op=ALU.max),
                  reads=[gwb_b], writes=[g4m2_b])
            P.ins("dve", lambda e, ab=broadcast_tensor_aps(g3, g4m2):
                  e.tensor_tensor(w3_, ab[0], ab[1], ALU.is_ge),
                  reads=[gtsE_b, g4m2_b], writes=[gwb_b])
            P.ins("dve", lambda e: e.tensor_mul(gwb[:, :], gwb[:, :],
                                                gtsE[:, :]),
                  reads=[gwb_b, gtsE_b], writes=[gwb_b])
            P.ins("dve", lambda e, ab=broadcast_tensor_aps(w3_, g4r):
                  e.tensor_tensor(w3_, ab[0], ab[1], ALU.mult),
                  reads=[gwb_b, g4r_b], writes=[gwb_b])
            P.ins("pe", lambda e: e.transpose(ps[5][0:32, 0:128], gwb[:, :],
                                              identf[:, :]),
                  reads=[gwb_b, identf_b], writes=[psb[5]])
            P.ins("dve", lambda e: e.tensor_copy(gstg[:, :],
                                                 ps[5][0:32, 0:128]),
                  reads=[psb[5]], writes=[gstg_b])
            P.dma("sp", lambda e,
                  d=g_scr[:, grp * 512:grp * 512 + 512].rearrange(
                      "e (t j) -> t e j", j=128):
                  e.dma_start(out=d, in_=gstg[:, :]),
                  reads=[gstg_b], writes=[DB["g_scr"]])
            if grp % 2 == 1:
                for c_ in range(8):
                    P.dma("sp", lambda e,
                          d=ln1_scr[c_, :, (grp // 2) * 1024:
                                    (grp // 2) * 1024 + 1024],
                          s=ln1s[:, c_ * 1024:(c_ + 1) * 1024]:
                          e.dma_start(out=d, in_=s),
                          reads=[ln1s_b], writes=[DB["ln1_scr"]])

        es_c.close()
        es_bk2.close()
        es_bk1.close()
        P.barrier()
        if stop_after == "C":
            P.retire()
            P.replay(nc)
            return nc

        es_d = contextlib.ExitStack()
        LN1F = [sb(es_d, f"ln1f{i}", [128, 8 * FTAU], BF16) for i in range(2)]
        XE = [sb(es_d, f"xe{i}", [128, 8 * FTAU], BF16) for i in range(2)]
        hbuf, hbuf_b = sb(es_d, "hbuf", [128, 8 * FTAU], BF16)
        w1s_, w1s_b = sb(es_d, "w1s", [128, 8 * C], BF16)
        w3s_, w3s_b = sb(es_d, "w3s", [128, 8 * C], BF16)
        w2s_, w2s_b = sb(es_d, "w2s", [128, 8 * C], BF16)
        oacc, oacc_b = sb(es_d, "oacc", [128, 8 * FTAU], F32)
        GROW = [sb(es_d, f"grow{i}", [1, FTAU], BF16) for i in range(2)]
        GSB = [sb(es_d, f"gsb{i}", [128, FTAU], BF16) for i in range(2)]
        onesb, onesb_b = sb(es_d, "onesb", [1, 128], BF16)
        P.ins("dve", lambda e: e.memset(onesb[:, :], 1.0), writes=[onesb_b])
        TSIL = [sb(es_d, f"tsil{i}", [128, 512], BF16) for i in range(2)]
        X1D = [sb(es_d, f"x1d{i}", [128, C], F32) for i in range(4)]
        SQD = [sb(es_d, f"sqd{i}", [128, C], F32) for i in range(4)]

        def load_w(e_idx, ft, wh, db, cache, cache_db, dst, dst_b):
            for c_ in range(8):
                P.dma("sp", lambda e, d=dst[:, c_ * C:(c_ + 1) * C],
                      s=cache[e_idx, c_ * 128:(c_ + 1) * 128, :]:
                      e.dma_start(out=d, in_=s),
                      reads=[cache_db], writes=[dst_b])

        def load_ln1f(ft):
            ln1f, ln1f_b = LN1F[ft % 2]
            t0 = ft * FTAU
            for c_ in range(8):
                P.dma("sp", lambda e, d=ln1f[:, c_ * FTAU:(c_ + 1) * FTAU],
                      s=ln1_scr[c_, :, t0:t0 + FTAU]:
                      e.dma_start(out=d, in_=s),
                      reads=[DB["ln1_scr"]], writes=[ln1f_b])

        pend_epi = [None]
        load_ln1f(0)
        for ft in range(NFT):
            t0 = ft * FTAU
            ln1f, ln1f_b = LN1F[ft % 2]
            if ft + 1 < NFT:
                load_ln1f(ft + 1)

            def emit_gxe(nei, ei, ln1f=ln1f, ln1f_b=ln1f_b, t0=t0):
                gv, gv_b = GROW[ei % 2]
                gsb, gsb_b = GSB[ei % 2]
                xe, xe_b = XE[ei % 2]
                P.dma("pool", lambda e, d=gv[:, :],
                      s=g_scr[ei:ei + 1, t0:t0 + FTAU]:
                      e.dma_start(out=d, in_=s),
                      reads=[DB["g_scr"]], writes=[gv_b])
                for half in range(2):
                    pgi = (0, 4)[half]
                    P.ins("pe", lambda e, o=ps[pgi][:, 0:512],
                          r=gv[0:1, half * 512:half * 512 + 512]:
                          e.matmul(o, onesb[0:1, :], r,
                                   start=True, stop=True),
                          reads=[onesb_b, gv_b], writes=[psb[pgi]])
                    P.ins("act", lambda e,
                          d=gsb[:, half * 512:half * 512 + 512],
                          s=ps[pgi][:, 0:512]:
                          e.activation(d, s, AF.Identity),
                          reads=[psb[pgi]], writes=[gsb_b])
                for c_ in range(8):
                    P.ins("dve", lambda e,
                          d=xe[:, c_ * FTAU:(c_ + 1) * FTAU],
                          a=ln1f[:, c_ * FTAU:(c_ + 1) * FTAU]:
                          e.tensor_mul(d, a, gsb[:, :]),
                          reads=[ln1f_b, gsb_b], writes=[xe_b])

            order = [8] + list(range(8))
            emit_gxe(1, order[1])
            for nei, ei in enumerate(order):
                if ei < 8:
                    xe_t, xe_tb = XE[ei % 2]
                else:
                    xe_t, xe_tb = ln1f, ln1f_b

                load_w(ei, ft, w1_h, DB["w1"], w1b_scr, DB["w1b"], w1s_, w1s_b)
                load_w(ei, ft, w3_h, DB["w3"], w3b_scr, DB["w3b"], w3s_, w3s_b)
                if nei == 0 and pend_epi[0] is not None:
                    pend_epi[0]()
                    pend_epi[0] = None
                for m in range(8):
                    for sub in range(2):
                        par = (m * 2 + sub) % 2
                        p_a1, p_a3 = (0, 1) if par == 0 else (4, 5)
                        tsil, tsil_b = TSIL[par]
                        for (pidx, ws, wsb) in ((p_a1, w1s_, w1s_b),
                                                (p_a3, w3s_, w3s_b)):
                            for c_ in range(8):
                                P.ins("pe", lambda e, o=ps[pidx][:, 0:512],
                                      a=ws[:, c_ * C + m * 128:
                                           c_ * C + (m + 1) * 128],
                                      b=xe_t[:, c_ * FTAU + sub * 512:
                                             c_ * FTAU + sub * 512 + 512],
                                      st=(c_ == 0), sp=(c_ == 7):
                                      e.matmul(o, a, b, start=st, stop=sp),
                                      reads=[wsb, xe_tb], writes=[psb[pidx]])
                        P.ins("act", lambda e, d=tsil[:, :], s=ps[p_a1][:, 0:512]:
                              e.activation(d, s, AF.Silu),
                              reads=[psb[p_a1]], writes=[tsil_b])
                        P.ins("dve", lambda e,
                              d=hbuf[:, m * FTAU + sub * 512:
                                     m * FTAU + sub * 512 + 512],
                              s3=ps[p_a3][:, 0:512], ts=tsil[:, :]:
                              e.tensor_mul(d, s3, ts),
                              reads=[psb[p_a3], tsil_b], writes=[hbuf_b])
                load_w(ei, ft, w2_h, DB["w2"], w2b_scr, DB["w2b"], w2s_, w2s_b)
                if nei + 1 < len(order) and order[nei + 1] != order[1]:
                    emit_gxe(nei + 1, order[nei + 1])
                for t2 in range(8):
                    for half in range(2):
                        pidx = 2 + half
                        for m in range(8):
                            P.ins("pe", lambda e, o=ps[pidx][:, 0:512],
                                  a=hbuf[:, m * FTAU + t2 * 128:
                                         m * FTAU + (t2 + 1) * 128],
                                  b=w2s_[:, m * C + half * 512:
                                         m * C + half * 512 + 512],
                                  st=(m == 0), sp=(m == 7):
                                  e.matmul(o, a, b, start=st, stop=sp),
                                  reads=[hbuf_b, w2s_b], writes=[psb[pidx]])
                        dsl = oacc[:, t2 * C + half * 512:t2 * C + half * 512 + 512]
                        if nei == 0:
                            P.ins("dve", lambda e, d=dsl, s=ps[pidx][:, 0:512]:
                                  e.tensor_copy(d, s),
                                  reads=[psb[pidx]], writes=[oacc_b])
                        else:
                            P.ins("dve", lambda e, d=dsl, s=ps[pidx][:, 0:512]:
                                  e.tensor_add(d, s, d),
                                  reads=[psb[pidx], oacc_b], writes=[oacc_b])

            def mk_epi(ft=ft):
              def epi():
                for half4 in range(2):
                    t2s = list(range(half4 * 4, half4 * 4 + 4))
                    epi = []
                    for t2 in t2s:
                        tg = ft * 8 + t2
                        xv, xv_b = X1D[t2 % 4]
                        P.dma("sp", lambda e, d=xv[:, :],
                              s=x1_scr[tg * 128:(tg + 1) * 128, :]:
                              e.dma_start(out=d, in_=s),
                              reads=[DB["x1_scr"]], writes=[xv_b])
                        st_t, st_b = STATS[t2 % 4]
                        epi.append((t2, tg, xv, xv_b,
                                    oacc[:, t2 * C:(t2 + 1) * C],
                                    SQD[t2 % 4][0], SQD[t2 % 4][1],
                                    st_t[:, 0:1], st_t[:, 1:2], st_t[:, 2:3],
                                    st_t[:, 3:4], st_t[:, 4:5], st_t[:, 5:6],
                                    st_t[:, 6:7], st_b))
                    for (t2, tg, xv, xv_b, osl, sq_t, sq_tb, s1, s2, mm_, m2,
                         vv, rr, bb, st_b) in epi:
                        P.ins("dve", lambda e, o=s1, i=osl:
                              e.reduce_sum(o, i, axis=AX.X),
                              reads=[oacc_b], writes=[st_b])
                    for (t2, tg, xv, xv_b, osl, sq_t, sq_tb, s1, s2, mm_, m2,
                         vv, rr, bb, st_b) in epi:
                        P.ins("act", lambda e, o=sq_t[:, :], i=osl:
                              e.activation(o, i, AF.Square),
                              reads=[oacc_b], writes=[sq_tb])
                    for (t2, tg, xv, xv_b, osl, sq_t, sq_tb, s1, s2, mm_, m2,
                         vv, rr, bb, st_b) in epi:
                        P.ins("dve", lambda e, o=s2, i=sq_t[:, :]:
                              e.reduce_sum(o, i, axis=AX.X),
                              reads=[sq_tb], writes=[st_b])
                    for (t2, tg, xv, xv_b, osl, sq_t, sq_tb, s1, s2, mm_, m2,
                         vv, rr, bb, st_b) in epi:
                        P.ins("dve", lambda e, o=mm_, i=s1:
                              e.tensor_scalar_mul(o, i, 1.0 / C),
                              reads=[st_b], writes=[st_b])
                    for (t2, tg, xv, xv_b, osl, sq_t, sq_tb, s1, s2, mm_, m2,
                         vv, rr, bb, st_b) in epi:
                        P.ins("dve", lambda e, o=m2, i=s1: e.scalar_tensor_tensor(
                            o, i, 1.0 / (C * C), i, ALU.mult, ALU.mult),
                            reads=[st_b], writes=[st_b])
                    for (t2, tg, xv, xv_b, osl, sq_t, sq_tb, s1, s2, mm_, m2,
                         vv, rr, bb, st_b) in epi:
                        P.ins("dve", lambda e, o=vv, i=s2, m=m2:
                              e.scalar_tensor_tensor(o, i, 1.0 / C, m,
                                                     ALU.mult, ALU.subtract),
                              reads=[st_b], writes=[st_b])
                    for (t2, tg, xv, xv_b, osl, sq_t, sq_tb, s1, s2, mm_, m2,
                         vv, rr, bb, st_b) in epi:
                        P.ins("act", lambda e, o=vv: e.activation(
                            o, o, AF.Sqrt, bias=epsb[:, 0:1]),
                            reads=[st_b, epsb_b], writes=[st_b])
                    for (t2, tg, xv, xv_b, osl, sq_t, sq_tb, s1, s2, mm_, m2,
                         vv, rr, bb, st_b) in epi:
                        P.ins("dve", lambda e, o=rr, i=vv: e.reciprocal(o, i),
                              reads=[st_b], writes=[st_b])
                    for (t2, tg, xv, xv_b, osl, sq_t, sq_tb, s1, s2, mm_, m2,
                         vv, rr, bb, st_b) in epi:
                        P.ins("dve", lambda e, o=bb, m=mm_, r=rr:
                              e.scalar_tensor_tensor(o, m, -1.0, r,
                                                     ALU.mult, ALU.mult),
                              reads=[st_b], writes=[st_b])
                    for (t2, tg, xv, xv_b, osl, sq_t, sq_tb, s1, s2, mm_, m2,
                         vv, rr, bb, st_b) in epi:
                        P.ins("dve", lambda e, d=xv[:, :], b=bb:
                              e.tensor_scalar_add(d, d, b),
                              reads=[xv_b, st_b], writes=[xv_b])
                    for (t2, tg, xv, xv_b, osl, sq_t, sq_tb, s1, s2, mm_, m2,
                         vv, rr, bb, st_b) in epi:
                        P.ins("dve", lambda e, d=xv[:, :], i=osl, r=rr:
                              e.scalar_tensor_tensor(d, i, r, d, ALU.mult, ALU.add),
                              reads=[oacc_b, st_b, xv_b], writes=[xv_b])
                    for (t2, tg, xv, xv_b, osl, sq_t, sq_tb, s1, s2, mm_, m2,
                         vv, rr, bb, st_b) in epi:
                        P.dma("sp", lambda e, s=xv[:, :],
                              d=out_h[tg * 128:(tg + 1) * 128, :]:
                              e.dma_start(out=d, in_=s),
                              reads=[xv_b], writes=[DB["out"]])

              return epi
            pend_epi[0] = mk_epi()

        if pend_epi[0] is not None:
            pend_epi[0]()
            pend_epi[0] = None
        es_d.close()
        P.retire()
        P.replay(nc)
    return nc


_CACHE = {}


def _get_nc():
    if "nc" not in _CACHE:
        _CACHE["nc"] = build_program()
    return _CACHE["nc"]


def make_inputs(L0_attn_x, L0_attn_W_DQ, L0_attn_W_UQ, L0_attn_W_DKV,
                L0_attn_W_UK, L0_attn_W_UV, L0_attn_W_QR, L0_attn_W_KR,
                L0_attn_W_O, L0_ffn_gate_w, L0_ffn_w1, L0_ffn_w2, L0_ffn_w3,
                L0_ffn_w1s, L0_ffn_w2s, L0_ffn_w3s):
    f = np.float32
    c = np.ascontiguousarray
    sc4, masks, ident, emat = make_consts()
    shared = dict(
        wdqt=c(np.asarray(L0_attn_W_DQ, f).T),
        wuq=c(np.asarray(L0_attn_W_UQ, f)),
        wqr=c(np.asarray(L0_attn_W_QR, f)),
        wdkv=c(np.asarray(L0_attn_W_DKV, f)),
        wuk=c(np.asarray(L0_attn_W_UK, f)),
        wuv=c(np.asarray(L0_attn_W_UV, f)),
        wkr=c(np.asarray(L0_attn_W_KR, f)),
        wo=c(np.asarray(L0_attn_W_O, f)),
        gwt=c(np.asarray(L0_ffn_gate_w, f).T),
        w1=c(np.concatenate([np.asarray(L0_ffn_w1, f),
                             np.asarray(L0_ffn_w1s, f)], 0)),
        w2=c(np.concatenate([np.asarray(L0_ffn_w2, f),
                             np.asarray(L0_ffn_w2s, f)], 0)),
        w3=c(np.concatenate([np.asarray(L0_ffn_w3, f),
                             np.asarray(L0_ffn_w3s, f)], 0)),
        sc4=sc4, maskc=masks, identc=ident, ematc=emat,
    )
    xs = np.asarray(L0_attn_x, f)
    return [dict(shared, x=c(xs[b])) for b in range(8)]


def kernel(**inputs):
    in_maps = make_inputs(**inputs)
    nc = _get_nc()
    res = run_bass_kernel_spmd(nc, in_maps, core_ids=list(range(8)))
    return np.stack([res.results[b]["out"] for b in range(8)], 0)


# version 31
# speedup vs baseline: 1.0283x; 1.0283x over previous
import bisect
import contextlib
import numpy as np

import concourse.bass as bass
from concourse.bass import broadcast_tensor_aps
import concourse.mybir as mybir
from concourse.bass_utils import run_bass_kernel_spmd

F32 = mybir.dt.float32
BF16 = mybir.dt.bfloat16
AF = mybir.ActivationFunctionType
ALU = mybir.AluOpType
AX = mybir.AxisListType

T = 4096
C = 1024
H = 16
NT = T // 128
TAU = 512
NTAU = T // TAU
NQB = 8
FTAU = 1024
NFT = T // FTAU
NE = 9
INV_SQ = float(1.0 / np.sqrt(np.float32(96.0)))
SQ96 = float(np.sqrt(np.float32(96.0)))
EPS = 1e-5
NEG = -30000.0

QUEUES = ("pe", "act", "dve", "pool", "sp")


class Buf:
    __slots__ = ("writers", "readers", "chan")

    def __init__(self):
        self.writers = {}
        self.readers = {}
        self.chan = {}


class Prog:

    def __init__(self):
        self.q = {e: [] for e in QUEUES}
        self.cnt = {p: 0 for p in QUEUES}
        self.marks = {p: set() for p in QUEUES}
        self.know = {e: {} for e in QUEUES}
        self.clock = {p: {} for p in QUEUES}
        self.nchan = 0

    def _chan(self, buf, queue):
        ch = buf.chan.get(queue)
        if ch is None:
            ch = buf.chan[queue] = f"D{self.nchan}"
            self.nchan += 1
            self.cnt[ch] = 0
            self.marks[ch] = set()
            self.clock[ch] = {}
        return ch

    def _emit(self, eng, fn, reads, writes, producer):
        need = {}
        for b in reads:
            for p, v in b.writers.items():
                if need.get(p, 0) < v:
                    need[p] = v
        for b in writes:
            for d in (b.writers, b.readers):
                for p, v in d.items():
                    if need.get(p, 0) < v:
                        need[p] = v
        k = self.know[eng]
        for p in sorted(need):
            v = need[p]
            if p == "pe" and producer == "pe":
                continue
            if k.get(p, 0) >= v:
                continue
            self.q[eng].append(("w", p, v))
            self.marks[p].add(v)
            c = self.clock[p].get(v)
            if c:
                for pp, vv in c.items():
                    if k.get(pp, 0) < vv:
                        k[pp] = vv
            if k.get(p, 0) < v:
                k[p] = v
        vidx = self.cnt[producer] + 1
        self.cnt[producer] = vidx
        if producer.startswith("D"):
            self.marks[producer].add(vidx)
        c = dict(k)
        c[producer] = vidx
        self.clock[producer][vidx] = c
        self.q[eng].append(("i", fn, producer, vidx))
        for b in reads:
            if b.readers.get(producer, 0) < vidx:
                b.readers[producer] = vidx
        for b in writes:
            if b.writers.get(producer, 0) < vidx:
                b.writers[producer] = vidx

    def ins(self, eng, fn, reads=(), writes=()):
        self._emit(eng, fn, reads, writes, eng)

    def dma(self, queue, fn, reads=(), writes=()):
        assert writes, "dma needs a target buf (its channel)"
        self._emit(queue, fn, reads, writes, self._chan(writes[0], queue))

    def barrier(self):
        snap = {p: v for p, v in self.cnt.items() if v > 0}
        for eng in QUEUES:
            k = self.know[eng]
            for p, v in snap.items():
                if k.get(p, 0) < v:
                    self.q[eng].append(("w", p, v))
                    self.marks[p].add(v)
                    k[p] = v

    def retire(self):
        k = self.know["sp"]
        for p, v in self.cnt.items():
            if p.startswith("D") and v > 0 and k.get(p, 0) < v:
                self.q["sp"].append(("w", p, v))
                self.marks[p].add(v)
                k[p] = v

    def replay(self, nc):
        smarks = {p: sorted(m) for p, m in self.marks.items()}
        step = {p: (16 if p.startswith("D") else 1) for p in self.marks}
        with contextlib.ExitStack() as es:
            sems = {p: es.enter_context(nc.semaphore("sem_" + p))
                    for p, m in self.marks.items() if m}

            def runner(entries):
                def run(handle):
                    for e in entries:
                        if e[0] == "w":
                            _, p, v = e
                            handle.wait_ge(
                                sems[p],
                                bisect.bisect_right(smarks[p], v) * step[p])
                        else:
                            _, fn, p, vidx = e
                            inst = fn(handle)
                            if vidx in self.marks[p]:
                                inst.then_inc(sems[p], step[p])
                return run

            with nc.Block() as block:
                block.tensor(runner(self.q["pe"]))
                block.scalar(runner(self.q["act"]))
                block.vector(runner(self.q["dve"]))
                block.gpsimd(runner(self.q["pool"]))
                block.sync(runner(self.q["sp"]))


def rope_scale_table():
    t = np.arange(T, dtype=np.float64)
    inv = 1.0 / 10000.0 ** (np.arange(0, 32, 2, dtype=np.float64) / 32)
    freqs = t[:, None] * inv[None, :]
    emb = np.concatenate([freqs, freqs], -1)
    return (np.cos(emb) + np.sin(emb)).astype(np.float32)


def make_consts():
    scT = np.ascontiguousarray(rope_scale_table().T)
    sc4 = np.tile(scT, (4, 1)) * np.float32(INV_SQ)
    masks = np.zeros((4, 128, 512), np.float32)
    p = np.arange(128)[:, None]
    cc = np.arange(512)[None, :]
    for m in range(4):
        masks[m] = np.where(cc >= 128 * m + p, 0.0, NEG)
    ident = np.eye(128, dtype=np.float32)
    emat = np.zeros((13, 128, 16), np.float32)
    for m in range(8):
        emat[m, 0:64, 2 * m] = 1.0
        emat[m, 64:128, 2 * m + 1] = 1.0
    for m in range(4):
        for g in range(4):
            emat[8 + m, 32 * g:32 * g + 32, 4 * m + g] = 1.0
    emat[12, 0:32, :] = 1.0
    return sc4, masks, ident, emat


def build_program(stop_after=None):
    nc = bass.Bass("TRN2", target_bir_lowering=False, debug=False)
    P = Prog()

    def din(name, shape):
        return nc.dram_tensor(name, shape, F32, kind="ExternalInput").ap()

    x_h = din("x", [T, C])
    wdqt_h = din("wdqt", [4096, C])
    wuq_h = din("wuq", [4096, C])
    wqr_h = din("wqr", [4096, 512])
    wdkv_h = din("wdkv", [C, 256])
    wuk_h = din("wuk", [256, C])
    wuv_h = din("wuv", [256, C])
    wkr_h = din("wkr", [C, 32])
    wo_h = din("wo", [C, C])
    gwt_h = din("gwt", [C, 8])
    w1_h = din("w1", [NE, C, C])
    w2_h = din("w2", [NE, C, C])
    w3_h = din("w3", [NE, C, C])
    sc4_h = din("sc4", [128, T])
    mask_h = din("maskc", [4, 128, 512])
    ident_h = din("identc", [128, 128])
    emat_h = din("ematc", [13, 128, 16])

    out_h = nc.dram_tensor("out", [T, C], F32, kind="ExternalOutput").ap()

    q_scr = nc.dram_tensor("q_scr", [H, 96, T], BF16, kind="Internal").ap()
    k_scr = nc.dram_tensor("k_scr", [H, 64, T], BF16, kind="Internal").ap()
    v_scr = nc.dram_tensor("v_scr", [H, 64, T], BF16, kind="Internal").ap()
    x1_scr = nc.dram_tensor("x1_scr", [T, C], F32, kind="Internal").ap()
    w1b_scr = nc.dram_tensor("w1b_scr", [NE, C, C], BF16, kind="Internal").ap()
    w2b_scr = nc.dram_tensor("w2b_scr", [NE, C, C], BF16, kind="Internal").ap()
    w3b_scr = nc.dram_tensor("w3b_scr", [NE, C, C], BF16, kind="Internal").ap()
    ln1_scr = nc.dram_tensor("ln1_scr", [8, 128, T], BF16, kind="Internal").ap()
    g_scr = nc.dram_tensor("g_scr", [8, T], F32, kind="Internal").ap()

    DB = {}
    for n in ("x", "wdqt", "wuq", "wqr", "wdkv", "wuk", "wuv", "wkr", "wo",
              "gwt", "w1", "w2", "w3", "sc4", "maskc", "identc", "ematc",
              "out", "q_scr", "k_scr", "v_scr", "x1_scr", "ln1_scr", "g_scr",
              "w1b", "w2b", "w3b"):
        DB[n] = Buf()

    def sb(es, name, shape, dtype):
        t = es.enter_context(nc.sbuf_tensor(name, shape, dtype))
        return t, Buf()

    es_top = contextlib.ExitStack()
    with es_top:
        identf, identf_b = sb(es_top, "identf", [128, 128], F32)
        identb, identb_b = sb(es_top, "identb", [128, 128], BF16)
        onesf, onesf_b = sb(es_top, "onesf", [1, 128], F32)

        P.dma("sp", lambda e: e.dma_start(out=identf[:, :], in_=ident_h[:, :]),
              reads=[DB["identc"]], writes=[identf_b])
        P.dma("pool", lambda e: e.dma_start(out=identb[:, :], in_=ident_h[:, :]),
              reads=[DB["identc"]], writes=[identb_b])
        epsb, epsb_b = sb(es_top, "epsb", [128, 1], F32)
        P.ins("dve", lambda e: e.memset(onesf[:, :], 1.0), writes=[onesf_b])
        P.ins("dve", lambda e: e.memset(epsb[:, :], EPS), writes=[epsb_b])

        STATS = [sb(es_top, f"st{i}", [128, 8], F32) for i in range(16)]

        def layer_norm_group(items):
            views = []
            for (x_ap, x_buf, out_ap, out_buf, sq_t, sq_tb, slot) in items:
                st_t, st_b = slot
                views.append((x_ap, x_buf, out_ap, out_buf, sq_t, sq_tb,
                              st_t[:, 0:1], st_t[:, 1:2], st_t[:, 2:3],
                              st_t[:, 3:4], st_t[:, 4:5], st_t[:, 5:6], st_b))
            for (x_ap, x_buf, _, _, _, _, s1, *_r, st_b) in views:
                P.ins("dve", lambda e, o=s1, i=x_ap: e.reduce_sum(o, i, axis=AX.X),
                      reads=[x_buf], writes=[st_b])
            for (x_ap, x_buf, _, _, sq_t, sq_tb, *_r) in views:
                P.ins("act", lambda e, o=sq_t[:, :], i=x_ap:
                      e.activation(o, i, AF.Square),
                      reads=[x_buf], writes=[sq_tb])
            for (_, _, _, _, sq_t, sq_tb, s1, s2, *_r, st_b) in views:
                P.ins("dve", lambda e, o=s2, i=sq_t[:, :]:
                      e.reduce_sum(o, i, axis=AX.X),
                      reads=[sq_tb], writes=[st_b])
            for (_, _, _, _, _, _, s1, s2, mm_, m2, vv, rr, st_b) in views:
                P.ins("dve", lambda e, o=mm_, i=s1:
                      e.tensor_scalar_mul(o, i, 1.0 / C),
                      reads=[st_b], writes=[st_b])
            for (_, _, _, _, _, _, s1, s2, mm_, m2, vv, rr, st_b) in views:
                P.ins("dve", lambda e, o=m2, i=s1: e.scalar_tensor_tensor(
                    o, i, 1.0 / (C * C), i, ALU.mult, ALU.mult),
                    reads=[st_b], writes=[st_b])
            for (_, _, _, _, _, _, s1, s2, mm_, m2, vv, rr, st_b) in views:
                P.ins("dve", lambda e, o=vv, i=s2, m=m2: e.scalar_tensor_tensor(
                    o, i, 1.0 / C, m, ALU.mult, ALU.subtract),
                    reads=[st_b], writes=[st_b])
            for (_, _, _, _, _, _, s1, s2, mm_, m2, vv, rr, st_b) in views:
                P.ins("act", lambda e, o=vv: e.activation(o, o, AF.Sqrt,
                                                          bias=epsb[:, 0:1]),
                      reads=[st_b, epsb_b], writes=[st_b])
            for (_, _, _, _, _, _, s1, s2, mm_, m2, vv, rr, st_b) in views:
                P.ins("dve", lambda e, o=rr, i=vv: e.reciprocal(o, i),
                      reads=[st_b], writes=[st_b])
            for (x_ap, x_buf, out_ap, out_buf, _, _, s1, s2, mm_, m2, vv, rr,
                 st_b) in views:
                P.ins("dve", lambda e, o=out_ap, i=x_ap, m=mm_, r=rr:
                      e.tensor_scalar(o, i, m, r, ALU.subtract, ALU.mult),
                      reads=[x_buf, st_b], writes=[out_buf])

        def layer_norm(x_ap, x_buf, out_ap, out_buf, sq_t, sq_tb, slot):
            layer_norm_group([(x_ap, x_buf, out_ap, out_buf, sq_t, sq_tb, slot)])

        def make_gstat(es, name):
            t, _ = sb(es, name, [128, 32], F32)
            return (t, [Buf() for _ in range(4)], [Buf() for _ in range(4)],
                    Buf(), Buf(), Buf(), Buf())

        def layer_norm_group_v2(items, gs):
            gst, s1b, s2b, mqb, m2qb, vvqb, rrqb = gs
            n = len(items)
            assert n <= 4
            s1q, s2q = gst[:, 0:4], gst[:, 4:8]
            mq, m2q = gst[:, 8:12], gst[:, 12:16]
            vvq, rrq = gst[:, 16:20], gst[:, 20:24]
            for i, (x_ap, x_buf, _, _, _, _) in enumerate(items):
                P.ins("dve", lambda e, o=gst[:, i:i + 1], x=x_ap:
                      e.reduce_sum(o, x, axis=AX.X),
                      reads=[x_buf], writes=[s1b[i]])
            for i, (x_ap, x_buf, _, _, sq_t, sq_tb) in enumerate(items):
                P.ins("act", lambda e, o=sq_t[:, :], x=x_ap,
                      a=gst[:, 4 + i:5 + i]:
                      e.activation(o, x, AF.Square, accum_out=a),
                      reads=[x_buf], writes=[sq_tb, s2b[i]])
            P.ins("dve", lambda e: e.tensor_scalar_mul(mq, s1q, 1.0 / C),
                  reads=s1b, writes=[mqb])
            P.ins("dve", lambda e: e.scalar_tensor_tensor(
                m2q, s1q, 1.0 / (C * C), s1q, ALU.mult, ALU.mult),
                reads=s1b, writes=[m2qb])
            P.ins("dve", lambda e: e.scalar_tensor_tensor(
                vvq, s2q, 1.0 / C, m2q, ALU.mult, ALU.subtract),
                reads=s2b + [m2qb], writes=[vvqb])
            P.ins("act", lambda e: e.activation(vvq, vvq, AF.Sqrt,
                                                bias=epsb[:, 0:1]),
                  reads=[vvqb, epsb_b], writes=[vvqb])
            P.ins("dve", lambda e: e.reciprocal(rrq, vvq),
                  reads=[vvqb], writes=[rrqb])
            for i, (x_ap, x_buf, out_ap, out_buf, _, _) in enumerate(items):
                P.ins("dve", lambda e, o=out_ap, x=x_ap, m=gst[:, 8 + i:9 + i],
                      r=gst[:, 20 + i:21 + i]:
                      e.tensor_scalar(o, x, m, r, ALU.subtract, ALU.mult),
                      reads=[x_buf, mqb, rrqb], writes=[out_buf])

        es_bk1 = contextlib.ExitStack()
        bound, bound_b = sb(es_bk1, "bound", [16, T], F32)
        kr_s, kr_b = sb(es_bk1, "kr_s", [32, T], BF16)
        neg1b, neg1b_b = sb(es_bk1, "neg1b", [1, T], BF16)
        P.ins("dve", lambda e: e.memset(neg1b[:, :], -1.0), writes=[neg1b_b])

        es_weff = contextlib.ExitStack()
        weff, weff_b = sb(es_weff, "weff", [128, 8 * C], BF16)
        wqre, wqre_b = sb(es_weff, "wqre", [128, 8 * 512], BF16)
        es_a = contextlib.ExitStack()
        kn2, kn2_b = sb(es_a, "kn2", [16, T], BF16)
        qn2, qn2_b = sb(es_a, "qn2", [16, T], BF16)
        sc4s, sc4s_b = sb(es_a, "sc4s", [128, T], BF16)
        ematb, ematb_b = sb(es_a, "ematb", [128, 13 * 16], BF16)
        wdkv_s, wdkv_sb = sb(es_a, "wdkv_s", [128, 8 * 256], BF16)
        wuk_s, wuk_sb = sb(es_a, "wuk_s", [128, 2 * C], BF16)
        wuv_s, wuv_sb = sb(es_a, "wuv_s", [128, 2 * C], BF16)
        wkr_s, wkr_sb = sb(es_a, "wkr_s", [128, 8 * 32], BF16)

        STG = [sb(es_a, f"stg{i}", [128, 512], BF16) for i in range(4)]
        ckv, ckv_b = sb(es_a, "ckv", [128, 2 * 512], BF16)

        es_w = contextlib.ExitStack()
        wdqt_s, wdqt_sb = sb(es_w, "wdqt_s", [128, 32 * C], BF16)
        WRH = [sb(es_w, f"wrhs{i}", [128, 512], BF16) for i in range(4)]
        ps, psb = [], []
        for i in range(8):
            ps.append(es_w.enter_context(
                nc.psum_tensor(f"pw{i}", [128, 512], F32)))
            psb.append(Buf())

        for kb in range(32):
            P.dma("pool", lambda e, d=wdqt_s[:, kb * C:(kb + 1) * C],
                  s=wdqt_h[kb * 128:(kb + 1) * 128, :]:
                  e.dma_start(out=d, in_=s),
                  reads=[DB["wdqt"]], writes=[wdqt_sb])

        passes = [(wuq_h, DB["wuq"], 0, "weff"),
                  (wuq_h, DB["wuq"], 512, "weff"),
                  (wqr_h, DB["wqr"], 0, "wqre")]
        for rhs_h, rhs_db, off, dname in passes:
            for kb in range(32):
                wr, wr_b = WRH[kb % 4]
                P.dma("pool", lambda e, d=wr[:, :],
                      s=rhs_h[kb * 128:(kb + 1) * 128, off:off + 512]:
                      e.dma_start(out=d, in_=s),
                      reads=[rhs_db], writes=[wr_b])
                for mi in range(8):
                    P.ins("pe", lambda e, o=ps[mi][:, 0:512],
                          a=wdqt_s[:, kb * C + mi * 128:kb * C + (mi + 1) * 128],
                          b=wr[:, :],
                          st=(kb == 0), sp=(kb == 31):
                          e.matmul(o, a, b, start=st, stop=sp),
                          reads=[wdqt_sb, wr_b], writes=[psb[mi]])
            for mi in range(8):
                if dname == "weff":
                    d = weff[:, mi * C + off:mi * C + off + 512]
                    dbf = weff_b
                else:
                    d = wqre[:, mi * 512:(mi + 1) * 512]
                    dbf = wqre_b
                P.ins("act", lambda e, dd=d, s=ps[mi][:, 0:512]:
                      e.activation(dd, s, AF.Identity),
                      reads=[psb[mi]], writes=[dbf])
        es_w.close()
        P.barrier()

        es_a2 = contextlib.ExitStack()
        SQB = [sb(es_a2, f"sqb{i}", [128, 512], BF16) for i in range(12)]
        SQK = [sb(es_a2, f"sqk{i}", [128, 512], BF16) for i in range(9)]
        LNX = [sb(es_a2, f"lnx{i}", [128, 8 * TAU], BF16) for i in range(2)]
        XT = [sb(es_a2, f"xta{i}", [128, C], F32) for i in range(4)]
        SQA = [sb(es_a2, f"sqa{i}", [128, C], F32) for i in range(4)]
        LNT = [sb(es_a2, f"lnt{i}", [128, C], BF16) for i in range(4)]

        ps, psb = [], []
        for i in range(6):
            ps.append(es_top.enter_context(
                nc.psum_tensor(f"p{i}", [128, 512], F32)))
            psb.append(Buf())
        for i in (6, 7):
            ps.append(es_top.enter_context(
                nc.psum_tensor(f"p{i}", [128, 1024], BF16)))
            psb.append(Buf())

        P.dma("pool", lambda e: e.dma_start(out=sc4s[:, :], in_=sc4_h[:, :]),
              reads=[DB["sc4"]], writes=[sc4s_b])
        P.dma("pool", lambda e: e.dma_start(
            out=ematb[:, :].rearrange("p (k j) -> p k j", j=16),
            in_=emat_h.rearrange("k p j -> p k j")),
            reads=[DB["ematc"]], writes=[ematb_b])
        P.dma("pool", lambda e: e.dma_start(
            out=wdkv_s[:, :].rearrange("p (c j) -> p c j", j=256),
            in_=wdkv_h.rearrange("(c p) j -> p c j", p=128)),
            reads=[DB["wdkv"]], writes=[wdkv_sb])
        P.dma("pool", lambda e: e.dma_start(
            out=wuk_s[:, :].rearrange("p (c j) -> p c j", j=C),
            in_=wuk_h.rearrange("(c p) j -> p c j", p=128)),
            reads=[DB["wuk"]], writes=[wuk_sb])
        P.dma("pool", lambda e: e.dma_start(
            out=wuv_s[:, :].rearrange("p (c j) -> p c j", j=C),
            in_=wuv_h.rearrange("(c p) j -> p c j", p=128)),
            reads=[DB["wuv"]], writes=[wuv_sb])
        P.dma("pool", lambda e: e.dma_start(
            out=wkr_s[:, :].rearrange("p (c j) -> p c j", j=32),
            in_=wkr_h.rearrange("(c p) j -> p c j", p=128)),
            reads=[DB["wkr"]], writes=[wkr_sb])

        def emit_ln1_chain(tau):
            items = []
            for tt in range(4):
                tg = tau * 4 + tt
                xv, xv_b = XT[tt]
                sqv_t, sqv_tb = SQA[tt]
                lnt, lnt_b = LNT[tt]
                P.dma("sp", lambda e, d=xv[:, :],
                      s=x_h[tg * 128:(tg + 1) * 128, :]:
                      e.dma_start(out=d, in_=s),
                      reads=[DB["x"]], writes=[xv_b])
                items.append((xv[:, :], xv_b, lnt[:, :], lnt_b, sqv_t, sqv_tb,
                              STATS[tt]))
            layer_norm_group(items)

        def emit_ln1_transposes(tau):
            lnxv, lnx_b = LNX[tau % 2]
            for tt in range(4):
                lnt, lnt_b = LNT[tt]
                for g in range(2):
                    pt, pt_b = ps[6 + g], psb[6 + g]
                    for c4 in range(4):
                        c = g * 4 + c4
                        P.ins("pe", lambda e, o=pt[:, c4 * 128:(c4 + 1) * 128],
                              i=lnt[:, c * 128:(c + 1) * 128]:
                              e.transpose(o, i, identb[:, :]),
                              reads=[lnt_b, identb_b], writes=[pt_b])
                    src = pt[:, 0:512].rearrange("p (c j) -> p c j", j=128)
                    dst = lnxv[:, :].rearrange("p (c j) -> p c j", j=TAU)[
                        :, g * 4:(g + 1) * 4, tt * 128:(tt + 1) * 128]
                    P.ins("dve", lambda e, d=dst, s=src: e.tensor_copy(d, s),
                          reads=[pt_b], writes=[lnx_b])

        emit_ln1_chain(0)
        emit_ln1_transposes(0)
        for tau in range(NTAU):
            lnxv, lnx_b = LNX[tau % 2]
            t0 = tau * TAU
            if tau + 1 < NTAU:
                emit_ln1_chain(tau + 1)

            qnorm = []
            knorm = []
            for m in range(8):
                pm, pm_b = ps[m % 2], psb[m % 2]
                for c_ in range(8):
                    P.ins("pe", lambda e, o=pm[:, 0:512],
                          a=weff[:, c_ * C + m * 128:c_ * C + (m + 1) * 128],
                          b=lnxv[:, c_ * TAU:(c_ + 1) * TAU],
                          st=(c_ == 0), sp=(c_ == 7):
                          e.matmul(o, a, b, start=st, stop=sp),
                          reads=[weff_b, lnx_b], writes=[pm_b])
                stv, stv_b = STG[m % 4]
                P.ins("act", lambda e, d=stv[:, :], s=pm[:, 0:512]:
                      e.activation(d, s, AF.Identity, scale=INV_SQ),
                      reads=[pm_b], writes=[stv_b])
                dst = q_scr[2 * m:2 * m + 2, 0:64, t0:t0 + TAU]
                P.dma("sp", lambda e, d=dst, s=stv[:, :]: e.dma_start(out=d, in_=s),
                      reads=[stv_b], writes=[DB["q_scr"]])
                sqv, sqv_b = SQB[m]
                P.ins("act", lambda e, d=sqv[:, :], s=stv[:, :]:
                      e.activation(d, s, AF.Square),
                      reads=[stv_b], writes=[sqv_b])
                qnorm.append((ematb[:, m * 16:(m + 1) * 16], sqv[:, :], sqv_b,
                              128))

            for m in range(4):
                pm, pm_b = ps[m % 2], psb[m % 2]
                for c_ in range(8):
                    P.ins("pe", lambda e, o=pm[:, 0:512],
                          a=wqre[:, c_ * 512 + m * 128:c_ * 512 + (m + 1) * 128],
                          b=lnxv[:, c_ * TAU:(c_ + 1) * TAU],
                          st=(c_ == 0), sp=(c_ == 7):
                          e.matmul(o, a, b, start=st, stop=sp),
                          reads=[wqre_b, lnx_b], writes=[pm_b])
                stv, stv_b = STG[m % 4]
                P.ins("dve", lambda e, d=stv[:, :], s=pm[:, 0:512],
                      sc=sc4s[:, t0:t0 + TAU]: e.tensor_mul(d, s, sc),
                      reads=[pm_b, sc4s_b], writes=[stv_b])
                dst = q_scr[4 * m:4 * m + 4, 64:96, t0:t0 + TAU]
                P.dma("sp", lambda e, d=dst, s=stv[:, :]: e.dma_start(out=d, in_=s),
                      reads=[stv_b], writes=[DB["q_scr"]])
                sqv, sqv_b = SQB[8 + m]
                P.ins("act", lambda e, d=sqv[:, :], s=stv[:, :]:
                      e.activation(d, s, AF.Square),
                      reads=[stv_b], writes=[sqv_b])
                qnorm.append((ematb[:, (8 + m) * 16:(9 + m) * 16], sqv[:, :],
                              sqv_b, 128))

            for mc in range(2):
                for c_ in range(8):
                    P.ins("pe", lambda e, o=ps[4][:, 0:512],
                          a=wdkv_s[:, c_ * 256 + mc * 128:c_ * 256 + (mc + 1) * 128],
                          b=lnxv[:, c_ * TAU:(c_ + 1) * TAU],
                          st=(c_ == 0), sp=(c_ == 7):
                          e.matmul(o, a, b, start=st, stop=sp),
                          reads=[wdkv_sb, lnx_b], writes=[psb[4]])
                P.ins("act", lambda e, d=ckv[:, mc * 512:(mc + 1) * 512],
                      s=ps[4][:, 0:512]: e.activation(d, s, AF.Identity),
                      reads=[psb[4]], writes=[ckv_b])

            for m in range(8):
                pm, pm_b = ps[m % 2], psb[m % 2]
                for c2 in range(2):
                    P.ins("pe", lambda e, o=pm[:, 0:512],
                          a=wuk_s[:, c2 * C + m * 128:c2 * C + (m + 1) * 128],
                          b=ckv[:, c2 * 512:(c2 + 1) * 512],
                          st=(c2 == 0), sp=(c2 == 1):
                          e.matmul(o, a, b, start=st, stop=sp),
                          reads=[wuk_sb, ckv_b], writes=[pm_b])
                stv, stv_b = STG[m % 4]
                P.ins("act", lambda e, d=stv[:, :], s=pm[:, 0:512]:
                      e.activation(d, s, AF.Identity),
                      reads=[pm_b], writes=[stv_b])
                dst = k_scr[2 * m:2 * m + 2, :, t0:t0 + TAU]
                P.dma("sp", lambda e, d=dst, s=stv[:, :]: e.dma_start(out=d, in_=s),
                      reads=[stv_b], writes=[DB["k_scr"]])
                sqv, sqv_b = SQK[m]
                P.ins("act", lambda e, d=sqv[:, :], s=stv[:, :]:
                      e.activation(d, s, AF.Square),
                      reads=[stv_b], writes=[sqv_b])
                knorm.append((ematb[:, m * 16:(m + 1) * 16], sqv[:, :],
                              sqv_b, 128))

            for c_ in range(8):
                P.ins("pe", lambda e, o=ps[4][0:32, 0:512],
                      a=wkr_s[:, c_ * 32:(c_ + 1) * 32],
                      b=lnxv[:, c_ * TAU:(c_ + 1) * TAU],
                      st=(c_ == 0), sp=(c_ == 7):
                      e.matmul(o, a, b, start=st, stop=sp),
                      reads=[wkr_sb, lnx_b], writes=[psb[4]])
            P.ins("dve", lambda e, d=kr_s[:, t0:t0 + TAU],
                  s=ps[4][0:32, 0:512], sc=sc4s[0:32, t0:t0 + TAU]:
                  e.scalar_tensor_tensor(d, s, SQ96, sc, ALU.mult, ALU.mult),
                  reads=[psb[4], sc4s_b], writes=[kr_b])
            sqv, sqv_b = SQK[8]
            P.ins("act", lambda e, d=sqv[0:32, :],
                  s=kr_s[0:32, t0:t0 + TAU]: e.activation(d, s, AF.Square),
                  reads=[kr_b], writes=[sqv_b])
            knorm.append((ematb[0:32, 12 * 16:13 * 16], sqv[0:32, :],
                          sqv_b, 32))

            for m in range(8):
                for c2 in range(2):
                    P.ins("pe", lambda e, o=ps[5][:, 0:512],
                          a=wuv_s[:, c2 * C + m * 128:c2 * C + (m + 1) * 128],
                          b=ckv[:, c2 * 512:(c2 + 1) * 512],
                          st=(c2 == 0), sp=(c2 == 1):
                          e.matmul(o, a, b, start=st, stop=sp),
                          reads=[wuv_sb, ckv_b], writes=[psb[5]])
                stv, stv_b = STG[m % 4]
                P.ins("act", lambda e, d=stv[:, :], s=ps[5][:, 0:512]:
                      e.activation(d, s, AF.Identity),
                      reads=[psb[5]], writes=[stv_b])
                dst = v_scr[2 * m:2 * m + 2, :, t0:t0 + TAU]
                P.dma("sp", lambda e, d=dst, s=stv[:, :]: e.dma_start(out=d, in_=s),
                      reads=[stv_b], writes=[DB["v_scr"]])

            for i, (ea, sqa_, sqb_, kdim) in enumerate(qnorm):
                P.ins("pe", lambda e, a=ea, bq=sqa_,
                      st=(i == 0), sp=(i == len(qnorm) - 1):
                      e.matmul(ps[2][0:16, 0:512], a, bq, start=st, stop=sp),
                      reads=[ematb_b, sqb_], writes=[psb[2]])
            P.ins("dve", lambda e, d=qn2[:, t0:t0 + TAU],
                  s=ps[2][0:16, 0:512]: e.tensor_copy(d, s),
                  reads=[psb[2]], writes=[qn2_b])
            for i, (ea, sqa_, sqb_, kdim) in enumerate(knorm):
                P.ins("pe", lambda e, a=ea, bq=sqa_,
                      st=(i == 0), sp=(i == len(knorm) - 1):
                      e.matmul(ps[3][0:16, 0:512], a, bq, start=st, stop=sp),
                      reads=[ematb_b, sqb_], writes=[psb[3]])
            P.ins("dve", lambda e, d=kn2[:, t0:t0 + TAU],
                  s=ps[3][0:16, 0:512]: e.tensor_copy(d, s),
                  reads=[psb[3]], writes=[kn2_b])

            if tau + 1 < NTAU:
                emit_ln1_transposes(tau + 1)

        kst, kst_b = STATS[2]
        kmax = kst[0:16, 6:7]
        P.ins("dve", lambda e: e.reduce_max(kmax, kn2[:, :], axis=AX.X),
              reads=[kn2_b], writes=[kst_b])
        P.ins("act", lambda e: e.activation(kmax, kmax, AF.Sqrt),
              reads=[kst_b], writes=[kst_b])
        P.ins("act", lambda e: e.activation(bound[:, :], qn2[:, :], AF.Sqrt),
              reads=[qn2_b], writes=[bound_b])
        P.ins("dve", lambda e: e.tensor_scalar_mul(bound[:, :], bound[:, :], kmax),
              reads=[bound_b, kst_b], writes=[bound_b])

        es_a2.close()
        es_a.close()
        es_weff.close()
        P.barrier()
        if stop_after == "A":
            es_bk1.close()
            P.retire()
            P.replay(nc)
            return nc

        es_bk2 = contextlib.ExitStack()
        o_all, o_all_b = sb(es_bk2, "o_all", [128, 8 * T], BF16)

        es_bs = contextlib.ExitStack()
        QBUF = [sb(es_bs, f"qbuf{i}", [128, T], BF16) for i in range(2)]
        KBUF = [sb(es_bs, f"kbuf{i}", [128, T], BF16) for i in range(2)]
        VFM = [sb(es_bs, f"vfm{i}", [64, T], BF16) for i in range(2)]
        VA = [sb(es_bs, f"va{i}", [128, 32 * 65], BF16) for i in range(2)]
        PTB = [sb(es_bs, f"ptb{i}", [128, 512], BF16) for i in range(4)]
        maskb, maskb_b = sb(es_bs, "maskb", [128, 4 * 512], BF16)
        tsc, tsc_b = sb(es_bs, "tsc", [128, 8 * C], BF16)
        RB64 = [sb(es_bs, f"rb64_{i}", [64, 512], F32) for i in range(2)]
        RROW = [sb(es_bs, f"rrow{i}", [1, 512], F32) for i in range(2)]
        RH = [sb(es_bs, f"rh{i}", [1, 512], BF16) for i in range(2)]
        RL = [sb(es_bs, f"rl{i}", [1, 512], BF16) for i in range(2)]
        RLF = [sb(es_bs, f"rlf{i}", [1, 512], F32) for i in range(2)]
        onesbf, onesbf_b = sb(es_bs, "onesbf", [1, 128], BF16)
        P.ins("dve", lambda e: e.memset(onesbf[:, :], 1.0),
              writes=[onesbf_b])

        P.dma("pool", lambda e: e.dma_start(
            out=maskb[:, :].rearrange("p (m j) -> p m j", j=512),
            in_=mask_h.rearrange("m p j -> p m j")),
            reads=[DB["maskc"]], writes=[maskb_b])

        for sl in range(2):
            qb_, qb_b = QBUF[sl]
            kb_, kb_b = KBUF[sl]
            va_, va_b = VA[sl]
            P.ins("dve", lambda e, d=qb_[64:128, :]: e.memset(d, 0.0),
                  writes=[qb_b])
            P.ins("dve", lambda e, d=kb_[64:128, :]: e.memset(d, 0.0),
                  writes=[kb_b])
            P.ins("dve", lambda e, d=va_[:, :]: e.memset(d, 1.0), writes=[va_b])
            P.dma("sp", lambda e, d=kb_[64:96, :]:
                  e.dma_start(out=d, in_=kr_s[:, :]),
                  reads=[kr_b], writes=[kb_b])
            P.dma("sp", lambda e, d=kb_[96:97, :]:
                  e.dma_start(out=d, in_=neg1b[:, :]),
                  reads=[neg1b_b], writes=[kb_b])

        _tlist = [(wh, db, cache, cdb, e)
                  for (wh, db, cache, cdb) in
                  ((w1_h, DB["w1"], w1b_scr, DB["w1b"]),
                   (w3_h, DB["w3"], w3b_scr, DB["w3b"]),
                   (w2_h, DB["w2"], w2b_scr, DB["w2b"]))
                  for e in range(NE)]

        def emit_transcode(idx):
            if idx >= len(_tlist):
                return
            wh, db, cache, cdb, e_idx = _tlist[idx]
            dst, dst_b = tsc, tsc_b
            for c_ in range(8):
                P.dma("pool", lambda e, d=dst[:, c_ * C:(c_ + 1) * C],
                      s=wh[e_idx, c_ * 128:(c_ + 1) * 128, :]:
                      e.dma_start(out=d, in_=s),
                      reads=[db], writes=[dst_b])
            for c_ in range(8):
                P.dma("sp", lambda e,
                      d=cache[e_idx, c_ * 128:(c_ + 1) * 128, :],
                      s=dst[:, c_ * C:(c_ + 1) * C]:
                      e.dma_start(out=d, in_=s),
                      reads=[dst_b], writes=[cdb])

        def emit_head_loads(h):
            sl = h % 2
            qb_, qb_b = QBUF[sl]
            kb_, kb_b = KBUF[sl]
            vf_, vf_b = VFM[sl]
            P.dma("sp", lambda e, d=qb_[0:96, :], s=q_scr[h, :, :]:
                  e.dma_start(out=d, in_=s),
                  reads=[DB["q_scr"]], writes=[qb_b])
            P.dma("pool", lambda e, d=qb_[96:97, :], s=bound[h:h + 1, :]:
                  e.dma_start(out=d, in_=s),
                  reads=[bound_b], writes=[qb_b])
            P.dma("sp", lambda e, d=kb_[0:64, :], s=k_scr[h, :, :]:
                  e.dma_start(out=d, in_=s),
                  reads=[DB["k_scr"]], writes=[kb_b])
            P.dma("sp", lambda e, d=vf_[:, :], s=v_scr[h, :, :]:
                  e.dma_start(out=d, in_=s),
                  reads=[DB["v_scr"]], writes=[vf_b])

        def emit_vtrans(h):
            sl = h % 2
            vf_, vf_b = VFM[sl]
            va_, va_b = VA[sl]
            thunks = []
            for g in range(4):
                def th(g=g):
                    pt, pt_b = ps[6 + (g % 2)], psb[6 + (g % 2)]
                    for j in range(8):
                        kt = g * 8 + j
                        P.ins("pe", lambda e, o=pt[:, j * 64:(j + 1) * 64],
                              i=vf_[:, kt * 128:(kt + 1) * 128]:
                              e.transpose(o, i, identb[0:64, 0:64]),
                              reads=[vf_b, identb_b], writes=[pt_b])
                    src = pt[:, 0:512].rearrange("p (t c) -> p t c", c=64)
                    dst = va_[:, :].rearrange("p (t c) -> p t c", c=65)[
                        :, g * 8:(g + 1) * 8, 0:64]
                    P.ins("dve", lambda e, d=dst, s=src: e.tensor_copy(d, s),
                          reads=[pt_b], writes=[va_b])
                thunks.append(th)
            return thunks

        pend_norm = [None]
        pend_vt = []
        emit_head_loads(0)
        for _t in emit_vtrans(0):
            _t()
        for h in range(H):
            sl = h % 2
            qb_, qb_b = QBUF[sl]
            kb_, kb_b = KBUF[sl]
            va_, va_b = VA[sl]
            emit_transcode(2 * h)

            for qi in range(NQB):
                if qi == 4:
                    emit_transcode(2 * h + 1)
                    if h + 1 < H:
                        emit_head_loads(h + 1)
                if qi == 6 and h + 1 < H:
                    pend_vt.extend(emit_vtrans(h + 1))
                po_i = 2 + (qi % 2)
                nkt = 4 * qi + 4

                def emit_ST(kt, qi2=None, h2=None, qi=qi, h=h):
                    if qi2 is None:
                        qi2 = qi
                    if h2 is None:
                        h2 = h
                    qb2, qb2_b = QBUF[h2 % 2]
                    kb2, kb2_b = KBUF[h2 % 2]
                    si = (0, 1, 4, 5)[kt % 4]
                    pS, pS_b = ps[si], psb[si]
                    diag = kt >= 4 * qi2
                    P.ins("pe", lambda e, o=pS[:, 0:512],
                          a=kb2[:, kt * 128:(kt + 1) * 128],
                          b=qb2[:, qi2 * 512:(qi2 + 1) * 512], sp=not diag:
                          e.matmul(o, a, b, start=True, stop=sp),
                          reads=[kb2_b, qb2_b], writes=[pS_b])
                    if diag:
                        mvar = kt - 4 * qi2
                        P.ins("pe", lambda e, o=pS[:, 0:512],
                              mk=maskb[:, mvar * 512:(mvar + 1) * 512]:
                              e.matmul(o, identb[:, :], mk,
                                       start=False, stop=True),
                              reads=[identb_b, maskb_b], writes=[pS_b])

                if h == 0 and qi == 0:
                    for kt in range(4):
                        emit_ST(kt)
                for kt in range(nkt):
                    if pend_vt and kt % 7 == 6:
                        pend_vt.pop(0)()
                    if pend_norm[0] is not None:
                        if kt == 1:
                            pend_norm[0][0]()
                        if kt == min(4, nkt - 1):
                            pend_norm[0][1]()
                            pend_norm[0] = None
                    si = (0, 1, 4, 5)[kt % 4]
                    pS, pS_b = ps[si], psb[si]
                    ptv, ptv_b = PTB[kt % 4]
                    P.ins("act", lambda e, d=ptv[:, :], s=pS[:, 0:512]:
                          e.activation(d, s, AF.Exp),
                          reads=[pS_b], writes=[ptv_b])
                    P.ins("pe", lambda e, o=ps[po_i][0:65, 0:512],
                          a=va_[:, kt * 65:(kt + 1) * 65], b=ptv[:, :],
                          st=(kt == 0), sp=(kt == nkt - 1):
                          e.matmul(o, a, b, start=st, stop=sp),
                          reads=[va_b, ptv_b], writes=[psb[po_i]])
                    nx4 = kt + 4
                    if nx4 < nkt:
                        emit_ST(nx4)
                    elif qi + 1 < NQB:
                        emit_ST(nx4 - nkt, qi2=qi + 1)
                    elif h + 1 < H:
                        emit_ST(nx4 - nkt, qi2=0, h2=h + 1)

                def mk_norm(qi=qi, po_i=po_i, h=h):
                    rrow, rrow_b = RROW[qi % 2]
                    rh, rh_b = RH[qi % 2]
                    rl, rl_b = RL[qi % 2]
                    rlf, rlf_b = RLF[qi % 2]
                    rb64, rb64_b = RB64[qi % 2]

                    def norm_a():
                        P.ins("dve", lambda e, d=rrow[:, :], o=ps[po_i]:
                              e.reciprocal(d, o[64:65, 0:512]),
                              reads=[psb[po_i]], writes=[rrow_b])
                        P.ins("dve", lambda e: e.tensor_copy(rh[:, :],
                                                             rrow[:, :]),
                              reads=[rrow_b], writes=[rh_b])
                        P.ins("dve", lambda e: e.scalar_tensor_tensor(
                            rlf[:, :], rh[:, :], -1.0, rrow[:, :],
                            ALU.mult, ALU.add),
                            reads=[rh_b, rrow_b], writes=[rlf_b])
                        P.ins("dve", lambda e: e.tensor_copy(rl[:, :],
                                                             rlf[:, :]),
                              reads=[rlf_b], writes=[rl_b])

                    def norm_b():
                        P.ins("pe", lambda e, o=ps[po_i][64:128, 0:512],
                              r=rh[0:1, :]:
                              e.matmul(o, onesbf[0:1, 0:64], r,
                                       start=True, stop=False),
                              reads=[onesbf_b, rh_b], writes=[psb[po_i]])
                        P.ins("pe", lambda e, o=ps[po_i][64:128, 0:512],
                              r=rl[0:1, :]:
                              e.matmul(o, onesbf[0:1, 0:64], r,
                                       start=False, stop=True),
                              reads=[onesbf_b, rl_b], writes=[psb[po_i]])
                        P.ins("dve", lambda e, s=ps[po_i][64:128, 0:512],
                              d=rb64[:, :]: e.tensor_copy(d, s),
                              reads=[psb[po_i]], writes=[rb64_b])
                        dsto = o_all[64 * (h % 2):64 * (h % 2) + 64,
                                     (h // 2) * T + qi * 512:
                                     (h // 2) * T + (qi + 1) * 512]
                        P.ins("dve", lambda e, d=dsto, o=ps[po_i],
                              r=rb64[:, :]:
                              e.tensor_mul(d, o[0:64, 0:512], r),
                              reads=[psb[po_i], rb64_b], writes=[o_all_b])
                    return norm_a, norm_b

                pend_norm[0] = mk_norm()

        if pend_norm[0] is not None:
            pend_norm[0][0]()
            pend_norm[0][1]()
            pend_norm[0] = None
        es_bs.close()
        P.barrier()
        if stop_after == "B":
            es_bk2.close()
            es_bk1.close()
            P.retire()
            P.replay(nc)
            return nc

        es_c = contextlib.ExitStack()
        wo_s, wo_sb = sb(es_c, "wo_s", [128, 8 * C], BF16)
        gwt_s, gwt_sb = sb(es_c, "gwt_s", [128, 8 * 8], BF16)
        AO = [sb(es_c, f"ao{i}", [128, C], F32) for i in range(4)]
        SQC = [sb(es_c, f"sqc{i}", [128, C], F32) for i in range(4)]
        XTC = [sb(es_c, f"xtc{i}", [128, C], F32) for i in range(4)]
        LN1T = [sb(es_c, f"ln1t{i}", [128, C], BF16) for i in range(4)]
        ln1s, ln1s_b = sb(es_c, "ln1s", [128, 8 * 1024], BF16)
        gs_ln2 = make_gstat(es_c, "gsln2")
        gs_ln3 = make_gstat(es_c, "gsln3")
        gtsE, gtsE_b = sb(es_c, "gtsE", [128, 32], F32)
        gwb, gwb_b = sb(es_c, "gwb", [128, 32], F32)
        g4, _g4unused = sb(es_c, "g4", [128, 16], F32)
        g4s_b, g4r_b, g4m_b, g4m2_b = Buf(), Buf(), Buf(), Buf()
        gstg, gstg_b = sb(es_c, "gstg", [32, 128], F32)

        P.dma("pool", lambda e: e.dma_start(
            out=wo_s[:, :].rearrange("p (c j) -> p c j", j=C),
            in_=wo_h.rearrange("(c p) j -> p c j", p=128)),
            reads=[DB["wo"]], writes=[wo_sb])
        P.dma("pool", lambda e: e.dma_start(
            out=gwt_s[:, :].rearrange("p (c j) -> p c j", j=8),
            in_=gwt_h.rearrange("(c p) j -> p c j", p=128)),
            reads=[DB["gwt"]], writes=[gwt_sb])

        def emit_wo(t):
            ao, ao_b = AO[t % 4]
            for half in range(2):
                ph, ph_b = ps[half], psb[half]
                for c_ in range(8):
                    P.ins("pe", lambda e, o=ph[:, 0:512],
                          a=o_all[:, c_ * T + t * 128:c_ * T + (t + 1) * 128],
                          b=wo_s[:, c_ * C + half * 512:c_ * C + half * 512 + 512],
                          st=(c_ == 0), sp=(c_ == 7):
                          e.matmul(o, a, b, start=st, stop=sp),
                          reads=[o_all_b, wo_sb], writes=[ph_b])
                P.ins("act", lambda e, d=ao[:, half * 512:half * 512 + 512],
                      s=ph[:, 0:512]: e.activation(d, s, AF.Identity),
                      reads=[ph_b], writes=[ao_b])

        def emit_xload(t):
            xv, xv_b = XTC[t % 4]
            P.dma("sp", lambda e, d=xv[:, :], s=x_h[t * 128:(t + 1) * 128, :]:
                  e.dma_start(out=d, in_=s),
                  reads=[DB["x"]], writes=[xv_b])

        for t in range(4):
            emit_wo(t)
            emit_xload(t)
        for grp in range(NT // 4):
            ts4 = list(range(grp * 4, grp * 4 + 4))
            layer_norm_group_v2([
                (AO[t % 4][0][:, :], AO[t % 4][1], AO[t % 4][0][:, :],
                 AO[t % 4][1], SQC[t % 4][0], SQC[t % 4][1])
                for t in ts4], gs_ln2)
            for t in ts4:
                xv, xv_b = XTC[t % 4]
                P.ins("dve", lambda e, d=xv[:, :],
                      l=AO[t % 4][0][:, :]: e.tensor_add(d, d, l),
                      reads=[xv_b, AO[t % 4][1]], writes=[xv_b])
            if grp + 1 < NT // 4:
                for t in range(grp * 4 + 4, grp * 4 + 8):
                    emit_wo(t)
            for t in ts4:
                xv, xv_b = XTC[t % 4]
                P.dma("sp", lambda e, s=xv[:, :],
                      d=x1_scr[t * 128:(t + 1) * 128, :]:
                      e.dma_start(out=d, in_=s),
                      reads=[xv_b], writes=[DB["x1_scr"]])
            layer_norm_group_v2([
                (XTC[t % 4][0][:, :], XTC[t % 4][1], LN1T[t % 4][0][:, :],
                 LN1T[t % 4][1], SQC[t % 4][0], SQC[t % 4][1])
                for t in ts4], gs_ln3)
            if grp + 1 < NT // 4:
                for t in range(grp * 4 + 4, grp * 4 + 8):
                    emit_xload(t)
            for t in ts4:
                ln1t, ln1t_b = LN1T[t % 4]
                for g in range(2):
                    pt, pt_b = ps[6 + g], psb[6 + g]
                    for c4 in range(4):
                        c = g * 4 + c4
                        P.ins("pe", lambda e, o=pt[:, c4 * 128:(c4 + 1) * 128],
                              i=ln1t[:, c * 128:(c + 1) * 128]:
                              e.transpose(o, i, identb[:, :]),
                              reads=[ln1t_b, identb_b], writes=[pt_b])
                    src = pt[:, 0:512].rearrange("p (c j) -> p c j", j=128)
                    dst = ln1s[:, :].rearrange("p (c j) -> p c j", j=1024)[
                        :, g * 4:(g + 1) * 4, (t % 8) * 128:(t % 8) * 128 + 128]
                    P.ins("dve", lambda e, d=dst, s=src: e.tensor_copy(d, s),
                          reads=[pt_b], writes=[ln1s_b])
            for t in ts4:
                for c_ in range(8):
                    P.ins("pe", lambda e, o=ps[2][:, (t % 4) * 8:
                                                  (t % 4) * 8 + 8],
                          a=ln1s[:, c_ * 1024 + (t % 8) * 128:
                                 c_ * 1024 + (t % 8) * 128 + 128],
                          b=gwt_s[:, c_ * 8:(c_ + 1) * 8],
                          st=(c_ == 0), sp=(c_ == 7):
                          e.matmul(o, a, b, start=st, stop=sp),
                          reads=[ln1s_b, gwt_sb], writes=[psb[2]])
            g3 = gtsE[:, :].rearrange("p (g j) -> p g j", j=8)
            w3_ = gwb[:, :].rearrange("p (g j) -> p g j", j=8)
            g4s = g4[:, 0:4].rearrange("p (g j) -> p g j", j=1)
            g4r = g4[:, 4:8].rearrange("p (g j) -> p g j", j=1)
            g4m = g4[:, 8:12].rearrange("p (g j) -> p g j", j=1)
            g4m2 = g4[:, 12:16].rearrange("p (g j) -> p g j", j=1)
            P.ins("act", lambda e: e.activation(gtsE[:, :], ps[2][:, 0:32],
                                                AF.Exp),
                  reads=[psb[2]], writes=[gtsE_b])
            P.ins("dve", lambda e: e.tensor_reduce(g4s, g3, axis=AX.X,
                                                   op=ALU.add),
                  reads=[gtsE_b], writes=[g4s_b])
            P.ins("dve", lambda e: e.reciprocal(g4[:, 4:8], g4[:, 0:4]),
                  reads=[g4s_b], writes=[g4r_b])
            P.ins("dve", lambda e: e.tensor_reduce(g4m, g3, axis=AX.X,
                                                   op=ALU.max),
                  reads=[gtsE_b], writes=[g4m_b])
            P.ins("dve", lambda e, ab=broadcast_tensor_aps(g3, g4m):
                  e.tensor_tensor(w3_, ab[0], ab[1], ALU.is_ge),
                  reads=[gtsE_b, g4m_b], writes=[gwb_b])
            P.ins("dve", lambda e: e.scalar_tensor_tensor(
                gwb[:, :], gwb[:, :], -1e30, gtsE[:, :], ALU.mult, ALU.add),
                reads=[gwb_b, gtsE_b], writes=[gwb_b])
            P.ins("dve", lambda e: e.tensor_reduce(g4m2, w3_, axis=AX.X,
                                                   op=ALU.max),
                  reads=[gwb_b], writes=[g4m2_b])
            P.ins("dve", lambda e, ab=broadcast_tensor_aps(g3, g4m2):
                  e.tensor_tensor(w3_, ab[0], ab[1], ALU.is_ge),
                  reads=[gtsE_b, g4m2_b], writes=[gwb_b])
            P.ins("dve", lambda e: e.tensor_mul(gwb[:, :], gwb[:, :],
                                                gtsE[:, :]),
                  reads=[gwb_b, gtsE_b], writes=[gwb_b])
            P.ins("dve", lambda e, ab=broadcast_tensor_aps(w3_, g4r):
                  e.tensor_tensor(w3_, ab[0], ab[1], ALU.mult),
                  reads=[gwb_b, g4r_b], writes=[gwb_b])
            P.ins("pe", lambda e: e.transpose(ps[5][0:32, 0:128], gwb[:, :],
                                              identf[:, :]),
                  reads=[gwb_b, identf_b], writes=[psb[5]])
            P.ins("dve", lambda e: e.tensor_copy(gstg[:, :],
                                                 ps[5][0:32, 0:128]),
                  reads=[psb[5]], writes=[gstg_b])
            P.dma("sp", lambda e,
                  d=g_scr[:, grp * 512:grp * 512 + 512].rearrange(
                      "e (t j) -> t e j", j=128):
                  e.dma_start(out=d, in_=gstg[:, :]),
                  reads=[gstg_b], writes=[DB["g_scr"]])
            if grp % 2 == 1:
                for c_ in range(8):
                    P.dma("sp", lambda e,
                          d=ln1_scr[c_, :, (grp // 2) * 1024:
                                    (grp // 2) * 1024 + 1024],
                          s=ln1s[:, c_ * 1024:(c_ + 1) * 1024]:
                          e.dma_start(out=d, in_=s),
                          reads=[ln1s_b], writes=[DB["ln1_scr"]])

        es_c.close()
        es_bk2.close()
        es_bk1.close()
        P.barrier()
        if stop_after == "C":
            P.retire()
            P.replay(nc)
            return nc

        es_d = contextlib.ExitStack()
        LN1F = [sb(es_d, f"ln1f{i}", [128, 8 * FTAU], BF16) for i in range(2)]
        XE = [sb(es_d, f"xe{i}", [128, 8 * FTAU], BF16) for i in range(2)]
        hbuf, hbuf_b = sb(es_d, "hbuf", [128, 8 * FTAU], BF16)
        w1s_, w1s_b = sb(es_d, "w1s", [128, 8 * C], BF16)
        w3s_, w3s_b = sb(es_d, "w3s", [128, 8 * C], BF16)
        w2s_, w2s_b = sb(es_d, "w2s", [128, 8 * C], BF16)
        oacc, oacc_b = sb(es_d, "oacc", [128, 8 * FTAU], F32)
        GROW = [sb(es_d, f"grow{i}", [1, FTAU], BF16) for i in range(2)]
        GSB = [sb(es_d, f"gsb{i}", [128, FTAU], BF16) for i in range(2)]
        onesb, onesb_b = sb(es_d, "onesb", [1, 128], BF16)
        P.ins("dve", lambda e: e.memset(onesb[:, :], 1.0), writes=[onesb_b])
        TSIL = [sb(es_d, f"tsil{i}", [128, 512], BF16) for i in range(2)]
        X1D = [sb(es_d, f"x1d{i}", [128, C], F32) for i in range(4)]
        SQD = [sb(es_d, f"sqd{i}", [128, C], F32) for i in range(4)]

        def load_w(e_idx, ft, wh, db, cache, cache_db, dst, dst_b):
            for c_ in range(8):
                P.dma("sp", lambda e, d=dst[:, c_ * C:(c_ + 1) * C],
                      s=cache[e_idx, c_ * 128:(c_ + 1) * 128, :]:
                      e.dma_start(out=d, in_=s),
                      reads=[cache_db], writes=[dst_b])

        def load_ln1f(ft):
            ln1f, ln1f_b = LN1F[ft % 2]
            t0 = ft * FTAU
            for c_ in range(8):
                P.dma("sp", lambda e, d=ln1f[:, c_ * FTAU:(c_ + 1) * FTAU],
                      s=ln1_scr[c_, :, t0:t0 + FTAU]:
                      e.dma_start(out=d, in_=s),
                      reads=[DB["ln1_scr"]], writes=[ln1f_b])

        pend_epi = [None]
        load_ln1f(0)
        for ft in range(NFT):
            t0 = ft * FTAU
            ln1f, ln1f_b = LN1F[ft % 2]
            if ft + 1 < NFT:
                load_ln1f(ft + 1)

            def emit_gxe(nei, ei, ln1f=ln1f, ln1f_b=ln1f_b, t0=t0):
                gv, gv_b = GROW[ei % 2]
                gsb, gsb_b = GSB[ei % 2]
                xe, xe_b = XE[ei % 2]
                P.dma("pool", lambda e, d=gv[:, :],
                      s=g_scr[ei:ei + 1, t0:t0 + FTAU]:
                      e.dma_start(out=d, in_=s),
                      reads=[DB["g_scr"]], writes=[gv_b])
                for half in range(2):
                    pgi = (0, 4)[half]
                    P.ins("pe", lambda e, o=ps[pgi][:, 0:512],
                          r=gv[0:1, half * 512:half * 512 + 512]:
                          e.matmul(o, onesb[0:1, :], r,
                                   start=True, stop=True),
                          reads=[onesb_b, gv_b], writes=[psb[pgi]])
                    P.ins("act", lambda e,
                          d=gsb[:, half * 512:half * 512 + 512],
                          s=ps[pgi][:, 0:512]:
                          e.activation(d, s, AF.Identity),
                          reads=[psb[pgi]], writes=[gsb_b])
                for c_ in range(8):
                    P.ins("dve", lambda e,
                          d=xe[:, c_ * FTAU:(c_ + 1) * FTAU],
                          a=ln1f[:, c_ * FTAU:(c_ + 1) * FTAU]:
                          e.tensor_mul(d, a, gsb[:, :]),
                          reads=[ln1f_b, gsb_b], writes=[xe_b])

            order = [8] + list(range(8))
            emit_gxe(1, order[1])
            for nei, ei in enumerate(order):
                if ei < 8:
                    xe_t, xe_tb = XE[ei % 2]
                else:
                    xe_t, xe_tb = ln1f, ln1f_b

                load_w(ei, ft, w1_h, DB["w1"], w1b_scr, DB["w1b"], w1s_, w1s_b)
                load_w(ei, ft, w3_h, DB["w3"], w3b_scr, DB["w3b"], w3s_, w3s_b)
                if nei == 0 and pend_epi[0] is not None:
                    pend_epi[0]()
                    pend_epi[0] = None
                for m in range(8):
                    for sub in range(2):
                        par = (m * 2 + sub) % 2
                        p_a1, p_a3 = (0, 1) if par == 0 else (4, 5)
                        tsil, tsil_b = TSIL[par]
                        for (pidx, ws, wsb) in ((p_a1, w1s_, w1s_b),
                                                (p_a3, w3s_, w3s_b)):
                            for c_ in range(8):
                                P.ins("pe", lambda e, o=ps[pidx][:, 0:512],
                                      a=ws[:, c_ * C + m * 128:
                                           c_ * C + (m + 1) * 128],
                                      b=xe_t[:, c_ * FTAU + sub * 512:
                                             c_ * FTAU + sub * 512 + 512],
                                      st=(c_ == 0), sp=(c_ == 7):
                                      e.matmul(o, a, b, start=st, stop=sp),
                                      reads=[wsb, xe_tb], writes=[psb[pidx]])
                        P.ins("act", lambda e, d=tsil[:, :], s=ps[p_a1][:, 0:512]:
                              e.activation(d, s, AF.Silu),
                              reads=[psb[p_a1]], writes=[tsil_b])
                        P.ins("dve", lambda e,
                              d=hbuf[:, m * FTAU + sub * 512:
                                     m * FTAU + sub * 512 + 512],
                              s3=ps[p_a3][:, 0:512], ts=tsil[:, :]:
                              e.tensor_mul(d, s3, ts),
                              reads=[psb[p_a3], tsil_b], writes=[hbuf_b])
                load_w(ei, ft, w2_h, DB["w2"], w2b_scr, DB["w2b"], w2s_, w2s_b)
                if nei + 1 < len(order) and order[nei + 1] != order[1]:
                    emit_gxe(nei + 1, order[nei + 1])
                for t2 in range(8):
                    for half in range(2):
                        pidx = 2 + half
                        for m in range(8):
                            P.ins("pe", lambda e, o=ps[pidx][:, 0:512],
                                  a=hbuf[:, m * FTAU + t2 * 128:
                                         m * FTAU + (t2 + 1) * 128],
                                  b=w2s_[:, m * C + half * 512:
                                         m * C + half * 512 + 512],
                                  st=(m == 0), sp=(m == 7):
                                  e.matmul(o, a, b, start=st, stop=sp),
                                  reads=[hbuf_b, w2s_b], writes=[psb[pidx]])
                        dsl = oacc[:, t2 * C + half * 512:t2 * C + half * 512 + 512]
                        if nei == 0:
                            P.ins("dve", lambda e, d=dsl, s=ps[pidx][:, 0:512]:
                                  e.tensor_copy(d, s),
                                  reads=[psb[pidx]], writes=[oacc_b])
                        else:
                            P.ins("dve", lambda e, d=dsl, s=ps[pidx][:, 0:512]:
                                  e.tensor_add(d, s, d),
                                  reads=[psb[pidx], oacc_b], writes=[oacc_b])

            def mk_epi(ft=ft):
              def epi():
                for half4 in range(2):
                    t2s = list(range(half4 * 4, half4 * 4 + 4))
                    epi = []
                    for t2 in t2s:
                        tg = ft * 8 + t2
                        xv, xv_b = X1D[t2 % 4]
                        P.dma("sp", lambda e, d=xv[:, :],
                              s=x1_scr[tg * 128:(tg + 1) * 128, :]:
                              e.dma_start(out=d, in_=s),
                              reads=[DB["x1_scr"]], writes=[xv_b])
                        st_t, st_b = STATS[t2 % 4]
                        epi.append((t2, tg, xv, xv_b,
                                    oacc[:, t2 * C:(t2 + 1) * C],
                                    SQD[t2 % 4][0], SQD[t2 % 4][1],
                                    st_t[:, 0:1], st_t[:, 1:2], st_t[:, 2:3],
                                    st_t[:, 3:4], st_t[:, 4:5], st_t[:, 5:6],
                                    st_t[:, 6:7], st_b))
                    for (t2, tg, xv, xv_b, osl, sq_t, sq_tb, s1, s2, mm_, m2,
                         vv, rr, bb, st_b) in epi:
                        P.ins("dve", lambda e, o=s1, i=osl:
                              e.reduce_sum(o, i, axis=AX.X),
                              reads=[oacc_b], writes=[st_b])
                    for (t2, tg, xv, xv_b, osl, sq_t, sq_tb, s1, s2, mm_, m2,
                         vv, rr, bb, st_b) in epi:
                        P.ins("act", lambda e, o=sq_t[:, :], i=osl:
                              e.activation(o, i, AF.Square),
                              reads=[oacc_b], writes=[sq_tb])
                    for (t2, tg, xv, xv_b, osl, sq_t, sq_tb, s1, s2, mm_, m2,
                         vv, rr, bb, st_b) in epi:
                        P.ins("dve", lambda e, o=s2, i=sq_t[:, :]:
                              e.reduce_sum(o, i, axis=AX.X),
                              reads=[sq_tb], writes=[st_b])
                    for (t2, tg, xv, xv_b, osl, sq_t, sq_tb, s1, s2, mm_, m2,
                         vv, rr, bb, st_b) in epi:
                        P.ins("dve", lambda e, o=mm_, i=s1:
                              e.tensor_scalar_mul(o, i, 1.0 / C),
                              reads=[st_b], writes=[st_b])
                    for (t2, tg, xv, xv_b, osl, sq_t, sq_tb, s1, s2, mm_, m2,
                         vv, rr, bb, st_b) in epi:
                        P.ins("dve", lambda e, o=m2, i=s1: e.scalar_tensor_tensor(
                            o, i, 1.0 / (C * C), i, ALU.mult, ALU.mult),
                            reads=[st_b], writes=[st_b])
                    for (t2, tg, xv, xv_b, osl, sq_t, sq_tb, s1, s2, mm_, m2,
                         vv, rr, bb, st_b) in epi:
                        P.ins("dve", lambda e, o=vv, i=s2, m=m2:
                              e.scalar_tensor_tensor(o, i, 1.0 / C, m,
                                                     ALU.mult, ALU.subtract),
                              reads=[st_b], writes=[st_b])
                    for (t2, tg, xv, xv_b, osl, sq_t, sq_tb, s1, s2, mm_, m2,
                         vv, rr, bb, st_b) in epi:
                        P.ins("act", lambda e, o=vv: e.activation(
                            o, o, AF.Sqrt, bias=epsb[:, 0:1]),
                            reads=[st_b, epsb_b], writes=[st_b])
                    for (t2, tg, xv, xv_b, osl, sq_t, sq_tb, s1, s2, mm_, m2,
                         vv, rr, bb, st_b) in epi:
                        P.ins("dve", lambda e, o=rr, i=vv: e.reciprocal(o, i),
                              reads=[st_b], writes=[st_b])
                    for (t2, tg, xv, xv_b, osl, sq_t, sq_tb, s1, s2, mm_, m2,
                         vv, rr, bb, st_b) in epi:
                        P.ins("dve", lambda e, o=bb, m=mm_, r=rr:
                              e.scalar_tensor_tensor(o, m, -1.0, r,
                                                     ALU.mult, ALU.mult),
                              reads=[st_b], writes=[st_b])
                    for (t2, tg, xv, xv_b, osl, sq_t, sq_tb, s1, s2, mm_, m2,
                         vv, rr, bb, st_b) in epi:
                        P.ins("dve", lambda e, d=xv[:, :], b=bb:
                              e.tensor_scalar_add(d, d, b),
                              reads=[xv_b, st_b], writes=[xv_b])
                    for (t2, tg, xv, xv_b, osl, sq_t, sq_tb, s1, s2, mm_, m2,
                         vv, rr, bb, st_b) in epi:
                        P.ins("dve", lambda e, d=xv[:, :], i=osl, r=rr:
                              e.scalar_tensor_tensor(d, i, r, d, ALU.mult, ALU.add),
                              reads=[oacc_b, st_b, xv_b], writes=[xv_b])
                    for (t2, tg, xv, xv_b, osl, sq_t, sq_tb, s1, s2, mm_, m2,
                         vv, rr, bb, st_b) in epi:
                        P.dma("sp", lambda e, s=xv[:, :],
                              d=out_h[tg * 128:(tg + 1) * 128, :]:
                              e.dma_start(out=d, in_=s),
                              reads=[xv_b], writes=[DB["out"]])

              return epi
            pend_epi[0] = mk_epi()

        if pend_epi[0] is not None:
            pend_epi[0]()
            pend_epi[0] = None
        es_d.close()
        P.retire()
        P.replay(nc)
    return nc


_CACHE = {}


def _get_nc():
    if "nc" not in _CACHE:
        _CACHE["nc"] = build_program()
    return _CACHE["nc"]


def make_inputs(L0_attn_x, L0_attn_W_DQ, L0_attn_W_UQ, L0_attn_W_DKV,
                L0_attn_W_UK, L0_attn_W_UV, L0_attn_W_QR, L0_attn_W_KR,
                L0_attn_W_O, L0_ffn_gate_w, L0_ffn_w1, L0_ffn_w2, L0_ffn_w3,
                L0_ffn_w1s, L0_ffn_w2s, L0_ffn_w3s):
    f = np.float32
    c = np.ascontiguousarray
    sc4, masks, ident, emat = make_consts()
    shared = dict(
        wdqt=c(np.asarray(L0_attn_W_DQ, f).T),
        wuq=c(np.asarray(L0_attn_W_UQ, f)),
        wqr=c(np.asarray(L0_attn_W_QR, f)),
        wdkv=c(np.asarray(L0_attn_W_DKV, f)),
        wuk=c(np.asarray(L0_attn_W_UK, f)),
        wuv=c(np.asarray(L0_attn_W_UV, f)),
        wkr=c(np.asarray(L0_attn_W_KR, f)),
        wo=c(np.asarray(L0_attn_W_O, f)),
        gwt=c(np.asarray(L0_ffn_gate_w, f).T),
        w1=c(np.concatenate([np.asarray(L0_ffn_w1, f),
                             np.asarray(L0_ffn_w1s, f)], 0)),
        w2=c(np.concatenate([np.asarray(L0_ffn_w2, f),
                             np.asarray(L0_ffn_w2s, f)], 0)),
        w3=c(np.concatenate([np.asarray(L0_ffn_w3, f),
                             np.asarray(L0_ffn_w3s, f)], 0)),
        sc4=sc4, maskc=masks, identc=ident, ematc=emat,
    )
    xs = np.asarray(L0_attn_x, f)
    return [dict(shared, x=c(xs[b])) for b in range(8)]


def kernel(**inputs):
    in_maps = make_inputs(**inputs)
    nc = _get_nc()
    res = run_bass_kernel_spmd(nc, in_maps, core_ids=list(range(8)))
    return np.stack([res.results[b]["out"] for b in range(8)], 0)


# version 32
# speedup vs baseline: 1.0385x; 1.0099x over previous
import bisect
import contextlib
import numpy as np

import concourse.bass as bass
from concourse.bass import broadcast_tensor_aps
import concourse.mybir as mybir
from concourse.bass_utils import run_bass_kernel_spmd

F32 = mybir.dt.float32
BF16 = mybir.dt.bfloat16
AF = mybir.ActivationFunctionType
ALU = mybir.AluOpType
AX = mybir.AxisListType

T = 4096
C = 1024
H = 16
NT = T // 128
TAU = 512
NTAU = T // TAU
NQB = 8
FTAU = 1024
NFT = T // FTAU
NE = 9
INV_SQ = float(1.0 / np.sqrt(np.float32(96.0)))
SQ96 = float(np.sqrt(np.float32(96.0)))
EPS = 1e-5
NEG = -30000.0

QUEUES = ("pe", "act", "dve", "pool", "sp")


class Buf:
    __slots__ = ("writers", "readers", "chan")

    def __init__(self):
        self.writers = {}
        self.readers = {}
        self.chan = {}


class Prog:

    def __init__(self):
        self.q = {e: [] for e in QUEUES}
        self.cnt = {p: 0 for p in QUEUES}
        self.marks = {p: set() for p in QUEUES}
        self.know = {e: {} for e in QUEUES}
        self.clock = {p: {} for p in QUEUES}
        self.nchan = 0

    def _chan(self, buf, queue):
        ch = buf.chan.get(queue)
        if ch is None:
            ch = buf.chan[queue] = f"D{self.nchan}"
            self.nchan += 1
            self.cnt[ch] = 0
            self.marks[ch] = set()
            self.clock[ch] = {}
        return ch

    def _emit(self, eng, fn, reads, writes, producer):
        need = {}
        for b in reads:
            for p, v in b.writers.items():
                if need.get(p, 0) < v:
                    need[p] = v
        for b in writes:
            for d in (b.writers, b.readers):
                for p, v in d.items():
                    if need.get(p, 0) < v:
                        need[p] = v
        k = self.know[eng]
        for p in sorted(need):
            v = need[p]
            if p == "pe" and producer == "pe":
                continue
            if k.get(p, 0) >= v:
                continue
            self.q[eng].append(("w", p, v))
            self.marks[p].add(v)
            c = self.clock[p].get(v)
            if c:
                for pp, vv in c.items():
                    if k.get(pp, 0) < vv:
                        k[pp] = vv
            if k.get(p, 0) < v:
                k[p] = v
        vidx = self.cnt[producer] + 1
        self.cnt[producer] = vidx
        if producer.startswith("D"):
            self.marks[producer].add(vidx)
        c = dict(k)
        c[producer] = vidx
        self.clock[producer][vidx] = c
        self.q[eng].append(("i", fn, producer, vidx))
        for b in reads:
            if b.readers.get(producer, 0) < vidx:
                b.readers[producer] = vidx
        for b in writes:
            if b.writers.get(producer, 0) < vidx:
                b.writers[producer] = vidx

    def ins(self, eng, fn, reads=(), writes=()):
        self._emit(eng, fn, reads, writes, eng)

    def dma(self, queue, fn, reads=(), writes=()):
        assert writes, "dma needs a target buf (its channel)"
        self._emit(queue, fn, reads, writes, self._chan(writes[0], queue))

    def barrier(self):
        snap = {p: v for p, v in self.cnt.items() if v > 0}
        for eng in QUEUES:
            k = self.know[eng]
            for p, v in snap.items():
                if k.get(p, 0) < v:
                    self.q[eng].append(("w", p, v))
                    self.marks[p].add(v)
                    k[p] = v

    def retire(self):
        k = self.know["sp"]
        for p, v in self.cnt.items():
            if p.startswith("D") and v > 0 and k.get(p, 0) < v:
                self.q["sp"].append(("w", p, v))
                self.marks[p].add(v)
                k[p] = v

    def replay(self, nc):
        smarks = {p: sorted(m) for p, m in self.marks.items()}
        step = {p: (16 if p.startswith("D") else 1) for p in self.marks}
        with contextlib.ExitStack() as es:
            sems = {p: es.enter_context(nc.semaphore("sem_" + p))
                    for p, m in self.marks.items() if m}

            def runner(entries):
                def run(handle):
                    for e in entries:
                        if e[0] == "w":
                            _, p, v = e
                            handle.wait_ge(
                                sems[p],
                                bisect.bisect_right(smarks[p], v) * step[p])
                        else:
                            _, fn, p, vidx = e
                            inst = fn(handle)
                            if vidx in self.marks[p]:
                                inst.then_inc(sems[p], step[p])
                return run

            with nc.Block() as block:
                block.tensor(runner(self.q["pe"]))
                block.scalar(runner(self.q["act"]))
                block.vector(runner(self.q["dve"]))
                block.gpsimd(runner(self.q["pool"]))
                block.sync(runner(self.q["sp"]))


def rope_scale_table():
    t = np.arange(T, dtype=np.float64)
    inv = 1.0 / 10000.0 ** (np.arange(0, 32, 2, dtype=np.float64) / 32)
    freqs = t[:, None] * inv[None, :]
    emb = np.concatenate([freqs, freqs], -1)
    return (np.cos(emb) + np.sin(emb)).astype(np.float32)


def make_consts():
    scT = np.ascontiguousarray(rope_scale_table().T)
    sc4 = np.tile(scT, (4, 1)) * np.float32(INV_SQ)
    masks = np.zeros((4, 128, 512), np.float32)
    p = np.arange(128)[:, None]
    cc = np.arange(512)[None, :]
    for m in range(4):
        masks[m] = np.where(cc >= 128 * m + p, 0.0, NEG)
    ident = np.eye(128, dtype=np.float32)
    emat = np.zeros((13, 128, 16), np.float32)
    for m in range(8):
        emat[m, 0:64, 2 * m] = 1.0
        emat[m, 64:128, 2 * m + 1] = 1.0
    for m in range(4):
        for g in range(4):
            emat[8 + m, 32 * g:32 * g + 32, 4 * m + g] = 1.0
    emat[12, 0:32, :] = 1.0
    return sc4, masks, ident, emat


def build_program(stop_after=None):
    nc = bass.Bass("TRN2", target_bir_lowering=False, debug=False)
    P = Prog()

    def din(name, shape):
        return nc.dram_tensor(name, shape, F32, kind="ExternalInput").ap()

    x_h = din("x", [T, C])
    wdqt_h = din("wdqt", [4096, C])
    wuq_h = din("wuq", [4096, C])
    wqr_h = din("wqr", [4096, 512])
    wdkv_h = din("wdkv", [C, 256])
    wuk_h = din("wuk", [256, C])
    wuv_h = din("wuv", [256, C])
    wkr_h = din("wkr", [C, 32])
    wo_h = din("wo", [C, C])
    gwt_h = din("gwt", [C, 8])
    w1_h = din("w1", [NE, C, C])
    w2_h = din("w2", [NE, C, C])
    w3_h = din("w3", [NE, C, C])
    sc4_h = din("sc4", [128, T])
    mask_h = din("maskc", [4, 128, 512])
    ident_h = din("identc", [128, 128])
    emat_h = din("ematc", [13, 128, 16])

    out_h = nc.dram_tensor("out", [T, C], F32, kind="ExternalOutput").ap()

    q_scr = nc.dram_tensor("q_scr", [H, 96, T], BF16, kind="Internal").ap()
    k_scr = nc.dram_tensor("k_scr", [H, 64, T], BF16, kind="Internal").ap()
    v_scr = nc.dram_tensor("v_scr", [H, 64, T], BF16, kind="Internal").ap()
    x1_scr = nc.dram_tensor("x1_scr", [T, C], F32, kind="Internal").ap()
    w1b_scr = nc.dram_tensor("w1b_scr", [NE, C, C], BF16, kind="Internal").ap()
    w2b_scr = nc.dram_tensor("w2b_scr", [NE, C, C], BF16, kind="Internal").ap()
    w3b_scr = nc.dram_tensor("w3b_scr", [NE, C, C], BF16, kind="Internal").ap()
    ln1_scr = nc.dram_tensor("ln1_scr", [8, 128, T], BF16, kind="Internal").ap()
    g_scr = nc.dram_tensor("g_scr", [8, T], F32, kind="Internal").ap()

    DB = {}
    for n in ("x", "wdqt", "wuq", "wqr", "wdkv", "wuk", "wuv", "wkr", "wo",
              "gwt", "w1", "w2", "w3", "sc4", "maskc", "identc", "ematc",
              "out", "q_scr", "k_scr", "v_scr", "x1_scr", "ln1_scr", "g_scr",
              "w1b", "w2b", "w3b"):
        DB[n] = Buf()

    def sb(es, name, shape, dtype):
        t = es.enter_context(nc.sbuf_tensor(name, shape, dtype))
        return t, Buf()

    es_top = contextlib.ExitStack()
    with es_top:
        identf, identf_b = sb(es_top, "identf", [128, 128], F32)
        identb, identb_b = sb(es_top, "identb", [128, 128], BF16)
        onesf, onesf_b = sb(es_top, "onesf", [1, 128], F32)

        P.dma("sp", lambda e: e.dma_start(out=identf[:, :], in_=ident_h[:, :]),
              reads=[DB["identc"]], writes=[identf_b])
        P.dma("pool", lambda e: e.dma_start(out=identb[:, :], in_=ident_h[:, :]),
              reads=[DB["identc"]], writes=[identb_b])
        epsb, epsb_b = sb(es_top, "epsb", [128, 1], F32)
        P.ins("dve", lambda e: e.memset(onesf[:, :], 1.0), writes=[onesf_b])
        P.ins("dve", lambda e: e.memset(epsb[:, :], EPS), writes=[epsb_b])

        STATS = [sb(es_top, f"st{i}", [128, 8], F32) for i in range(16)]

        def layer_norm_group(items):
            views = []
            for (x_ap, x_buf, out_ap, out_buf, sq_t, sq_tb, slot) in items:
                st_t, st_b = slot
                views.append((x_ap, x_buf, out_ap, out_buf, sq_t, sq_tb,
                              st_t[:, 0:1], st_t[:, 1:2], st_t[:, 2:3],
                              st_t[:, 3:4], st_t[:, 4:5], st_t[:, 5:6], st_b))
            for (x_ap, x_buf, _, _, _, _, s1, *_r, st_b) in views:
                P.ins("dve", lambda e, o=s1, i=x_ap: e.reduce_sum(o, i, axis=AX.X),
                      reads=[x_buf], writes=[st_b])
            for (x_ap, x_buf, _, _, sq_t, sq_tb, *_r) in views:
                P.ins("act", lambda e, o=sq_t[:, :], i=x_ap:
                      e.activation(o, i, AF.Square),
                      reads=[x_buf], writes=[sq_tb])
            for (_, _, _, _, sq_t, sq_tb, s1, s2, *_r, st_b) in views:
                P.ins("dve", lambda e, o=s2, i=sq_t[:, :]:
                      e.reduce_sum(o, i, axis=AX.X),
                      reads=[sq_tb], writes=[st_b])
            for (_, _, _, _, _, _, s1, s2, mm_, m2, vv, rr, st_b) in views:
                P.ins("dve", lambda e, o=mm_, i=s1:
                      e.tensor_scalar_mul(o, i, 1.0 / C),
                      reads=[st_b], writes=[st_b])
            for (_, _, _, _, _, _, s1, s2, mm_, m2, vv, rr, st_b) in views:
                P.ins("dve", lambda e, o=m2, i=s1: e.scalar_tensor_tensor(
                    o, i, 1.0 / (C * C), i, ALU.mult, ALU.mult),
                    reads=[st_b], writes=[st_b])
            for (_, _, _, _, _, _, s1, s2, mm_, m2, vv, rr, st_b) in views:
                P.ins("dve", lambda e, o=vv, i=s2, m=m2: e.scalar_tensor_tensor(
                    o, i, 1.0 / C, m, ALU.mult, ALU.subtract),
                    reads=[st_b], writes=[st_b])
            for (_, _, _, _, _, _, s1, s2, mm_, m2, vv, rr, st_b) in views:
                P.ins("act", lambda e, o=vv: e.activation(o, o, AF.Sqrt,
                                                          bias=epsb[:, 0:1]),
                      reads=[st_b, epsb_b], writes=[st_b])
            for (_, _, _, _, _, _, s1, s2, mm_, m2, vv, rr, st_b) in views:
                P.ins("dve", lambda e, o=rr, i=vv: e.reciprocal(o, i),
                      reads=[st_b], writes=[st_b])
            for (x_ap, x_buf, out_ap, out_buf, _, _, s1, s2, mm_, m2, vv, rr,
                 st_b) in views:
                P.ins("dve", lambda e, o=out_ap, i=x_ap, m=mm_, r=rr:
                      e.tensor_scalar(o, i, m, r, ALU.subtract, ALU.mult),
                      reads=[x_buf, st_b], writes=[out_buf])

        def layer_norm(x_ap, x_buf, out_ap, out_buf, sq_t, sq_tb, slot):
            layer_norm_group([(x_ap, x_buf, out_ap, out_buf, sq_t, sq_tb, slot)])

        def make_gstat(es, name):
            t, _ = sb(es, name, [128, 32], F32)
            return (t, [Buf() for _ in range(4)], [Buf() for _ in range(4)],
                    Buf(), Buf(), Buf(), Buf())

        def layer_norm_group_v2(items, gs):
            gst, s1b, s2b, mqb, m2qb, vvqb, rrqb = gs
            n = len(items)
            assert n <= 4
            s1q, s2q = gst[:, 0:4], gst[:, 4:8]
            mq, m2q = gst[:, 8:12], gst[:, 12:16]
            vvq, rrq = gst[:, 16:20], gst[:, 20:24]
            for i, (x_ap, x_buf, _, _, _, _) in enumerate(items):
                P.ins("dve", lambda e, o=gst[:, i:i + 1], x=x_ap:
                      e.reduce_sum(o, x, axis=AX.X),
                      reads=[x_buf], writes=[s1b[i]])
            for i, (x_ap, x_buf, _, _, sq_t, sq_tb) in enumerate(items):
                P.ins("act", lambda e, o=sq_t[:, :], x=x_ap,
                      a=gst[:, 4 + i:5 + i]:
                      e.activation(o, x, AF.Square, accum_out=a),
                      reads=[x_buf], writes=[sq_tb, s2b[i]])
            P.ins("dve", lambda e: e.tensor_scalar_mul(mq, s1q, 1.0 / C),
                  reads=s1b, writes=[mqb])
            P.ins("dve", lambda e: e.scalar_tensor_tensor(
                m2q, s1q, 1.0 / (C * C), s1q, ALU.mult, ALU.mult),
                reads=s1b, writes=[m2qb])
            P.ins("dve", lambda e: e.scalar_tensor_tensor(
                vvq, s2q, 1.0 / C, m2q, ALU.mult, ALU.subtract),
                reads=s2b + [m2qb], writes=[vvqb])
            P.ins("act", lambda e: e.activation(vvq, vvq, AF.Sqrt,
                                                bias=epsb[:, 0:1]),
                  reads=[vvqb, epsb_b], writes=[vvqb])
            P.ins("dve", lambda e: e.reciprocal(rrq, vvq),
                  reads=[vvqb], writes=[rrqb])
            for i, (x_ap, x_buf, out_ap, out_buf, _, _) in enumerate(items):
                P.ins("dve", lambda e, o=out_ap, x=x_ap, m=gst[:, 8 + i:9 + i],
                      r=gst[:, 20 + i:21 + i]:
                      e.tensor_scalar(o, x, m, r, ALU.subtract, ALU.mult),
                      reads=[x_buf, mqb, rrqb], writes=[out_buf])

        es_bk1 = contextlib.ExitStack()
        bound, bound_b = sb(es_bk1, "bound", [16, T], F32)
        kr_s, kr_b = sb(es_bk1, "kr_s", [32, T], BF16)
        neg1b, neg1b_b = sb(es_bk1, "neg1b", [1, T], BF16)
        P.ins("dve", lambda e: e.memset(neg1b[:, :], -1.0), writes=[neg1b_b])

        es_weff = contextlib.ExitStack()
        weff, weff_b = sb(es_weff, "weff", [128, 8 * C], BF16)
        wqre, wqre_b = sb(es_weff, "wqre", [128, 8 * 512], BF16)
        es_a = contextlib.ExitStack()
        kn2, kn2_b = sb(es_a, "kn2", [16, T], BF16)
        qn2, qn2_b = sb(es_a, "qn2", [16, T], BF16)
        sc4s, sc4s_b = sb(es_a, "sc4s", [128, T], BF16)
        ematb, ematb_b = sb(es_a, "ematb", [128, 13 * 16], BF16)
        wdkv_s, wdkv_sb = sb(es_a, "wdkv_s", [128, 8 * 256], BF16)
        wuk_s, wuk_sb = sb(es_a, "wuk_s", [128, 2 * C], BF16)
        wuv_s, wuv_sb = sb(es_a, "wuv_s", [128, 2 * C], BF16)
        wkr_s, wkr_sb = sb(es_a, "wkr_s", [128, 8 * 32], BF16)

        STG = [sb(es_a, f"stg{i}", [128, 512], BF16) for i in range(4)]
        ckv, ckv_b = sb(es_a, "ckv", [128, 2 * 512], BF16)

        es_w = contextlib.ExitStack()
        wdqt_s, wdqt_sb = sb(es_w, "wdqt_s", [128, 32 * C], BF16)
        WRH = [sb(es_w, f"wrhs{i}", [128, 512], BF16) for i in range(4)]
        ps, psb = [], []
        for i in range(8):
            ps.append(es_w.enter_context(
                nc.psum_tensor(f"pw{i}", [128, 512], F32)))
            psb.append(Buf())

        for kb in range(32):
            P.dma("pool", lambda e, d=wdqt_s[:, kb * C:(kb + 1) * C],
                  s=wdqt_h[kb * 128:(kb + 1) * 128, :]:
                  e.dma_start(out=d, in_=s),
                  reads=[DB["wdqt"]], writes=[wdqt_sb])

        passes = [(wuq_h, DB["wuq"], 0, "weff"),
                  (wuq_h, DB["wuq"], 512, "weff"),
                  (wqr_h, DB["wqr"], 0, "wqre")]
        for rhs_h, rhs_db, off, dname in passes:
            for kb in range(32):
                wr, wr_b = WRH[kb % 4]
                P.dma("pool", lambda e, d=wr[:, :],
                      s=rhs_h[kb * 128:(kb + 1) * 128, off:off + 512]:
                      e.dma_start(out=d, in_=s),
                      reads=[rhs_db], writes=[wr_b])
                for mi in range(8):
                    P.ins("pe", lambda e, o=ps[mi][:, 0:512],
                          a=wdqt_s[:, kb * C + mi * 128:kb * C + (mi + 1) * 128],
                          b=wr[:, :],
                          st=(kb == 0), sp=(kb == 31):
                          e.matmul(o, a, b, start=st, stop=sp),
                          reads=[wdqt_sb, wr_b], writes=[psb[mi]])
            for mi in range(8):
                if dname == "weff":
                    d = weff[:, mi * C + off:mi * C + off + 512]
                    dbf = weff_b
                else:
                    d = wqre[:, mi * 512:(mi + 1) * 512]
                    dbf = wqre_b
                P.ins("act", lambda e, dd=d, s=ps[mi][:, 0:512]:
                      e.activation(dd, s, AF.Identity),
                      reads=[psb[mi]], writes=[dbf])
        es_w.close()
        P.barrier()

        es_a2 = contextlib.ExitStack()
        SQB = [sb(es_a2, f"sqb{i}", [128, 512], BF16) for i in range(12)]
        SQK = [sb(es_a2, f"sqk{i}", [128, 512], BF16) for i in range(9)]
        LNX = [sb(es_a2, f"lnx{i}", [128, 8 * TAU], BF16) for i in range(2)]
        XT = [sb(es_a2, f"xta{i}", [128, C], F32) for i in range(4)]
        SQA = [sb(es_a2, f"sqa{i}", [128, C], F32) for i in range(4)]
        LNT = [sb(es_a2, f"lnt{i}", [128, C], BF16) for i in range(4)]

        ps, psb = [], []
        for i in range(6):
            ps.append(es_top.enter_context(
                nc.psum_tensor(f"p{i}", [128, 512], F32)))
            psb.append(Buf())
        for i in (6, 7):
            ps.append(es_top.enter_context(
                nc.psum_tensor(f"p{i}", [128, 1024], BF16)))
            psb.append(Buf())

        P.dma("pool", lambda e: e.dma_start(out=sc4s[:, :], in_=sc4_h[:, :]),
              reads=[DB["sc4"]], writes=[sc4s_b])
        P.dma("pool", lambda e: e.dma_start(
            out=ematb[:, :].rearrange("p (k j) -> p k j", j=16),
            in_=emat_h.rearrange("k p j -> p k j")),
            reads=[DB["ematc"]], writes=[ematb_b])
        P.dma("pool", lambda e: e.dma_start(
            out=wdkv_s[:, :].rearrange("p (c j) -> p c j", j=256),
            in_=wdkv_h.rearrange("(c p) j -> p c j", p=128)),
            reads=[DB["wdkv"]], writes=[wdkv_sb])
        P.dma("pool", lambda e: e.dma_start(
            out=wuk_s[:, :].rearrange("p (c j) -> p c j", j=C),
            in_=wuk_h.rearrange("(c p) j -> p c j", p=128)),
            reads=[DB["wuk"]], writes=[wuk_sb])
        P.dma("pool", lambda e: e.dma_start(
            out=wuv_s[:, :].rearrange("p (c j) -> p c j", j=C),
            in_=wuv_h.rearrange("(c p) j -> p c j", p=128)),
            reads=[DB["wuv"]], writes=[wuv_sb])
        P.dma("pool", lambda e: e.dma_start(
            out=wkr_s[:, :].rearrange("p (c j) -> p c j", j=32),
            in_=wkr_h.rearrange("(c p) j -> p c j", p=128)),
            reads=[DB["wkr"]], writes=[wkr_sb])

        def emit_ln1_chain(tau):
            items = []
            for tt in range(4):
                tg = tau * 4 + tt
                xv, xv_b = XT[tt]
                sqv_t, sqv_tb = SQA[tt]
                lnt, lnt_b = LNT[tt]
                P.dma("sp", lambda e, d=xv[:, :],
                      s=x_h[tg * 128:(tg + 1) * 128, :]:
                      e.dma_start(out=d, in_=s),
                      reads=[DB["x"]], writes=[xv_b])
                items.append((xv[:, :], xv_b, lnt[:, :], lnt_b, sqv_t, sqv_tb,
                              STATS[tt]))
            layer_norm_group(items)

        def emit_ln1_transposes(tau):
            lnxv, lnx_b = LNX[tau % 2]
            for tt in range(4):
                lnt, lnt_b = LNT[tt]
                for g in range(2):
                    pt, pt_b = ps[6 + g], psb[6 + g]
                    for c4 in range(4):
                        c = g * 4 + c4
                        P.ins("pe", lambda e, o=pt[:, c4 * 128:(c4 + 1) * 128],
                              i=lnt[:, c * 128:(c + 1) * 128]:
                              e.transpose(o, i, identb[:, :]),
                              reads=[lnt_b, identb_b], writes=[pt_b])
                    src = pt[:, 0:512].rearrange("p (c j) -> p c j", j=128)
                    dst = lnxv[:, :].rearrange("p (c j) -> p c j", j=TAU)[
                        :, g * 4:(g + 1) * 4, tt * 128:(tt + 1) * 128]
                    P.ins("dve", lambda e, d=dst, s=src: e.tensor_copy(d, s),
                          reads=[pt_b], writes=[lnx_b])

        emit_ln1_chain(0)
        emit_ln1_transposes(0)
        for tau in range(NTAU):
            lnxv, lnx_b = LNX[tau % 2]
            t0 = tau * TAU
            if tau + 1 < NTAU:
                emit_ln1_chain(tau + 1)

            qnorm = []
            knorm = []
            for m in range(8):
                pm, pm_b = ps[m % 2], psb[m % 2]
                for c_ in range(8):
                    P.ins("pe", lambda e, o=pm[:, 0:512],
                          a=weff[:, c_ * C + m * 128:c_ * C + (m + 1) * 128],
                          b=lnxv[:, c_ * TAU:(c_ + 1) * TAU],
                          st=(c_ == 0), sp=(c_ == 7):
                          e.matmul(o, a, b, start=st, stop=sp),
                          reads=[weff_b, lnx_b], writes=[pm_b])
                stv, stv_b = STG[m % 4]
                P.ins("act", lambda e, d=stv[:, :], s=pm[:, 0:512]:
                      e.activation(d, s, AF.Identity, scale=INV_SQ),
                      reads=[pm_b], writes=[stv_b])
                dst = q_scr[2 * m:2 * m + 2, 0:64, t0:t0 + TAU]
                P.dma("sp", lambda e, d=dst, s=stv[:, :]: e.dma_start(out=d, in_=s),
                      reads=[stv_b], writes=[DB["q_scr"]])
                sqv, sqv_b = SQB[m]
                P.ins("act", lambda e, d=sqv[:, :], s=stv[:, :]:
                      e.activation(d, s, AF.Square),
                      reads=[stv_b], writes=[sqv_b])
                qnorm.append((ematb[:, m * 16:(m + 1) * 16], sqv[:, :], sqv_b,
                              128))

            for m in range(4):
                pm, pm_b = ps[m % 2], psb[m % 2]
                for c_ in range(8):
                    P.ins("pe", lambda e, o=pm[:, 0:512],
                          a=wqre[:, c_ * 512 + m * 128:c_ * 512 + (m + 1) * 128],
                          b=lnxv[:, c_ * TAU:(c_ + 1) * TAU],
                          st=(c_ == 0), sp=(c_ == 7):
                          e.matmul(o, a, b, start=st, stop=sp),
                          reads=[wqre_b, lnx_b], writes=[pm_b])
                stv, stv_b = STG[m % 4]
                P.ins("dve", lambda e, d=stv[:, :], s=pm[:, 0:512],
                      sc=sc4s[:, t0:t0 + TAU]: e.tensor_mul(d, s, sc),
                      reads=[pm_b, sc4s_b], writes=[stv_b])
                dst = q_scr[4 * m:4 * m + 4, 64:96, t0:t0 + TAU]
                P.dma("sp", lambda e, d=dst, s=stv[:, :]: e.dma_start(out=d, in_=s),
                      reads=[stv_b], writes=[DB["q_scr"]])
                sqv, sqv_b = SQB[8 + m]
                P.ins("act", lambda e, d=sqv[:, :], s=stv[:, :]:
                      e.activation(d, s, AF.Square),
                      reads=[stv_b], writes=[sqv_b])
                qnorm.append((ematb[:, (8 + m) * 16:(9 + m) * 16], sqv[:, :],
                              sqv_b, 128))

            for mc in range(2):
                for c_ in range(8):
                    P.ins("pe", lambda e, o=ps[4][:, 0:512],
                          a=wdkv_s[:, c_ * 256 + mc * 128:c_ * 256 + (mc + 1) * 128],
                          b=lnxv[:, c_ * TAU:(c_ + 1) * TAU],
                          st=(c_ == 0), sp=(c_ == 7):
                          e.matmul(o, a, b, start=st, stop=sp),
                          reads=[wdkv_sb, lnx_b], writes=[psb[4]])
                P.ins("act", lambda e, d=ckv[:, mc * 512:(mc + 1) * 512],
                      s=ps[4][:, 0:512]: e.activation(d, s, AF.Identity),
                      reads=[psb[4]], writes=[ckv_b])

            for m in range(8):
                pm, pm_b = ps[m % 2], psb[m % 2]
                for c2 in range(2):
                    P.ins("pe", lambda e, o=pm[:, 0:512],
                          a=wuk_s[:, c2 * C + m * 128:c2 * C + (m + 1) * 128],
                          b=ckv[:, c2 * 512:(c2 + 1) * 512],
                          st=(c2 == 0), sp=(c2 == 1):
                          e.matmul(o, a, b, start=st, stop=sp),
                          reads=[wuk_sb, ckv_b], writes=[pm_b])
                stv, stv_b = STG[m % 4]
                P.ins("act", lambda e, d=stv[:, :], s=pm[:, 0:512]:
                      e.activation(d, s, AF.Identity),
                      reads=[pm_b], writes=[stv_b])
                dst = k_scr[2 * m:2 * m + 2, :, t0:t0 + TAU]
                P.dma("sp", lambda e, d=dst, s=stv[:, :]: e.dma_start(out=d, in_=s),
                      reads=[stv_b], writes=[DB["k_scr"]])
                sqv, sqv_b = SQK[m]
                P.ins("act", lambda e, d=sqv[:, :], s=stv[:, :]:
                      e.activation(d, s, AF.Square),
                      reads=[stv_b], writes=[sqv_b])
                knorm.append((ematb[:, m * 16:(m + 1) * 16], sqv[:, :],
                              sqv_b, 128))

            for c_ in range(8):
                P.ins("pe", lambda e, o=ps[4][0:32, 0:512],
                      a=wkr_s[:, c_ * 32:(c_ + 1) * 32],
                      b=lnxv[:, c_ * TAU:(c_ + 1) * TAU],
                      st=(c_ == 0), sp=(c_ == 7):
                      e.matmul(o, a, b, start=st, stop=sp),
                      reads=[wkr_sb, lnx_b], writes=[psb[4]])
            P.ins("dve", lambda e, d=kr_s[:, t0:t0 + TAU],
                  s=ps[4][0:32, 0:512], sc=sc4s[0:32, t0:t0 + TAU]:
                  e.scalar_tensor_tensor(d, s, SQ96, sc, ALU.mult, ALU.mult),
                  reads=[psb[4], sc4s_b], writes=[kr_b])
            sqv, sqv_b = SQK[8]
            P.ins("act", lambda e, d=sqv[0:32, :],
                  s=kr_s[0:32, t0:t0 + TAU]: e.activation(d, s, AF.Square),
                  reads=[kr_b], writes=[sqv_b])
            knorm.append((ematb[0:32, 12 * 16:13 * 16], sqv[0:32, :],
                          sqv_b, 32))

            for m in range(8):
                for c2 in range(2):
                    P.ins("pe", lambda e, o=ps[5][:, 0:512],
                          a=wuv_s[:, c2 * C + m * 128:c2 * C + (m + 1) * 128],
                          b=ckv[:, c2 * 512:(c2 + 1) * 512],
                          st=(c2 == 0), sp=(c2 == 1):
                          e.matmul(o, a, b, start=st, stop=sp),
                          reads=[wuv_sb, ckv_b], writes=[psb[5]])
                stv, stv_b = STG[m % 4]
                P.ins("act", lambda e, d=stv[:, :], s=ps[5][:, 0:512]:
                      e.activation(d, s, AF.Identity),
                      reads=[psb[5]], writes=[stv_b])
                dst = v_scr[2 * m:2 * m + 2, :, t0:t0 + TAU]
                P.dma("sp", lambda e, d=dst, s=stv[:, :]: e.dma_start(out=d, in_=s),
                      reads=[stv_b], writes=[DB["v_scr"]])

            for i, (ea, sqa_, sqb_, kdim) in enumerate(qnorm):
                P.ins("pe", lambda e, a=ea, bq=sqa_,
                      st=(i == 0), sp=(i == len(qnorm) - 1):
                      e.matmul(ps[2][0:16, 0:512], a, bq, start=st, stop=sp),
                      reads=[ematb_b, sqb_], writes=[psb[2]])
            P.ins("dve", lambda e, d=qn2[:, t0:t0 + TAU],
                  s=ps[2][0:16, 0:512]: e.tensor_copy(d, s),
                  reads=[psb[2]], writes=[qn2_b])
            for i, (ea, sqa_, sqb_, kdim) in enumerate(knorm):
                P.ins("pe", lambda e, a=ea, bq=sqa_,
                      st=(i == 0), sp=(i == len(knorm) - 1):
                      e.matmul(ps[3][0:16, 0:512], a, bq, start=st, stop=sp),
                      reads=[ematb_b, sqb_], writes=[psb[3]])
            P.ins("dve", lambda e, d=kn2[:, t0:t0 + TAU],
                  s=ps[3][0:16, 0:512]: e.tensor_copy(d, s),
                  reads=[psb[3]], writes=[kn2_b])

            if tau + 1 < NTAU:
                emit_ln1_transposes(tau + 1)

        kst, kst_b = STATS[2]
        kmax = kst[0:16, 6:7]
        P.ins("dve", lambda e: e.reduce_max(kmax, kn2[:, :], axis=AX.X),
              reads=[kn2_b], writes=[kst_b])
        P.ins("act", lambda e: e.activation(kmax, kmax, AF.Sqrt),
              reads=[kst_b], writes=[kst_b])
        P.ins("act", lambda e: e.activation(bound[:, :], qn2[:, :], AF.Sqrt),
              reads=[qn2_b], writes=[bound_b])
        P.ins("dve", lambda e: e.tensor_scalar_mul(bound[:, :], bound[:, :], kmax),
              reads=[bound_b, kst_b], writes=[bound_b])

        es_a2.close()
        es_a.close()
        es_weff.close()
        P.barrier()
        if stop_after == "A":
            es_bk1.close()
            P.retire()
            P.replay(nc)
            return nc

        es_bk2 = contextlib.ExitStack()
        o_all, o_all_b = sb(es_bk2, "o_all", [128, 8 * T], BF16)

        es_bs = contextlib.ExitStack()
        QBUF = [sb(es_bs, f"qbuf{i}", [128, T], BF16) for i in range(2)]
        KBUF = [sb(es_bs, f"kbuf{i}", [128, T], BF16) for i in range(2)]
        VFM = [sb(es_bs, f"vfm{i}", [64, T], BF16) for i in range(2)]
        VA = [sb(es_bs, f"va{i}", [128, 32 * 65], BF16) for i in range(2)]
        PTB = [sb(es_bs, f"ptb{i}", [128, 512], BF16) for i in range(4)]
        maskb, maskb_b = sb(es_bs, "maskb", [128, 4 * 512], BF16)
        tsc, tsc_b = sb(es_bs, "tsc", [128, 8 * C], BF16)
        RB64 = [sb(es_bs, f"rb64_{i}", [64, 512], F32) for i in range(2)]
        RROW = [sb(es_bs, f"rrow{i}", [1, 512], F32) for i in range(2)]
        RH = [sb(es_bs, f"rh{i}", [1, 512], BF16) for i in range(2)]
        RL = [sb(es_bs, f"rl{i}", [1, 512], BF16) for i in range(2)]
        RLF = [sb(es_bs, f"rlf{i}", [1, 512], F32) for i in range(2)]
        onesbf, onesbf_b = sb(es_bs, "onesbf", [1, 128], BF16)
        P.ins("dve", lambda e: e.memset(onesbf[:, :], 1.0),
              writes=[onesbf_b])

        P.dma("pool", lambda e: e.dma_start(
            out=maskb[:, :].rearrange("p (m j) -> p m j", j=512),
            in_=mask_h.rearrange("m p j -> p m j")),
            reads=[DB["maskc"]], writes=[maskb_b])

        for sl in range(2):
            qb_, qb_b = QBUF[sl]
            kb_, kb_b = KBUF[sl]
            va_, va_b = VA[sl]
            P.ins("dve", lambda e, d=qb_[64:128, :]: e.memset(d, 0.0),
                  writes=[qb_b])
            P.ins("dve", lambda e, d=kb_[64:128, :]: e.memset(d, 0.0),
                  writes=[kb_b])
            P.ins("dve", lambda e, d=va_[:, :]: e.memset(d, 1.0), writes=[va_b])
            P.dma("sp", lambda e, d=kb_[64:96, :]:
                  e.dma_start(out=d, in_=kr_s[:, :]),
                  reads=[kr_b], writes=[kb_b])
            P.dma("sp", lambda e, d=kb_[96:97, :]:
                  e.dma_start(out=d, in_=neg1b[:, :]),
                  reads=[neg1b_b], writes=[kb_b])

        _tlist = [(wh, db, cache, cdb, e)
                  for (wh, db, cache, cdb) in
                  ((w1_h, DB["w1"], w1b_scr, DB["w1b"]),
                   (w3_h, DB["w3"], w3b_scr, DB["w3b"]),
                   (w2_h, DB["w2"], w2b_scr, DB["w2b"]))
                  for e in range(NE)]

        def emit_transcode(idx):
            if idx >= len(_tlist):
                return
            wh, db, cache, cdb, e_idx = _tlist[idx]
            dst, dst_b = tsc, tsc_b
            for c_ in range(8):
                P.dma("pool", lambda e, d=dst[:, c_ * C:(c_ + 1) * C],
                      s=wh[e_idx, c_ * 128:(c_ + 1) * 128, :]:
                      e.dma_start(out=d, in_=s),
                      reads=[db], writes=[dst_b])
            for c_ in range(8):
                P.dma("sp", lambda e,
                      d=cache[e_idx, c_ * 128:(c_ + 1) * 128, :],
                      s=dst[:, c_ * C:(c_ + 1) * C]:
                      e.dma_start(out=d, in_=s),
                      reads=[dst_b], writes=[cdb])

        def emit_head_loads(h):
            sl = h % 2
            qb_, qb_b = QBUF[sl]
            kb_, kb_b = KBUF[sl]
            vf_, vf_b = VFM[sl]
            P.dma("sp", lambda e, d=qb_[0:96, :], s=q_scr[h, :, :]:
                  e.dma_start(out=d, in_=s),
                  reads=[DB["q_scr"]], writes=[qb_b])
            P.dma("pool", lambda e, d=qb_[96:97, :], s=bound[h:h + 1, :]:
                  e.dma_start(out=d, in_=s),
                  reads=[bound_b], writes=[qb_b])
            P.dma("sp", lambda e, d=kb_[0:64, :], s=k_scr[h, :, :]:
                  e.dma_start(out=d, in_=s),
                  reads=[DB["k_scr"]], writes=[kb_b])
            P.dma("sp", lambda e, d=vf_[:, :], s=v_scr[h, :, :]:
                  e.dma_start(out=d, in_=s),
                  reads=[DB["v_scr"]], writes=[vf_b])

        def emit_vtrans(h):
            sl = h % 2
            vf_, vf_b = VFM[sl]
            va_, va_b = VA[sl]
            for g in range(4):
                pt, pt_b = ps[6 + (g % 2)], psb[6 + (g % 2)]
                for j in range(8):
                    kt = g * 8 + j
                    P.ins("pe", lambda e, o=pt[:, j * 64:(j + 1) * 64],
                          i=vf_[:, kt * 128:(kt + 1) * 128]:
                          e.transpose(o, i, identb[0:64, 0:64]),
                          reads=[vf_b, identb_b], writes=[pt_b])
                src = pt[:, 0:512].rearrange("p (t c) -> p t c", c=64)
                dst = va_[:, :].rearrange("p (t c) -> p t c", c=65)[
                    :, g * 8:(g + 1) * 8, 0:64]
                P.ins("dve", lambda e, d=dst, s=src: e.tensor_copy(d, s),
                      reads=[pt_b], writes=[va_b])

        pend_norm = [None]
        emit_head_loads(0)
        emit_vtrans(0)
        for h in range(H):
            sl = h % 2
            qb_, qb_b = QBUF[sl]
            kb_, kb_b = KBUF[sl]
            va_, va_b = VA[sl]
            emit_transcode(2 * h)

            for qi in range(NQB):
                if qi == 4:
                    emit_transcode(2 * h + 1)
                    if h + 1 < H:
                        emit_head_loads(h + 1)
                if qi == 6 and h + 1 < H:
                    emit_vtrans(h + 1)
                po_i = 2 + (qi % 2)
                nkt = 4 * qi + 4

                def emit_ST(kt, qi2=None, h2=None, qi=qi, h=h):
                    if qi2 is None:
                        qi2 = qi
                    if h2 is None:
                        h2 = h
                    qb2, qb2_b = QBUF[h2 % 2]
                    kb2, kb2_b = KBUF[h2 % 2]
                    si = (0, 1, 4, 5)[kt % 4]
                    pS, pS_b = ps[si], psb[si]
                    diag = kt >= 4 * qi2
                    P.ins("pe", lambda e, o=pS[:, 0:512],
                          a=kb2[:, kt * 128:(kt + 1) * 128],
                          b=qb2[:, qi2 * 512:(qi2 + 1) * 512], sp=not diag:
                          e.matmul(o, a, b, start=True, stop=sp),
                          reads=[kb2_b, qb2_b], writes=[pS_b])
                    if diag:
                        mvar = kt - 4 * qi2
                        P.ins("pe", lambda e, o=pS[:, 0:512],
                              mk=maskb[:, mvar * 512:(mvar + 1) * 512]:
                              e.matmul(o, identb[:, :], mk,
                                       start=False, stop=True),
                              reads=[identb_b, maskb_b], writes=[pS_b])

                if h == 0 and qi == 0:
                    for kt in range(4):
                        emit_ST(kt)
                for kt in range(nkt):
                    if pend_norm[0] is not None:
                        if kt == 1:
                            pend_norm[0][0]()
                        if kt == min(max(4, nkt - 4), nkt - 1):
                            pend_norm[0][1]()
                            pend_norm[0] = None
                    si = (0, 1, 4, 5)[kt % 4]
                    pS, pS_b = ps[si], psb[si]
                    ptv, ptv_b = PTB[kt % 4]
                    P.ins("act", lambda e, d=ptv[:, :], s=pS[:, 0:512]:
                          e.activation(d, s, AF.Exp),
                          reads=[pS_b], writes=[ptv_b])
                    P.ins("pe", lambda e, o=ps[po_i][0:65, 0:512],
                          a=va_[:, kt * 65:(kt + 1) * 65], b=ptv[:, :],
                          st=(kt == 0), sp=(kt == nkt - 1):
                          e.matmul(o, a, b, start=st, stop=sp),
                          reads=[va_b, ptv_b], writes=[psb[po_i]])
                    nx4 = kt + 4
                    if nx4 < nkt:
                        emit_ST(nx4)
                    elif qi + 1 < NQB:
                        emit_ST(nx4 - nkt, qi2=qi + 1)
                    elif h + 1 < H:
                        emit_ST(nx4 - nkt, qi2=0, h2=h + 1)

                def mk_norm(qi=qi, po_i=po_i, h=h):
                    rrow, rrow_b = RROW[qi % 2]
                    rh, rh_b = RH[qi % 2]
                    rl, rl_b = RL[qi % 2]
                    rlf, rlf_b = RLF[qi % 2]
                    rb64, rb64_b = RB64[qi % 2]

                    def norm_a():
                        P.ins("dve", lambda e, d=rrow[:, :], o=ps[po_i]:
                              e.reciprocal(d, o[64:65, 0:512]),
                              reads=[psb[po_i]], writes=[rrow_b])
                        P.ins("dve", lambda e: e.tensor_copy(rh[:, :],
                                                             rrow[:, :]),
                              reads=[rrow_b], writes=[rh_b])
                        P.ins("dve", lambda e: e.scalar_tensor_tensor(
                            rlf[:, :], rh[:, :], -1.0, rrow[:, :],
                            ALU.mult, ALU.add),
                            reads=[rh_b, rrow_b], writes=[rlf_b])
                        P.ins("dve", lambda e: e.tensor_copy(rl[:, :],
                                                             rlf[:, :]),
                              reads=[rlf_b], writes=[rl_b])

                    def norm_b():
                        P.ins("pe", lambda e, o=ps[po_i][64:128, 0:512],
                              r=rh[0:1, :]:
                              e.matmul(o, onesbf[0:1, 0:64], r,
                                       start=True, stop=False),
                              reads=[onesbf_b, rh_b], writes=[psb[po_i]])
                        P.ins("pe", lambda e, o=ps[po_i][64:128, 0:512],
                              r=rl[0:1, :]:
                              e.matmul(o, onesbf[0:1, 0:64], r,
                                       start=False, stop=True),
                              reads=[onesbf_b, rl_b], writes=[psb[po_i]])
                        P.ins("dve", lambda e, s=ps[po_i][64:128, 0:512],
                              d=rb64[:, :]: e.tensor_copy(d, s),
                              reads=[psb[po_i]], writes=[rb64_b])
                        dsto = o_all[64 * (h % 2):64 * (h % 2) + 64,
                                     (h // 2) * T + qi * 512:
                                     (h // 2) * T + (qi + 1) * 512]
                        P.ins("dve", lambda e, d=dsto, o=ps[po_i],
                              r=rb64[:, :]:
                              e.tensor_mul(d, o[0:64, 0:512], r),
                              reads=[psb[po_i], rb64_b], writes=[o_all_b])
                    return norm_a, norm_b

                pend_norm[0] = mk_norm()

        if pend_norm[0] is not None:
            pend_norm[0][0]()
            pend_norm[0][1]()
            pend_norm[0] = None
        es_bs.close()
        P.barrier()
        if stop_after == "B":
            es_bk2.close()
            es_bk1.close()
            P.retire()
            P.replay(nc)
            return nc

        es_c = contextlib.ExitStack()
        wo_s, wo_sb = sb(es_c, "wo_s", [128, 8 * C], BF16)
        gwt_s, gwt_sb = sb(es_c, "gwt_s", [128, 8 * 8], BF16)
        AO = [sb(es_c, f"ao{i}", [128, C], F32) for i in range(4)]
        SQC = [sb(es_c, f"sqc{i}", [128, C], F32) for i in range(4)]
        XTC = [sb(es_c, f"xtc{i}", [128, C], F32) for i in range(4)]
        LN1T = [sb(es_c, f"ln1t{i}", [128, C], BF16) for i in range(4)]
        ln1s, ln1s_b = sb(es_c, "ln1s", [128, 8 * 1024], BF16)
        gs_ln2 = make_gstat(es_c, "gsln2")
        gs_ln3 = make_gstat(es_c, "gsln3")
        gtsE, gtsE_b = sb(es_c, "gtsE", [128, 32], F32)
        gwb, gwb_b = sb(es_c, "gwb", [128, 32], F32)
        g4, _g4unused = sb(es_c, "g4", [128, 16], F32)
        g4s_b, g4r_b, g4m_b, g4m2_b = Buf(), Buf(), Buf(), Buf()
        gstg, gstg_b = sb(es_c, "gstg", [32, 128], F32)

        P.dma("pool", lambda e: e.dma_start(
            out=wo_s[:, :].rearrange("p (c j) -> p c j", j=C),
            in_=wo_h.rearrange("(c p) j -> p c j", p=128)),
            reads=[DB["wo"]], writes=[wo_sb])
        P.dma("pool", lambda e: e.dma_start(
            out=gwt_s[:, :].rearrange("p (c j) -> p c j", j=8),
            in_=gwt_h.rearrange("(c p) j -> p c j", p=128)),
            reads=[DB["gwt"]], writes=[gwt_sb])

        def emit_wo(t):
            ao, ao_b = AO[t % 4]
            for half in range(2):
                ph, ph_b = ps[half], psb[half]
                for c_ in range(8):
                    P.ins("pe", lambda e, o=ph[:, 0:512],
                          a=o_all[:, c_ * T + t * 128:c_ * T + (t + 1) * 128],
                          b=wo_s[:, c_ * C + half * 512:c_ * C + half * 512 + 512],
                          st=(c_ == 0), sp=(c_ == 7):
                          e.matmul(o, a, b, start=st, stop=sp),
                          reads=[o_all_b, wo_sb], writes=[ph_b])
                P.ins("act", lambda e, d=ao[:, half * 512:half * 512 + 512],
                      s=ph[:, 0:512]: e.activation(d, s, AF.Identity),
                      reads=[ph_b], writes=[ao_b])

        def emit_xload(t):
            xv, xv_b = XTC[t % 4]
            P.dma("sp", lambda e, d=xv[:, :], s=x_h[t * 128:(t + 1) * 128, :]:
                  e.dma_start(out=d, in_=s),
                  reads=[DB["x"]], writes=[xv_b])

        for t in range(4):
            emit_wo(t)
            emit_xload(t)
        for grp in range(NT // 4):
            ts4 = list(range(grp * 4, grp * 4 + 4))
            layer_norm_group_v2([
                (AO[t % 4][0][:, :], AO[t % 4][1], AO[t % 4][0][:, :],
                 AO[t % 4][1], SQC[t % 4][0], SQC[t % 4][1])
                for t in ts4], gs_ln2)
            for t in ts4:
                xv, xv_b = XTC[t % 4]
                P.ins("dve", lambda e, d=xv[:, :],
                      l=AO[t % 4][0][:, :]: e.tensor_add(d, d, l),
                      reads=[xv_b, AO[t % 4][1]], writes=[xv_b])
            if grp + 1 < NT // 4:
                for t in range(grp * 4 + 4, grp * 4 + 8):
                    emit_wo(t)
            for t in ts4:
                xv, xv_b = XTC[t % 4]
                P.dma("sp", lambda e, s=xv[:, :],
                      d=x1_scr[t * 128:(t + 1) * 128, :]:
                      e.dma_start(out=d, in_=s),
                      reads=[xv_b], writes=[DB["x1_scr"]])
            layer_norm_group_v2([
                (XTC[t % 4][0][:, :], XTC[t % 4][1], LN1T[t % 4][0][:, :],
                 LN1T[t % 4][1], SQC[t % 4][0], SQC[t % 4][1])
                for t in ts4], gs_ln3)
            if grp + 1 < NT // 4:
                for t in range(grp * 4 + 4, grp * 4 + 8):
                    emit_xload(t)
            for t in ts4:
                ln1t, ln1t_b = LN1T[t % 4]
                for g in range(2):
                    pt, pt_b = ps[6 + g], psb[6 + g]
                    for c4 in range(4):
                        c = g * 4 + c4
                        P.ins("pe", lambda e, o=pt[:, c4 * 128:(c4 + 1) * 128],
                              i=ln1t[:, c * 128:(c + 1) * 128]:
                              e.transpose(o, i, identb[:, :]),
                              reads=[ln1t_b, identb_b], writes=[pt_b])
                    src = pt[:, 0:512].rearrange("p (c j) -> p c j", j=128)
                    dst = ln1s[:, :].rearrange("p (c j) -> p c j", j=1024)[
                        :, g * 4:(g + 1) * 4, (t % 8) * 128:(t % 8) * 128 + 128]
                    P.ins("dve", lambda e, d=dst, s=src: e.tensor_copy(d, s),
                          reads=[pt_b], writes=[ln1s_b])
            for t in ts4:
                for c_ in range(8):
                    P.ins("pe", lambda e, o=ps[2][:, (t % 4) * 8:
                                                  (t % 4) * 8 + 8],
                          a=ln1s[:, c_ * 1024 + (t % 8) * 128:
                                 c_ * 1024 + (t % 8) * 128 + 128],
                          b=gwt_s[:, c_ * 8:(c_ + 1) * 8],
                          st=(c_ == 0), sp=(c_ == 7):
                          e.matmul(o, a, b, start=st, stop=sp),
                          reads=[ln1s_b, gwt_sb], writes=[psb[2]])
            g3 = gtsE[:, :].rearrange("p (g j) -> p g j", j=8)
            w3_ = gwb[:, :].rearrange("p (g j) -> p g j", j=8)
            g4s = g4[:, 0:4].rearrange("p (g j) -> p g j", j=1)
            g4r = g4[:, 4:8].rearrange("p (g j) -> p g j", j=1)
            g4m = g4[:, 8:12].rearrange("p (g j) -> p g j", j=1)
            g4m2 = g4[:, 12:16].rearrange("p (g j) -> p g j", j=1)
            P.ins("act", lambda e: e.activation(gtsE[:, :], ps[2][:, 0:32],
                                                AF.Exp),
                  reads=[psb[2]], writes=[gtsE_b])
            P.ins("dve", lambda e: e.tensor_reduce(g4s, g3, axis=AX.X,
                                                   op=ALU.add),
                  reads=[gtsE_b], writes=[g4s_b])
            P.ins("dve", lambda e: e.reciprocal(g4[:, 4:8], g4[:, 0:4]),
                  reads=[g4s_b], writes=[g4r_b])
            P.ins("dve", lambda e: e.tensor_reduce(g4m, g3, axis=AX.X,
                                                   op=ALU.max),
                  reads=[gtsE_b], writes=[g4m_b])
            P.ins("dve", lambda e, ab=broadcast_tensor_aps(g3, g4m):
                  e.tensor_tensor(w3_, ab[0], ab[1], ALU.is_ge),
                  reads=[gtsE_b, g4m_b], writes=[gwb_b])
            P.ins("dve", lambda e: e.scalar_tensor_tensor(
                gwb[:, :], gwb[:, :], -1e30, gtsE[:, :], ALU.mult, ALU.add),
                reads=[gwb_b, gtsE_b], writes=[gwb_b])
            P.ins("dve", lambda e: e.tensor_reduce(g4m2, w3_, axis=AX.X,
                                                   op=ALU.max),
                  reads=[gwb_b], writes=[g4m2_b])
            P.ins("dve", lambda e, ab=broadcast_tensor_aps(g3, g4m2):
                  e.tensor_tensor(w3_, ab[0], ab[1], ALU.is_ge),
                  reads=[gtsE_b, g4m2_b], writes=[gwb_b])
            P.ins("dve", lambda e: e.tensor_mul(gwb[:, :], gwb[:, :],
                                                gtsE[:, :]),
                  reads=[gwb_b, gtsE_b], writes=[gwb_b])
            P.ins("dve", lambda e, ab=broadcast_tensor_aps(w3_, g4r):
                  e.tensor_tensor(w3_, ab[0], ab[1], ALU.mult),
                  reads=[gwb_b, g4r_b], writes=[gwb_b])
            P.ins("pe", lambda e: e.transpose(ps[5][0:32, 0:128], gwb[:, :],
                                              identf[:, :]),
                  reads=[gwb_b, identf_b], writes=[psb[5]])
            P.ins("dve", lambda e: e.tensor_copy(gstg[:, :],
                                                 ps[5][0:32, 0:128]),
                  reads=[psb[5]], writes=[gstg_b])
            P.dma("sp", lambda e,
                  d=g_scr[:, grp * 512:grp * 512 + 512].rearrange(
                      "e (t j) -> t e j", j=128):
                  e.dma_start(out=d, in_=gstg[:, :]),
                  reads=[gstg_b], writes=[DB["g_scr"]])
            if grp % 2 == 1:
                for c_ in range(8):
                    P.dma("sp", lambda e,
                          d=ln1_scr[c_, :, (grp // 2) * 1024:
                                    (grp // 2) * 1024 + 1024],
                          s=ln1s[:, c_ * 1024:(c_ + 1) * 1024]:
                          e.dma_start(out=d, in_=s),
                          reads=[ln1s_b], writes=[DB["ln1_scr"]])

        es_c.close()
        es_bk2.close()
        es_bk1.close()
        P.barrier()
        if stop_after == "C":
            P.retire()
            P.replay(nc)
            return nc

        es_d = contextlib.ExitStack()
        LN1F = [sb(es_d, f"ln1f{i}", [128, 8 * FTAU], BF16) for i in range(2)]
        XE = [sb(es_d, f"xe{i}", [128, 8 * FTAU], BF16) for i in range(2)]
        hbuf, hbuf_b = sb(es_d, "hbuf", [128, 8 * FTAU], BF16)
        w1s_, w1s_b = sb(es_d, "w1s", [128, 8 * C], BF16)
        w3s_, w3s_b = sb(es_d, "w3s", [128, 8 * C], BF16)
        w2s_, w2s_b = sb(es_d, "w2s", [128, 8 * C], BF16)
        oacc, oacc_b = sb(es_d, "oacc", [128, 8 * FTAU], F32)
        GROW = [sb(es_d, f"grow{i}", [1, FTAU], BF16) for i in range(2)]
        GSB = [sb(es_d, f"gsb{i}", [128, FTAU], BF16) for i in range(2)]
        onesb, onesb_b = sb(es_d, "onesb", [1, 128], BF16)
        P.ins("dve", lambda e: e.memset(onesb[:, :], 1.0), writes=[onesb_b])
        TSIL = [sb(es_d, f"tsil{i}", [128, 512], BF16) for i in range(2)]
        X1D = [sb(es_d, f"x1d{i}", [128, C], F32) for i in range(4)]
        SQD = [sb(es_d, f"sqd{i}", [128, C], F32) for i in range(4)]

        def load_w(e_idx, ft, wh, db, cache, cache_db, dst, dst_b):
            for c_ in range(8):
                P.dma("sp", lambda e, d=dst[:, c_ * C:(c_ + 1) * C],
                      s=cache[e_idx, c_ * 128:(c_ + 1) * 128, :]:
                      e.dma_start(out=d, in_=s),
                      reads=[cache_db], writes=[dst_b])

        def load_ln1f(ft):
            ln1f, ln1f_b = LN1F[ft % 2]
            t0 = ft * FTAU
            for c_ in range(8):
                P.dma("sp", lambda e, d=ln1f[:, c_ * FTAU:(c_ + 1) * FTAU],
                      s=ln1_scr[c_, :, t0:t0 + FTAU]:
                      e.dma_start(out=d, in_=s),
                      reads=[DB["ln1_scr"]], writes=[ln1f_b])

        pend_epi = [None]
        load_ln1f(0)
        for ft in range(NFT):
            t0 = ft * FTAU
            ln1f, ln1f_b = LN1F[ft % 2]
            if ft + 1 < NFT:
                load_ln1f(ft + 1)

            def emit_gxe(nei, ei, ln1f=ln1f, ln1f_b=ln1f_b, t0=t0):
                gv, gv_b = GROW[ei % 2]
                gsb, gsb_b = GSB[ei % 2]
                xe, xe_b = XE[ei % 2]
                P.dma("pool", lambda e, d=gv[:, :],
                      s=g_scr[ei:ei + 1, t0:t0 + FTAU]:
                      e.dma_start(out=d, in_=s),
                      reads=[DB["g_scr"]], writes=[gv_b])
                for half in range(2):
                    pgi = (0, 4)[half]
                    P.ins("pe", lambda e, o=ps[pgi][:, 0:512],
                          r=gv[0:1, half * 512:half * 512 + 512]:
                          e.matmul(o, onesb[0:1, :], r,
                                   start=True, stop=True),
                          reads=[onesb_b, gv_b], writes=[psb[pgi]])
                    P.ins("act", lambda e,
                          d=gsb[:, half * 512:half * 512 + 512],
                          s=ps[pgi][:, 0:512]:
                          e.activation(d, s, AF.Identity),
                          reads=[psb[pgi]], writes=[gsb_b])
                for c_ in range(8):
                    P.ins("dve", lambda e,
                          d=xe[:, c_ * FTAU:(c_ + 1) * FTAU],
                          a=ln1f[:, c_ * FTAU:(c_ + 1) * FTAU]:
                          e.tensor_mul(d, a, gsb[:, :]),
                          reads=[ln1f_b, gsb_b], writes=[xe_b])

            order = [8] + list(range(8))
            emit_gxe(1, order[1])
            for nei, ei in enumerate(order):
                if ei < 8:
                    xe_t, xe_tb = XE[ei % 2]
                else:
                    xe_t, xe_tb = ln1f, ln1f_b

                load_w(ei, ft, w1_h, DB["w1"], w1b_scr, DB["w1b"], w1s_, w1s_b)
                load_w(ei, ft, w3_h, DB["w3"], w3b_scr, DB["w3b"], w3s_, w3s_b)
                if nei == 0 and pend_epi[0] is not None:
                    pend_epi[0]()
                    pend_epi[0] = None
                for m in range(8):
                    for sub in range(2):
                        par = (m * 2 + sub) % 2
                        p_a1, p_a3 = (0, 1) if par == 0 else (4, 5)
                        tsil, tsil_b = TSIL[par]
                        for (pidx, ws, wsb) in ((p_a1, w1s_, w1s_b),
                                                (p_a3, w3s_, w3s_b)):
                            for c_ in range(8):
                                P.ins("pe", lambda e, o=ps[pidx][:, 0:512],
                                      a=ws[:, c_ * C + m * 128:
                                           c_ * C + (m + 1) * 128],
                                      b=xe_t[:, c_ * FTAU + sub * 512:
                                             c_ * FTAU + sub * 512 + 512],
                                      st=(c_ == 0), sp=(c_ == 7):
                                      e.matmul(o, a, b, start=st, stop=sp),
                                      reads=[wsb, xe_tb], writes=[psb[pidx]])
                        P.ins("act", lambda e, d=tsil[:, :], s=ps[p_a1][:, 0:512]:
                              e.activation(d, s, AF.Silu),
                              reads=[psb[p_a1]], writes=[tsil_b])
                        P.ins("dve", lambda e,
                              d=hbuf[:, m * FTAU + sub * 512:
                                     m * FTAU + sub * 512 + 512],
                              s3=ps[p_a3][:, 0:512], ts=tsil[:, :]:
                              e.tensor_mul(d, s3, ts),
                              reads=[psb[p_a3], tsil_b], writes=[hbuf_b])
                load_w(ei, ft, w2_h, DB["w2"], w2b_scr, DB["w2b"], w2s_, w2s_b)
                if nei + 1 < len(order) and order[nei + 1] != order[1]:
                    emit_gxe(nei + 1, order[nei + 1])
                for t2 in range(8):
                    for half in range(2):
                        pidx = 2 + half
                        for m in range(8):
                            P.ins("pe", lambda e, o=ps[pidx][:, 0:512],
                                  a=hbuf[:, m * FTAU + t2 * 128:
                                         m * FTAU + (t2 + 1) * 128],
                                  b=w2s_[:, m * C + half * 512:
                                         m * C + half * 512 + 512],
                                  st=(m == 0), sp=(m == 7):
                                  e.matmul(o, a, b, start=st, stop=sp),
                                  reads=[hbuf_b, w2s_b], writes=[psb[pidx]])
                        dsl = oacc[:, t2 * C + half * 512:t2 * C + half * 512 + 512]
                        if nei == 0:
                            P.ins("dve", lambda e, d=dsl, s=ps[pidx][:, 0:512]:
                                  e.tensor_copy(d, s),
                                  reads=[psb[pidx]], writes=[oacc_b])
                        else:
                            P.ins("dve", lambda e, d=dsl, s=ps[pidx][:, 0:512]:
                                  e.tensor_add(d, s, d),
                                  reads=[psb[pidx], oacc_b], writes=[oacc_b])

            def mk_epi(ft=ft):
              def epi():
                for half4 in range(2):
                    t2s = list(range(half4 * 4, half4 * 4 + 4))
                    epi = []
                    for t2 in t2s:
                        tg = ft * 8 + t2
                        xv, xv_b = X1D[t2 % 4]
                        P.dma("sp", lambda e, d=xv[:, :],
                              s=x1_scr[tg * 128:(tg + 1) * 128, :]:
                              e.dma_start(out=d, in_=s),
                              reads=[DB["x1_scr"]], writes=[xv_b])
                        st_t, st_b = STATS[t2 % 4]
                        epi.append((t2, tg, xv, xv_b,
                                    oacc[:, t2 * C:(t2 + 1) * C],
                                    SQD[t2 % 4][0], SQD[t2 % 4][1],
                                    st_t[:, 0:1], st_t[:, 1:2], st_t[:, 2:3],
                                    st_t[:, 3:4], st_t[:, 4:5], st_t[:, 5:6],
                                    st_t[:, 6:7], st_b))
                    for (t2, tg, xv, xv_b, osl, sq_t, sq_tb, s1, s2, mm_, m2,
                         vv, rr, bb, st_b) in epi:
                        P.ins("dve", lambda e, o=s1, i=osl:
                              e.reduce_sum(o, i, axis=AX.X),
                              reads=[oacc_b], writes=[st_b])
                    for (t2, tg, xv, xv_b, osl, sq_t, sq_tb, s1, s2, mm_, m2,
                         vv, rr, bb, st_b) in epi:
                        P.ins("act", lambda e, o=sq_t[:, :], i=osl:
                              e.activation(o, i, AF.Square),
                              reads=[oacc_b], writes=[sq_tb])
                    for (t2, tg, xv, xv_b, osl, sq_t, sq_tb, s1, s2, mm_, m2,
                         vv, rr, bb, st_b) in epi:
                        P.ins("dve", lambda e, o=s2, i=sq_t[:, :]:
                              e.reduce_sum(o, i, axis=AX.X),
                              reads=[sq_tb], writes=[st_b])
                    for (t2, tg, xv, xv_b, osl, sq_t, sq_tb, s1, s2, mm_, m2,
                         vv, rr, bb, st_b) in epi:
                        P.ins("dve", lambda e, o=mm_, i=s1:
                              e.tensor_scalar_mul(o, i, 1.0 / C),
                              reads=[st_b], writes=[st_b])
                    for (t2, tg, xv, xv_b, osl, sq_t, sq_tb, s1, s2, mm_, m2,
                         vv, rr, bb, st_b) in epi:
                        P.ins("dve", lambda e, o=m2, i=s1: e.scalar_tensor_tensor(
                            o, i, 1.0 / (C * C), i, ALU.mult, ALU.mult),
                            reads=[st_b], writes=[st_b])
                    for (t2, tg, xv, xv_b, osl, sq_t, sq_tb, s1, s2, mm_, m2,
                         vv, rr, bb, st_b) in epi:
                        P.ins("dve", lambda e, o=vv, i=s2, m=m2:
                              e.scalar_tensor_tensor(o, i, 1.0 / C, m,
                                                     ALU.mult, ALU.subtract),
                              reads=[st_b], writes=[st_b])
                    for (t2, tg, xv, xv_b, osl, sq_t, sq_tb, s1, s2, mm_, m2,
                         vv, rr, bb, st_b) in epi:
                        P.ins("act", lambda e, o=vv: e.activation(
                            o, o, AF.Sqrt, bias=epsb[:, 0:1]),
                            reads=[st_b, epsb_b], writes=[st_b])
                    for (t2, tg, xv, xv_b, osl, sq_t, sq_tb, s1, s2, mm_, m2,
                         vv, rr, bb, st_b) in epi:
                        P.ins("dve", lambda e, o=rr, i=vv: e.reciprocal(o, i),
                              reads=[st_b], writes=[st_b])
                    for (t2, tg, xv, xv_b, osl, sq_t, sq_tb, s1, s2, mm_, m2,
                         vv, rr, bb, st_b) in epi:
                        P.ins("dve", lambda e, o=bb, m=mm_, r=rr:
                              e.scalar_tensor_tensor(o, m, -1.0, r,
                                                     ALU.mult, ALU.mult),
                              reads=[st_b], writes=[st_b])
                    for (t2, tg, xv, xv_b, osl, sq_t, sq_tb, s1, s2, mm_, m2,
                         vv, rr, bb, st_b) in epi:
                        P.ins("dve", lambda e, d=xv[:, :], b=bb:
                              e.tensor_scalar_add(d, d, b),
                              reads=[xv_b, st_b], writes=[xv_b])
                    for (t2, tg, xv, xv_b, osl, sq_t, sq_tb, s1, s2, mm_, m2,
                         vv, rr, bb, st_b) in epi:
                        P.ins("dve", lambda e, d=xv[:, :], i=osl, r=rr:
                              e.scalar_tensor_tensor(d, i, r, d, ALU.mult, ALU.add),
                              reads=[oacc_b, st_b, xv_b], writes=[xv_b])
                    for (t2, tg, xv, xv_b, osl, sq_t, sq_tb, s1, s2, mm_, m2,
                         vv, rr, bb, st_b) in epi:
                        P.dma("sp", lambda e, s=xv[:, :],
                              d=out_h[tg * 128:(tg + 1) * 128, :]:
                              e.dma_start(out=d, in_=s),
                              reads=[xv_b], writes=[DB["out"]])

              return epi
            pend_epi[0] = mk_epi()

        if pend_epi[0] is not None:
            pend_epi[0]()
            pend_epi[0] = None
        es_d.close()
        P.retire()
        P.replay(nc)
    return nc


_CACHE = {}


def _get_nc():
    if "nc" not in _CACHE:
        _CACHE["nc"] = build_program()
    return _CACHE["nc"]


def make_inputs(L0_attn_x, L0_attn_W_DQ, L0_attn_W_UQ, L0_attn_W_DKV,
                L0_attn_W_UK, L0_attn_W_UV, L0_attn_W_QR, L0_attn_W_KR,
                L0_attn_W_O, L0_ffn_gate_w, L0_ffn_w1, L0_ffn_w2, L0_ffn_w3,
                L0_ffn_w1s, L0_ffn_w2s, L0_ffn_w3s):
    f = np.float32
    c = np.ascontiguousarray
    sc4, masks, ident, emat = make_consts()
    shared = dict(
        wdqt=c(np.asarray(L0_attn_W_DQ, f).T),
        wuq=c(np.asarray(L0_attn_W_UQ, f)),
        wqr=c(np.asarray(L0_attn_W_QR, f)),
        wdkv=c(np.asarray(L0_attn_W_DKV, f)),
        wuk=c(np.asarray(L0_attn_W_UK, f)),
        wuv=c(np.asarray(L0_attn_W_UV, f)),
        wkr=c(np.asarray(L0_attn_W_KR, f)),
        wo=c(np.asarray(L0_attn_W_O, f)),
        gwt=c(np.asarray(L0_ffn_gate_w, f).T),
        w1=c(np.concatenate([np.asarray(L0_ffn_w1, f),
                             np.asarray(L0_ffn_w1s, f)], 0)),
        w2=c(np.concatenate([np.asarray(L0_ffn_w2, f),
                             np.asarray(L0_ffn_w2s, f)], 0)),
        w3=c(np.concatenate([np.asarray(L0_ffn_w3, f),
                             np.asarray(L0_ffn_w3s, f)], 0)),
        sc4=sc4, maskc=masks, identc=ident, ematc=emat,
    )
    xs = np.asarray(L0_attn_x, f)
    return [dict(shared, x=c(xs[b])) for b in range(8)]


def kernel(**inputs):
    in_maps = make_inputs(**inputs)
    nc = _get_nc()
    res = run_bass_kernel_spmd(nc, in_maps, core_ids=list(range(8)))
    return np.stack([res.results[b]["out"] for b in range(8)], 0)
